# version 07
# baseline (speedup 1.0000x reference)
import contextlib
import os

import numpy as np

import concourse.bass as bass
import concourse.mybir as mybir
from concourse.bass_utils import run_bass_kernel_spmd

F32 = mybir.dt.float32
BF16 = mybir.dt.bfloat16
I16 = mybir.dt.int16
U16 = mybir.dt.uint16
I8 = mybir.dt.int8
AL = mybir.AluOpType
AF = mybir.ActivationFunctionType
AX = mybir.AxisListType.X

D = 1024
T = 4096
FI = 3072
NH = 16
HD = 64
WIN = 256
DC = D // 128
IC = FI // 128
TT = 512
NT = T // TT
NW = T // WIN

MAGIC = float(np.float32(12582912.0))
FMIN = float(np.finfo(np.float32).min)
FMAX = float(np.finfo(np.float32).max)
LN_EPS = 1e-5
RMS_EPS = 1.1920928955078125e-07
N_CORES = 8

_DBG = os.environ.get("MK_DBG", "") == "1"
_NPH = int(os.environ.get("MK_PHASES", "99"))


def _host_constants():
    c = {}
    c["ident"] = np.eye(128, dtype=np.float32)
    perm = np.zeros((128, 128), np.float32)
    for p in range(128):
        perm[p, p ^ 32] = 1.0
    c["perm32"] = perm
    half = HD // 2
    inv_freq = (1.0 / (10000.0 ** (np.arange(0, HD, 2, dtype=np.float32) / np.float32(HD)))).astype(np.float32)
    freqs = np.outer(np.arange(T, dtype=np.float32), inv_freq).astype(np.float32)
    cosv, sinv = np.cos(freqs).astype(np.float32), np.sin(freqs).astype(np.float32)
    A = np.zeros((128, T), np.float32)
    B = np.zeros((128, T), np.float32)
    for p in range(128):
        j = p % 64
        jj = j if j < half else j - half
        A[p] = cosv[:, jj]
        B[p] = sinv[:, jj] if j < half else -sinv[:, jj]
    c["ropeA"], c["ropeB"] = A, B
    for s in range(2):
        m = np.full((128, WIN), FMIN, np.float32)
        for r in range(128):
            m[r, : s * 128 + r + 1] = FMAX
        c[f"maskq{s}"] = m
    c["ones128"] = np.ones((128, 1), np.float32)
    c["ones1x"] = np.ones((1, 128), np.float32)
    rmsmask = np.zeros((128, 2), np.float32)
    rmsmask[:64, 0] = 1.0
    rmsmask[64:, 1] = 1.0
    c["rmsmask"] = rmsmask
    rmsbc = np.zeros((2, 128), np.float32)
    rmsbc[0, :64] = 1.0
    rmsbc[1, 64:] = 1.0
    c["rmsbc"] = rmsbc
    return c


_CONSTS = _host_constants()


class Sems:

    def __init__(self):
        self.sem = {}
        self.cnt = {}

    def add(self, nc, ctx, names):
        for n in names:
            self.sem[n] = ctx.enter_context(nc.semaphore(n))
            self.cnt[n] = 0

    def inc(self, instr, name, n=1):
        instr.then_inc(self.sem[name], n)
        self.cnt[name] += n
        return self.cnt[name]

    def wait(self, eng, name, v):
        if v > 0:
            eng.wait_ge(self.sem[name], v)

    def cinc(self, chain, name, n=1):
        chain.flush()
        chain.e.sem_inc(self.sem[name], n)
        self.cnt[name] += n
        return self.cnt[name]


class Chain:

    EMITS = {"tensor_scalar", "tensor_tensor", "scalar_tensor_tensor",
             "tensor_reduce", "tensor_copy", "tensor_scalar_mul",
             "tensor_add", "tensor_sub", "tensor_mul", "reciprocal",
             "memset", "activation", "copy", "sqrt", "mul", "add",
             "square"}

    def __init__(self, eng, S, sem):
        self.e = eng
        self.S = S
        self.semname = sem
        self.waited = S.cnt[sem]

    def flush(self):
        if self.S.cnt[self.semname] > self.waited:
            self.e.wait_ge(self.S.sem[self.semname], self.S.cnt[self.semname])
            self.waited = self.S.cnt[self.semname]

    def __getattr__(self, n):
        f = getattr(self.e, n)
        if n not in Chain.EMITS:
            return f

        def g(*a, **k):
            self.flush()
            i = f(*a, **k)
            self.S.cnt[self.semname] += 1
            i.then_inc(self.S.sem[self.semname], 1)
            return i
        return g


class DmaGroup:

    def __init__(self, nc, ctx, name, K):
        self.K = K
        self.sems = [ctx.enter_context(nc.semaphore(f"{name}{k}"))
                     for k in range(K)]
        self.cnt = [0] * K

    def plan_counters(self):
        return list(self.cnt)

    @staticmethod
    def plan(counters, slot, K):
        k = slot % K
        counters[k] += 16
        return (k, counters[k])

    def issue(self, instr, slot, expect=None):
        k = slot % self.K
        instr.then_inc(self.sems[k], 16)
        self.cnt[k] += 16
        tok = (k, self.cnt[k])
        if expect is not None:
            assert tok == expect, (tok, expect)
        return tok

    def wait(self, eng, tok):
        if tok is not None:
            eng.wait_ge(self.sems[tok[0]], tok[1])

    def wait_all(self, eng):
        for k, c in enumerate(self.cnt):
            if c:
                eng.wait_ge(self.sems[k], c)


def build_nc():
    nc = bass.Bass("TRN2", target_bir_lowering=False)
    ap = {}

    def din(name, shape, dt=F32):
        ap[name] = nc.dram_tensor(name, shape, dt, kind="ExternalInput").ap()

    def dint(name, shape, dt=F32, dbg=False):
        kind = "ExternalOutput" if (dbg and _DBG) else "Internal"
        ap[name] = nc.dram_tensor(name, shape, dt, kind=kind).ap()

    din("x", [D, T])
    for n, sh in (("wq", [D, D]), ("wk", [D, D]), ("wv", [D, D]), ("wo", [D, D]),
                  ("wg", [FI, D]), ("wu", [FI, D]), ("wd", [D, FI])):
        din(n, sh)
    din("qg", [2, NH // 2])
    for n, v in _CONSTS.items():
        din("c" + n, list(v.shape))
    ap["y"] = nc.dram_tensor("y", [D, T], F32, kind="ExternalOutput").ap()

    for n, sh in (("wqT", [D, D]), ("wkT", [D, D]), ("wvT", [D, D]),
                  ("woT", [D, D]), ("wgT", [D, FI]), ("wuT", [D, FI]),
                  ("wdT", [FI, D])):
        dint(n + "h", sh, BF16, dbg=(n == "wgT"))
        dint(n + "l", sh, BF16)
    for n in ("wg", "wu", "wd"):
        dint("q8" + n, list(ap[n].shape), I8)
    dint("up_raw", [FI, T], dbg=True)
    dint("gate_raw", [FI, T])
    dint("uq16", [FI, T], I16)
    dint("gq16", [FI, T], I16)
    dint("sq16", [FI, T], U16)
    dint("g2q16", [FI, T], I16, dbg=True)
    dint("xmid", [D, T], dbg=True)
    dint("xmid2", [D, T], dbg=True)
    dint("qrot", [D, T], dbg=True)
    dint("krot", [D, T])
    dint("vtm", [T, D], dbg=True)
    for k in range(10):
        dint(f"cci{k}", [1, 1])
        dint(f"cco{k}", [1, 1])
        dint(f"colb{k}", [128, 1])
        dint(f"sdr{k}", [1, 2])

    with contextlib.ExitStack() as ctx:
        ent = ctx.enter_context
        ARENA = ent(nc.sbuf_tensor("ARENA", [128, 32768], F32))
        POOL = ent(nc.sbuf_tensor("POOL", [128, 12288], F32))
        STAT = ent(nc.sbuf_tensor("STAT", [128, 768], F32))
        SCAL = ent(nc.sbuf_tensor("SCAL", [128, 16], F32))
        ROWS = ent(nc.sbuf_tensor("ROWS", [1, 4 * TT], F32))
        RMSW = ent(nc.sbuf_tensor("RMSW", [2, 4 * TT], F32))
        CST = {n: ent(nc.sbuf_tensor("cs" + n, list(v.shape), F32))
               for n, v in _CONSTS.items() if n not in ("ropeA", "ropeB")}
        QG = ent(nc.sbuf_tensor("qg_sb", [2, NH // 2], F32))
        EPS = ent(nc.sbuf_tensor("eps_sb", [128, 3], F32))
        PS = [ent(nc.psum_tensor(f"ps{k}", [128, 512], F32)) for k in range(8)]

        S = Sems()
        S.add(nc, ctx, ["TEN", "VEC", "SCA", "CCS", "VE2", "SC2"])
        GL = DmaGroup(nc, ctx, "GL", 8)
        GS = DmaGroup(nc, ctx, "GS", 8)
        GC = DmaGroup(nc, ctx, "GC", 8)
        GA = DmaGroup(nc, ctx, "GA", 1)
        BAR = ctx.enter_context(nc.semaphore("BAR"))

        block = ent(nc.Block())

        def bvec(f):
            block.vector(lambda v: f(Chain(v, S, "VE2")))

        def bsca(f):
            block.scalar(lambda s2: f(Chain(s2, S, "SC2")))

        barctr = [0]

        def my_barrier():
            barctr[0] += 1
            tgt = 5 * barctr[0]
            for name in ("sync", "scalar", "tensor", "vector", "gpsimd"):
                def f(eng):
                    eng.drain().then_inc(BAR, 1)
                    eng.wait_ge(BAR, tgt)
                getattr(block, name)(f)

        def ld3(eng, out2d, src2d):
            C = src2d.shape[0] // 128
            return eng.dma_start(
                out=out2d.rearrange("p (c t) -> p c t", c=C),
                in_=src2d.rearrange("(c p) t -> p c t", p=128))

        def TSL(k):
            return POOL[:, k * 512:(k + 1) * 512]

        def AR_(a, b):
            return ARENA[:, a:b]

        SLOT = {"s1": 0, "i1": 1, "s2": 2, "i2": 3, "s3": 4, "i3": 5,
                "s4": 6, "i4": 7, "s5": 8, "i5": 9, "s42": 10, "s3c": 11, "h5": 12}

        def SCs(name):
            return SCAL[:, SLOT[name]:SLOT[name] + 1]

        M0 = STAT[:, 0:192]
        M1 = STAT[:, 192:384]
        PMcol = STAT[:, 384:385]
        Grow = STAT[0:1, 385:386]
        SROW = STAT[0:1, 386:388]
        ROWB = STAT[0:1, 388:516]
        NMC = STAT[:, 520:538]

        murow = ROWS[0:1, 0:TT]
        rrow = ROWS[0:1, TT:2 * TT]
        var_row = ROWS[0:1, 2 * TT:3 * TT]
        sd_row = ROWS[0:1, 3 * TT:4 * TT]

        ROPE_A = AR_(24576, 28672)
        ROPE_B = AR_(28672, 32768)
        GBC = ARENA[0:2, 0:1024]

        ph = [0]

        def active():
            ph[0] += 1
            return ph[0] <= _NPH

        @block.sync
        def _(sync):
            names = [n for n in _CONSTS if n not in ("ropeA", "ropeB")]
            for idx, n in enumerate(names):
                i = sync.dma_start(out=CST[n][:], in_=ap["c" + n])
                GL.issue(i, idx)
            i = sync.dma_start(out=QG[:], in_=ap["qg"])
            GL.issue(i, len(names))
            GL.wait_all(sync)

        @bvec
        def _(vector):
            vector.memset(EPS[:, 0:1], LN_EPS)
            vector.memset(EPS[:, 1:2], RMS_EPS)
            vector.memset(EPS[:, 2:3], MAGIC)

        my_barrier()

        def emit_wprep():
            plans = [("wq", D, D, False), ("wk", D, D, False),
                     ("wv", D, D, False), ("wo", D, D, False),
                     ("wg", FI, D, True), ("wu", FI, D, True),
                     ("wd", D, FI, True)]
            loads = []
            for (n, rows, cols, quant) in plans:
                for oc in range(rows // 128):
                    loads.append((n, oc, cols, quant))
            NL = len(loads)

            wbuf = [AR_(0, 3072), AR_(3072, 6144)]
            qf = [AR_(6144, 9216), AR_(9216, 12288)]
            dq = [AR_(12288, 15360), AR_(15360, 18432)]
            sA = [M0[:, 0:96], M0[:, 96:192]]
            sR = [M1[:, 0:96], M1[:, 96:192]]
            EV = AR_(18432, 18944)
            EVH = AR_(18944, 19200).bitcast(BF16)
            EVL = AR_(19200, 19456).bitcast(BF16)

            b_vec, b_ten, b_sca = S.cnt["VEC"], S.cnt["TEN"], S.cnt["SCA"]
            cl, cc_, cs_ = (GL.plan_counters(), GC.plan_counters(),
                            GS.plan_counters())
            ld_at = [None] * NL
            q8st_at = [None] * NL
            q8ld_at = [None] * NL
            qrdy_at = [0] * NL
            deq_at = [0] * NL
            ntp = [0] * NL
            tp0 = [0] * NL
            g = 0
            for idx, (n, oc, cols, quant) in enumerate(loads):
                ld_at[idx] = DmaGroup.plan(cl, idx % 2, GL.K)
                if quant:
                    q8st_at[idx] = DmaGroup.plan(cc_, idx % 2, GC.K)
                    q8ld_at[idx] = DmaGroup.plan(cc_, idx % 2, GC.K)
                ntp[idx] = cols // 128
                tp0[idx] = g
                g += ntp[idx]
            NG = g
            isq = [None] * NG
            for idx in range(NL):
                for k in range(ntp[idx]):
                    isq[tp0[idx] + k] = idx
            vec = b_vec
            vsp_at = [0] * NG
            for idx in range(NL + 1):
                if idx < NL:
                    if loads[idx][3]:
                        vec += 1
                        qrdy_at[idx] = vec
                    vec += 1
                    deq_at[idx] = vec
                if idx >= 1:
                    for k in range(ntp[idx - 1]):
                        vec += 1
                        vsp_at[tp0[idx - 1] + k] = vec
            ten_at = [b_ten + j + 1 for j in range(NG)]
            sca_at = [b_sca + j + 1 for j in range(NG)]
            sth_at = [None] * NG
            stl_at = [None] * NG
            st_at = [None] * NG
            for idx in range(NL + 1):
                if idx >= 1:
                    for k in range(ntp[idx - 1]):
                        j = tp0[idx - 1] + k
                        sth_at[j] = DmaGroup.plan(cs_, j % 4, GS.K)
                        stl_at[j] = DmaGroup.plan(cs_, 4 + j % 4, GS.K)

            @block.sync
            def _(sync):
                for idx, (n, oc, cols, quant) in enumerate(loads):
                    b = idx % 2
                    if idx >= 2:
                        S.wait(sync, "TEN", ten_at[tp0[idx - 2] + ntp[idx - 2] - 1])
                    i = sync.dma_start(out=wbuf[b][:, 0:cols],
                                       in_=ap[n][oc * 128:(oc + 1) * 128, :])
                    GL.issue(i, idx % 2, ld_at[idx])

            @bvec
            def _(vector):
                for idx in range(NL + 1):
                    if idx < NL:
                        n, oc, cols, quant = loads[idx]
                        b = idx % 2
                        GL.wait(vector, ld_at[idx])
                        if quant:
                            nb = cols // 32
                            w3 = wbuf[b][:, 0:cols].rearrange("p (b k) -> p b k", k=32)
                            vector.tensor_reduce(
                                sA[b][:, 0:nb].rearrange("p (b o) -> p b o", o=1),
                                w3, AX, AL.max, apply_absolute_value=True)
                            vector.tensor_scalar(sA[b][:, 0:nb], sA[b][:, 0:nb],
                                                 1.0 / 7.0, 1e-12, AL.mult, AL.max)
                            vector.reciprocal(sR[b][:, 0:nb], sA[b][:, 0:nb])
                            rbc = sR[b][:, 0:nb].rearrange("p (b o) -> p b o", o=1).broadcast_to([128, nb, 32])
                            vector.tensor_tensor(
                                qf[b][:, 0:cols].rearrange("p (b k) -> p b k", k=32),
                                w3, rbc, AL.mult)
                            vector.tensor_scalar(qf[b][:, 0:cols], qf[b][:, 0:cols],
                                                 1.0, MAGIC, AL.mult, AL.add)
                            vector.tensor_scalar(qf[b][:, 0:cols], qf[b][:, 0:cols],
                                                 MAGIC, 7.0, AL.subtract, AL.min)
                            assert S.cinc(vector, "VEC") == qrdy_at[idx]
                            GC.wait(vector, q8ld_at[idx])
                            sbc = sA[b][:, 0:nb].rearrange("p (b o) -> p b o", o=1).broadcast_to([128, nb, 32])
                            vector.tensor_tensor(
                                dq[b][:, 0:cols].rearrange("p (b k) -> p b k", k=32),
                                qf[b][:, 0:cols].rearrange("p (b k) -> p b k", k=32),
                                sbc, AL.mult)
                            assert S.cinc(vector, "VEC") == deq_at[idx]
                        else:
                            vector.tensor_copy(dq[b][:, 0:cols], wbuf[b][:, 0:cols])
                            assert S.cinc(vector, "VEC") == deq_at[idx]
                    if idx >= 1:
                        for k in range(ntp[idx - 1]):
                            j = tp0[idx - 1] + k
                            S.wait(vector, "SCA", sca_at[j])
                            if j >= 4:
                                GS.wait(vector, sth_at[j - 4])
                                GS.wait(vector, stl_at[j - 4])
                            sl_ = slice((j % 4) * 128, (j % 4 + 1) * 128)
                            vector.tensor_copy(EVH[:, sl_], EV[:, sl_])
                            vector.tensor_tensor(EVL[:, sl_], EV[:, sl_],
                                                 EVH[:, sl_], AL.subtract)
                            assert S.cinc(vector, "VEC") == vsp_at[j]

            @block.gpsimd
            def _(gp):
                for idx in range(NL + 1):
                    if idx < NL:
                        n, oc, cols, quant = loads[idx]
                        b = idx % 2
                        if quant:
                            S.wait(gp, "VEC", qrdy_at[idx])
                            i = gp.dma_start(
                                out=ap["q8" + n][oc * 128:(oc + 1) * 128, :],
                                in_=qf[b][:, 0:cols])
                            GC.issue(i, b, q8st_at[idx])
                            GC.wait(gp, q8st_at[idx])
                            i = gp.dma_start(
                                out=qf[b][:, 0:cols],
                                in_=ap["q8" + n][oc * 128:(oc + 1) * 128, :])
                            GC.issue(i, b, q8ld_at[idx])
                    if idx >= 1:
                        n, oc, cols, quant = loads[idx - 1]
                        for k in range(ntp[idx - 1]):
                            j = tp0[idx - 1] + k
                            sl_ = slice((j % 4) * 128, (j % 4 + 1) * 128)
                            S.wait(gp, "VEC", vsp_at[j])
                            i = gp.dma_start(
                                out=ap[n + "Th"][k * 128:(k + 1) * 128,
                                                 oc * 128:(oc + 1) * 128],
                                in_=EVH[:, sl_])
                            GS.issue(i, j % 4, sth_at[j])
                            i = gp.dma_start(
                                out=ap[n + "Tl"][k * 128:(k + 1) * 128,
                                                 oc * 128:(oc + 1) * 128],
                                in_=EVL[:, sl_])
                            GS.issue(i, 4 + j % 4, stl_at[j])
                GS.wait_all(gp)

            @block.tensor
            def _(tensor):
                for idx, (n, oc, cols, quant) in enumerate(loads):
                    b = idx % 2
                    S.wait(tensor, "VEC", deq_at[idx])
                    for k in range(ntp[idx]):
                        j = tp0[idx] + k
                        if j >= 4:
                            S.wait(tensor, "SCA", sca_at[j - 4])
                        i = tensor.transpose(PS[4 + j % 4][:, 0:128],
                                             dq[b][:, k * 128:(k + 1) * 128],
                                             CST["ident"][:])
                        assert S.inc(i, "TEN") == ten_at[j]

            @bsca
            def _(scalar):
                for j in range(NG):
                    S.wait(scalar, "TEN", ten_at[j])
                    if j >= 4:
                        S.wait(scalar, "VEC", vsp_at[j - 4])
                    i = scalar.copy(EV[:, (j % 4) * 128:(j % 4 + 1) * 128],
                                    PS[4 + j % 4][:, 0:128])
                    assert S.cinc(scalar, "SCA") == sca_at[j]

        if active():
            emit_wprep()
            my_barrier()

        def emit_allreduce(k, statr, ncols, s_slot, i_slot):
            b_vec, b_ccs = S.cnt["VEC"], S.cnt["CCS"]
            v1, v2, v3 = b_vec + 1, b_vec + 2, b_vec + 3
            ca = GA.plan_counters()
            g = [DmaGroup.plan(ca, 0, 1) for _ in range(7)]
            cc1 = b_ccs + 1

            @bvec
            def _(vector):
                vector.tensor_reduce(PMcol[:], statr[:, 0:ncols], AX, AL.max)
                assert S.cinc(vector, "VEC") == v1
                GA.wait(vector, g[1])
                vector.tensor_reduce(Grow[:], ROWB[:], AX, AL.max)
                assert S.cinc(vector, "VEC") == v2
                GA.wait(vector, g[3])
                vector.tensor_scalar(SROW[0:1, 0:1], Grow[:],
                                         1.0 / 32767.0, 1e-12,
                                         AL.mult, AL.max)
                vector.reciprocal(SROW[0:1, 1:2], SROW[0:1, 0:1])
                assert S.cinc(vector, "VEC") == v3

            @block.gpsimd
            def _(gp):
                S.wait(gp, "VEC", v1)
                i = gp.dma_start(out=ap[f"colb{k}"], in_=PMcol[:])
                GA.issue(i, 0, g[0])
                GA.wait(gp, g[0])
                i = gp.dma_start(out=ROWB[:], in_=ap[f"colb{k}"].transpose([1, 0]))
                GA.issue(i, 0, g[1])
                S.wait(gp, "VEC", v2)
                i = gp.dma_start(out=ap[f"cci{k}"], in_=Grow[:])
                GA.issue(i, 0, g[2])
                GA.wait(gp, g[2])
                i = gp.collective_compute(
                    "AllReduce", AL.max, replica_groups=[list(range(N_CORES))],
                    ins=[ap[f"cci{k}"]], outs=[ap[f"cco{k}"]])
                assert S.inc(i, "CCS") == cc1
                S.wait(gp, "CCS", cc1)
                i = gp.dma_start(out=Grow[:], in_=ap[f"cco{k}"])
                GA.issue(i, 0, g[3])
                S.wait(gp, "VEC", v3)
                i = gp.dma_start(out=ap[f"sdr{k}"], in_=SROW[:])
                GA.issue(i, 0, g[4])
                GA.wait(gp, g[4])
                i = gp.dma_start(out=SCs(s_slot),
                                 in_=ap[f"sdr{k}"][0:1, 0:1].broadcast_to([128, 1]))
                GA.issue(i, 0, g[5])
                GA.wait(gp, g[5])
                i = gp.dma_start(out=SCs(i_slot),
                                 in_=ap[f"sdr{k}"][0:1, 1:2].broadcast_to([128, 1]))
                GA.issue(i, 0, g[6])
                GA.wait(gp, g[6])

            my_barrier()

        def emit_ln(x_src, consume, ninc_per_chunk):
            XT2 = [POOL[:, 0:4096], POOL[:, 4096:8192]]
            SQB = POOL[:, 8192:12288]
            b_ten, b_sca, b_vec = S.cnt["TEN"], S.cnt["SCA"], S.cnt["VEC"]
            cl = GL.plan_counters()
            ld_at = [DmaGroup.plan(cl, t % 2, GL.K) for t in range(NT)]
            sq_at = [b_sca + 2 * t + 1 for t in range(NT)]
            sd_at = [b_sca + 2 * t + 2 for t in range(NT)]
            stats_at = [b_ten + 2 * t + 1 for t in range(NT)]
            bcast_at = [b_ten + 2 * t + 2 for t in range(NT)]
            vpt = 2 + DC * ninc_per_chunk
            row_at = [b_vec + vpt * t + 1 for t in range(NT)]
            rr_at = [b_vec + vpt * t + 2 for t in range(NT)]
            tile_at = [b_vec + vpt * (t + 1) for t in range(NT)]

            @block.sync
            def _(sync):
                for t in range(NT):
                    if t >= 2:
                        S.wait(sync, "VEC", tile_at[t - 2])
                    i = ld3(sync, XT2[t % 2][:],
                            ap[x_src][:, t * TT:(t + 1) * TT])
                    GL.issue(i, t % 2, ld_at[t])

            @bsca
            def _(scalar):
                for t in range(NT):
                    GL.wait(scalar, ld_at[t])
                    if t >= 1:
                        S.wait(scalar, "TEN", stats_at[t - 1])
                        S.wait(scalar, "VEC", tile_at[t - 1])
                    i = scalar.activation(SQB[:], XT2[t % 2][:], AF.Square)
                    assert S.cinc(scalar, "SCA") == sq_at[t]
                    S.wait(scalar, "VEC", row_at[t])
                    i = scalar.activation(sd_row, var_row, AF.Sqrt,
                                          bias=EPS[0:1, 0:1], scale=1.0)
                    assert S.cinc(scalar, "SCA") == sd_at[t]

            @block.tensor
            def _(tensor):
                for t in range(NT):
                    GL.wait(tensor, ld_at[t])
                    S.wait(tensor, "SCA", sq_at[t])
                    if t >= 1:
                        S.wait(tensor, "VEC", row_at[t - 1])
                        S.wait(tensor, "VEC", tile_at[t - 1])
                    for c in range(DC):
                        tensor.matmul(PS[0][0:1, 0:TT], CST["ones128"][:],
                                      XT2[t % 2][:, c * TT:(c + 1) * TT],
                                      start=(c == 0), stop=(c == DC - 1))
                    for c in range(DC):
                        i = tensor.matmul(PS[1][0:1, 0:TT], CST["ones128"][:],
                                          SQB[:, c * TT:(c + 1) * TT],
                                          start=(c == 0), stop=(c == DC - 1))
                    assert S.inc(i, "TEN") == stats_at[t]
                    S.wait(tensor, "VEC", rr_at[t])
                    tensor.matmul(PS[2][:, 0:TT], CST["ones1x"][:],
                                  murow, start=True, stop=True)
                    i = tensor.matmul(PS[3][:, 0:TT], CST["ones1x"][:],
                                      rrow, start=True, stop=True)
                    assert S.inc(i, "TEN") == bcast_at[t]

            @bvec
            def _(vector):
                for t in range(NT):
                    S.wait(vector, "TEN", stats_at[t])
                    vector.tensor_scalar(murow, PS[0][0:1, 0:TT],
                                         1.0 / float(D), None, AL.mult)
                    vector.tensor_scalar(var_row, PS[1][0:1, 0:TT],
                                         1.0 / float(D), None, AL.mult)
                    vector.tensor_tensor(sd_row, murow, murow, AL.mult)
                    vector.tensor_tensor(var_row, var_row, sd_row,
                                             AL.subtract)
                    assert S.cinc(vector, "VEC") == row_at[t]
                    S.wait(vector, "SCA", sd_at[t])
                    vector.reciprocal(rrow, sd_row)
                    assert S.cinc(vector, "VEC") == rr_at[t]
                    S.wait(vector, "TEN", bcast_at[t])
                    for c in range(DC):
                        consume(vector, t, c, XT2[t % 2][:, c * TT:(c + 1) * TT])
                    assert S.cnt["VEC"] == tile_at[t]

        def emit_ffn(x_src, x_dst, ccb):
            def p0a_consume(vector, t, c, xc):
                tmp = AR_(0, 512)
                tmp2 = AR_(512, 1024)
                vector.tensor_tensor(tmp, xc, PS[2][:, 0:TT], AL.subtract)
                vector.tensor_tensor(tmp2, tmp, PS[3][:, 0:TT], AL.mult)
                vector.tensor_reduce(M0[:, t * DC + c:t * DC + c + 1],
                                         tmp2, AX, AL.max,
                                         apply_absolute_value=True)
                S.cinc(vector, "VEC")

            if active():
                emit_ln(x_src, p0a_consume, 1)
                my_barrier()
                emit_allreduce(ccb + 0, M0, 64, "s1", "i1")

            XQH = ARENA[:, 0:16384].bitcast(BF16)
            XQL = ARENA[:, 16384:32768].bitcast(BF16)

            def xqh_ct(c, t):
                return XQH[:, c * 4096 + t * TT:c * 4096 + (t + 1) * TT]

            def xql_ct(c, t):
                return XQL[:, c * 4096 + t * TT:c * 4096 + (t + 1) * TT]

            def p0b_consume(vector, t, c, xc):
                tmp = POOL[:, 8192:8704]
                tmp2 = POOL[:, 8704:9216]
                vector.tensor_tensor(tmp, xc, PS[2][:, 0:TT], AL.subtract)
                vector.tensor_tensor(tmp2, tmp, PS[3][:, 0:TT], AL.mult)
                vector.tensor_scalar(tmp, tmp2, SCs("i1"), MAGIC,
                                     AL.mult, AL.add)
                vector.tensor_scalar(tmp2, tmp, MAGIC, 32767.0,
                                     AL.subtract, AL.min)
                vector.tensor_copy(xqh_ct(c, t), tmp2)
                vector.tensor_tensor(xql_ct(c, t), tmp2, xqh_ct(c, t),
                                     AL.subtract)
                S.cinc(vector, "VEC")

            if active():
                emit_ln(x_src, p0b_consume, 1)
                my_barrier()

            def xq_c(c):
                return AR_(c * 4096, (c + 1) * 4096)

            if active():
                WST = [POOL[:, 8192:10240], POOL[:, 10240:12288]]
                b_ten, b_sca, b_vec = S.cnt["TEN"], S.cnt["SCA"], S.cnt["VEC"]
                cl, cs_ = GL.plan_counters(), GS.plan_counters()
                NJ = IC * NT
                ldu_at = [None] * IC
                ldg_at = [None] * IC
                for i in range(IC):
                    ldu_at[i] = (DmaGroup.plan(cl, i % 2, GL.K),
                                 DmaGroup.plan(cl, 2 + i % 2, GL.K))
                    ldg_at[i] = (DmaGroup.plan(cl, 4 + i % 2, GL.K),
                                 DmaGroup.plan(cl, 6 + i % 2, GL.K))
                mmu_at = [b_ten + 2 * j + 1 for j in range(NJ)]
                mmg_at = [b_ten + 2 * j + 2 for j in range(NJ)]
                evu_at = [b_sca + 2 * j + 1 for j in range(NJ)]
                evg_at = [b_sca + 2 * j + 2 for j in range(NJ)]
                mxu_at = [b_vec + 2 * j + 1 for j in range(NJ)]
                mxg_at = [b_vec + 2 * j + 2 for j in range(NJ)]
                stu_at = [None] * NJ
                stg_at = [None] * NJ
                cs2 = cs_
                for j in range(NJ):
                    stu_at[j] = DmaGroup.plan(cs2, j % 4, GS.K)
                    stg_at[j] = DmaGroup.plan(cs2, 4 + j % 4, GS.K)

                WB = POOL[:, 8192:12288].bitcast(BF16)

                def wsl(i, plane):
                    base = (i % 2) * 4096 + plane * 1024
                    return WB[:, base:base + 1024]

                @block.sync
                def _(sync):
                    for i in range(IC):
                        if i >= 2:
                            S.wait(sync, "TEN", mmg_at[(i - 1) * NT - 1])
                        for pl, (nm, tok) in enumerate(
                                (("wuTh", ldu_at[i][0]), ("wuTl", ldu_at[i][1]),
                                 ("wgTh", ldg_at[i][0]), ("wgTl", ldg_at[i][1]))):
                            k = ld3(sync, wsl(i, pl),
                                    ap[nm][:, i * 128:(i + 1) * 128])
                            GL.issue(k, 2 * pl + i % 2, tok)

                @block.tensor
                def _(tensor):
                    for i in range(IC):
                        for tok in ldu_at[i] + ldg_at[i]:
                            GL.wait(tensor, tok)
                        for t in range(NT):
                            j = i * NT + t
                            if j >= 2:
                                S.wait(tensor, "SCA", evu_at[j - 2])
                            for c in range(DC):
                                wh = wsl(i, 0)[:, c * 128:(c + 1) * 128]
                                wl = wsl(i, 1)[:, c * 128:(c + 1) * 128]
                                tensor.matmul(PS[j % 2][:, 0:TT], wh,
                                              xqh_ct(c, t),
                                              start=(c == 0), stop=False)
                                tensor.matmul(PS[j % 2][:, 0:TT], wh,
                                              xql_ct(c, t),
                                              start=False, stop=False)
                                k = tensor.matmul(PS[j % 2][:, 0:TT], wl,
                                                  xqh_ct(c, t), start=False,
                                                  stop=(c == DC - 1))
                            assert S.inc(k, "TEN") == mmu_at[j]
                            if j >= 2:
                                S.wait(tensor, "SCA", evg_at[j - 2])
                            for c in range(DC):
                                wh = wsl(i, 2)[:, c * 128:(c + 1) * 128]
                                wl = wsl(i, 3)[:, c * 128:(c + 1) * 128]
                                tensor.matmul(PS[2 + j % 2][:, 0:TT], wh,
                                              xqh_ct(c, t),
                                              start=(c == 0), stop=False)
                                tensor.matmul(PS[2 + j % 2][:, 0:TT], wh,
                                              xql_ct(c, t),
                                              start=False, stop=False)
                                k = tensor.matmul(PS[2 + j % 2][:, 0:TT], wl,
                                                  xqh_ct(c, t), start=False,
                                                  stop=(c == DC - 1))
                            assert S.inc(k, "TEN") == mmg_at[j]

                @bsca
                def _(scalar):
                    for j in range(NJ):
                        S.wait(scalar, "TEN", mmu_at[j])
                        if j >= 4:
                            S.wait(scalar, "VEC", mxu_at[j - 4])
                            GS.wait(scalar, stu_at[j - 4])
                        scalar.activation(TSL(j % 4), PS[j % 2][:, 0:TT],
                                          AF.Copy, scale=SCs("s1"))
                        assert S.cinc(scalar, "SCA") == evu_at[j]
                        S.wait(scalar, "TEN", mmg_at[j])
                        if j >= 4:
                            S.wait(scalar, "VEC", mxg_at[j - 4])
                            GS.wait(scalar, stg_at[j - 4])
                        scalar.activation(TSL(4 + j % 4), PS[2 + j % 2][:, 0:TT],
                                          AF.Copy, scale=SCs("s1"))
                        assert S.cinc(scalar, "SCA") == evg_at[j]

                @bvec
                def _(vector):
                    for j in range(NJ):
                        S.wait(vector, "SCA", evu_at[j])
                        vector.tensor_reduce(M0[:, j:j + 1], TSL(j % 4),
                                                 AX, AL.max,
                                                 apply_absolute_value=True)
                        assert S.cinc(vector, "VEC") == mxu_at[j]
                        S.wait(vector, "SCA", evg_at[j])
                        vector.tensor_reduce(M1[:, j:j + 1], TSL(4 + j % 4),
                                                 AX, AL.max,
                                                 apply_absolute_value=True)
                        assert S.cinc(vector, "VEC") == mxg_at[j]

                @block.gpsimd
                def _(gp):
                    for j in range(NJ):
                        i, t = j // NT, j % NT
                        S.wait(gp, "SCA", evu_at[j])
                        k = gp.dma_start(
                            out=ap["up_raw"][i * 128:(i + 1) * 128,
                                             t * TT:(t + 1) * TT],
                            in_=TSL(j % 4)[:])
                        GS.issue(k, j % 4, stu_at[j])
                        S.wait(gp, "SCA", evg_at[j])
                        k = gp.dma_start(
                            out=ap["gate_raw"][i * 128:(i + 1) * 128,
                                               t * TT:(t + 1) * TT],
                            in_=TSL(4 + j % 4)[:])
                        GS.issue(k, 4 + j % 4, stg_at[j])
                    GS.wait_all(gp)

                my_barrier()
                emit_allreduce(ccb + 1, M0, NJ, "s2", "i2")
                emit_allreduce(ccb + 2, M1, NJ, "s3", "i3")

            if active():
                NJ = IC * NT
                b_vec, b_sca = S.cnt["VEC"], S.cnt["SCA"]
                cl, cs_ = GL.plan_counters(), GS.plan_counters()
                ldu_at = [None] * NJ
                ldg_at = [None] * NJ
                stu_at = [None] * NJ
                stg_at = [None] * NJ
                sts_at = [None] * NJ
                for j in range(NJ):
                    ldu_at[j] = DmaGroup.plan(cl, (j % 2) * 2, GL.K)
                    ldg_at[j] = DmaGroup.plan(cl, (j % 2) * 2 + 1, GL.K)
                    stu_at[j] = DmaGroup.plan(cs_, (j % 2) * 3 + 0, GS.K)
                    stg_at[j] = DmaGroup.plan(cs_, (j % 2) * 3 + 1, GS.K)
                    sts_at[j] = DmaGroup.plan(cs_, (j % 2) * 3 + 2, GS.K)
                w1u_at, w1g_at, sig_at, w1s_at = {}, {}, {}, {}
                sc2 = b_sca
                for j in range(NJ + 1):
                    if j < NJ:
                        sc2 += 1
                        w1u_at[j] = sc2
                        sc2 += 1
                        w1g_at[j] = sc2
                    if j >= 1:
                        sc2 += 1
                        sig_at[j - 1] = sc2
                        sc2 += 1
                        w1s_at[j - 1] = sc2
                uq_at, gq_at, sq_at, tl_at = {}, {}, {}, {}
                vc2 = b_vec + 1
                for j in range(NJ + 1):
                    if j < NJ:
                        vc2 += 1
                        uq_at[j] = vc2
                        vc2 += 1
                        gq_at[j] = vc2
                    if j >= 1:
                        vc2 += 1
                        sq_at[j - 1] = vc2
                        vc2 += 1
                        tl_at[j - 1] = vc2

                def sl(j, k):
                    return TSL((j % 2) * 8 + k)

                @block.sync
                def _(sync):
                    for j in range(NJ):
                        i, t = j // NT, j % NT
                        if j >= 2:
                            S.wait(sync, "VEC", tl_at[j - 2])
                        k = sync.dma_start(
                            out=sl(j, 0)[:],
                            in_=ap["up_raw"][i * 128:(i + 1) * 128,
                                             t * TT:(t + 1) * TT])
                        GL.issue(k, (j % 2) * 2, ldu_at[j])
                        k = sync.dma_start(
                            out=sl(j, 1)[:],
                            in_=ap["gate_raw"][i * 128:(i + 1) * 128,
                                               t * TT:(t + 1) * TT])
                        GL.issue(k, (j % 2) * 2 + 1, ldg_at[j])

                @bsca
                def _(scalar):
                    for j in range(NJ + 1):
                        if j < NJ:
                            GL.wait(scalar, ldu_at[j])
                            if j >= 2:
                                S.wait(scalar, "VEC", sq_at[j - 2])
                            scalar.activation(sl(j, 2), sl(j, 0), AF.Copy,
                                              scale=SCs("i2"), bias=MAGIC)
                            assert S.cinc(scalar, "SCA") == w1u_at[j]
                            GL.wait(scalar, ldg_at[j])
                            if j >= 2:
                                S.wait(scalar, "VEC", tl_at[j - 2])
                            scalar.activation(sl(j, 5), sl(j, 1), AF.Copy,
                                              scale=SCs("i3"), bias=MAGIC)
                            assert S.cinc(scalar, "SCA") == w1g_at[j]
                        if j >= 1:
                            jj = j - 1
                            S.wait(scalar, "VEC", gq_at[jj])
                            scalar.activation(sl(jj, 6), sl(jj, 4), AF.Sigmoid,
                                              scale=SCs("s3"))
                            assert S.cinc(scalar, "SCA") == sig_at[jj]
                            S.wait(scalar, "VEC", uq_at[jj])
                            scalar.activation(sl(jj, 2), sl(jj, 6), AF.Copy,
                                              scale=65536.0, bias=MAGIC)
                            assert S.cinc(scalar, "SCA") == w1s_at[jj]

                @bvec
                def _(vector):
                    vector.tensor_scalar(SCs("s3c"), SCs("s3"),
                                             float(2.0 ** -16), None, AL.mult)
                    S.cinc(vector, "VEC")
                    for j in range(NJ + 1):
                        if j < NJ:
                            S.wait(vector, "SCA", w1u_at[j])
                            if j >= 2:
                                GS.wait(vector, stu_at[j - 2])
                            vector.tensor_scalar(sl(j, 3), sl(j, 2), MAGIC,
                                                     32767.0,
                                                     AL.subtract, AL.min)
                            assert S.cinc(vector, "VEC") == uq_at[j]
                            S.wait(vector, "SCA", w1g_at[j])
                            if j >= 2:
                                GS.wait(vector, stg_at[j - 2])
                            vector.tensor_scalar(sl(j, 4), sl(j, 5), MAGIC,
                                                     32767.0,
                                                     AL.subtract, AL.min)
                            assert S.cinc(vector, "VEC") == gq_at[j]
                        if j >= 1:
                            jj = j - 1
                            S.wait(vector, "SCA", w1s_at[jj])
                            if jj >= 2:
                                GS.wait(vector, sts_at[jj - 2])
                            vector.tensor_scalar(sl(jj, 7), sl(jj, 2), MAGIC,
                                                 65535.0,
                                                 AL.subtract, AL.min)
                            assert S.cinc(vector, "VEC") == sq_at[jj]
                            vector.scalar_tensor_tensor(sl(jj, 5), sl(jj, 7),
                                                            SCs("s3c"),
                                                            sl(jj, 4),
                                                            AL.mult, AL.mult)
                            vector.tensor_reduce(M0[:, jj:jj + 1], sl(jj, 5),
                                                     AX, AL.max,
                                                     apply_absolute_value=True)
                            assert S.cinc(vector, "VEC") == tl_at[jj]

                @block.gpsimd
                def _(gp):
                    for j in range(NJ):
                        i, t = j // NT, j % NT
                        dst = (slice(i * 128, (i + 1) * 128),
                               slice(t * TT, (t + 1) * TT))
                        S.wait(gp, "VEC", uq_at[j])
                        k = gp.dma_start(out=ap["uq16"][dst[0], dst[1]],
                                         in_=sl(j, 3)[:])
                        GS.issue(k, (j % 2) * 3 + 0, stu_at[j])
                        S.wait(gp, "VEC", gq_at[j])
                        k = gp.dma_start(out=ap["gq16"][dst[0], dst[1]],
                                         in_=sl(j, 4)[:])
                        GS.issue(k, (j % 2) * 3 + 1, stg_at[j])
                        S.wait(gp, "VEC", sq_at[j])
                        k = gp.dma_start(out=ap["sq16"][dst[0], dst[1]],
                                         in_=sl(j, 7)[:])
                        GS.issue(k, (j % 2) * 3 + 2, sts_at[j])
                    GS.wait_all(gp)

                my_barrier()
                emit_allreduce(ccb + 3, M0, NJ, "s4", "i4")

            if active():
                NJ = IC * NT
                b_vec = S.cnt["VEC"]
                cc_, cs_ = GC.plan_counters(), GS.plan_counters()
                ldu_at = [None] * NJ
                ldg_at = [None] * NJ
                lds_at = [None] * NJ
                st_at = [None] * NJ
                for j in range(NJ):
                    ldu_at[j] = DmaGroup.plan(cc_, (j % 2) * 3 + 0, GC.K)
                    ldg_at[j] = DmaGroup.plan(cc_, (j % 2) * 3 + 1, GC.K)
                    lds_at[j] = DmaGroup.plan(cc_, (j % 2) * 3 + 2, GC.K)
                    st_at[j] = DmaGroup.plan(cs_, 6 + j % 2, GS.K)
                g2_at = [b_vec + 1 + 2 * j + 1 for j in range(NJ)]
                tl_at = [b_vec + 1 + 2 * j + 2 for j in range(NJ)]

                def sl(j, k):
                    return TSL((j % 2) * 8 + k)

                @block.gpsimd
                def _(gp):
                    for j in range(NJ + 1):
                        if j < NJ:
                            i, t = j // NT, j % NT
                            src = (slice(i * 128, (i + 1) * 128),
                                   slice(t * TT, (t + 1) * TT))
                            if j >= 2:
                                S.wait(gp, "VEC", tl_at[j - 2])
                            k = gp.dma_start(out=sl(j, 0)[:],
                                             in_=ap["uq16"][src[0], src[1]])
                            GC.issue(k, (j % 2) * 3 + 0, ldu_at[j])
                            k = gp.dma_start(out=sl(j, 1)[:],
                                             in_=ap["gq16"][src[0], src[1]])
                            GC.issue(k, (j % 2) * 3 + 1, ldg_at[j])
                            k = gp.dma_start(out=sl(j, 2)[:],
                                             in_=ap["sq16"][src[0], src[1]])
                            GC.issue(k, (j % 2) * 3 + 2, lds_at[j])
                        if j >= 1:
                            jj = j - 1
                            i, t = jj // NT, jj % NT
                            S.wait(gp, "VEC", g2_at[jj])
                            k = gp.dma_start(
                                out=ap["g2q16"][i * 128:(i + 1) * 128,
                                                t * TT:(t + 1) * TT],
                                in_=sl(jj, 4)[:])
                            GS.issue(k, 6 + jj % 2, st_at[jj])
                    GS.wait_all(gp)

                @bvec
                def _(vector):
                    vector.tensor_tensor(SCs("s42"), SCs("s4"), SCs("s2"),
                                             AL.mult)
                    S.cinc(vector, "VEC")
                    for j in range(NJ):
                        GC.wait(vector, ldu_at[j])
                        GC.wait(vector, ldg_at[j])
                        GC.wait(vector, lds_at[j])
                        if j >= 2:
                            GS.wait(vector, st_at[j - 2])
                        uqf, gqf, sqf = sl(j, 0), sl(j, 1), sl(j, 2)
                        w4, g2qf = sl(j, 3), sl(j, 4)
                        vector.scalar_tensor_tensor(w4, sqf, SCs("s3c"), gqf,
                                                    AL.mult, AL.mult)
                        vector.tensor_scalar(w4, w4, SCs("i4"), MAGIC,
                                             AL.mult, AL.add)
                        vector.tensor_scalar(g2qf, w4, MAGIC, 32767.0,
                                                 AL.subtract, AL.min)
                        assert S.cinc(vector, "VEC") == g2_at[j]
                        vector.scalar_tensor_tensor(sl(j, 5), g2qf,
                                                        SCs("s42"), uqf,
                                                        AL.mult, AL.mult)
                        vector.tensor_reduce(M1[:, j:j + 1], sl(j, 5), AX,
                                                 AL.max,
                                                 apply_absolute_value=True)
                        assert S.cinc(vector, "VEC") == tl_at[j]

                my_barrier()
                emit_allreduce(ccb + 4, M1, NJ, "s5", "i5")

            if active():
                OBH = [AR_(0, 12288).bitcast(BF16),
                       AR_(12288, 24576).bitcast(BF16)]
                WDB = AR_(24576, 30720).bitcast(BF16)

                def obh(tb, ic):
                    return OBH[tb][:, ic * TT:(ic + 1) * TT]

                def obl(tb, ic):
                    return OBH[tb][:, 12288 + ic * TT:12288 + (ic + 1) * TT]

                def wdsl(j, plane):
                    base = (j % 2) * 6144 + plane * 3072
                    return WDB[:, base:base + 3072]
                b_vec, b_ten = S.cnt["VEC"], S.cnt["TEN"]
                cc_, cl, cs_ = (GC.plan_counters(), GL.plan_counters(),
                                GS.plan_counters())
                oldu_at = [[None] * IC for _ in range(NT)]
                oldg_at = [[None] * IC for _ in range(NT)]
                wld_at = [[None] * DC for _ in range(NT)]
                xo_at = [[None] * DC for _ in range(NT)]
                st_at = [[None] * DC for _ in range(NT)]
                for t in range(NT):
                    for ic in range(IC):
                        oldu_at[t][ic] = DmaGroup.plan(cc_, (ic % 2) * 2, GC.K)
                        oldg_at[t][ic] = DmaGroup.plan(cc_, (ic % 2) * 2 + 1, GC.K)
                    for h in range(DC):
                        wld_at[t][h] = (
                            DmaGroup.plan(cl, 2 + (t * DC + h) % 2, GL.K),
                            DmaGroup.plan(cl, 4 + (t * DC + h) % 2, GL.K))
                        xo_at[t][h] = DmaGroup.plan(cl, 6 + h % 2, GL.K)
                        st_at[t][h] = DmaGroup.plan(cs_, h % 2, GS.K)
                mm_at = [[b_ten + t * DC + h + 1 for h in range(DC)]
                         for t in range(NT)]
                ov_at = [[0] * IC for _ in range(NT)]
                xn_at = [[0] * DC for _ in range(NT)]
                vp = b_vec + 1
                for t in range(NT + 1):
                    if t < NT:
                        for ic in range(IC):
                            vp += 1
                            ov_at[t][ic] = vp
                    if t >= 1:
                        for h in range(DC):
                            vp += 1
                            xn_at[t - 1][h] = vp

                def slo(ic, k):
                    return TSL(16 + (ic % 2) * 2 + k)

                @block.gpsimd
                def _(gp):
                    for t in range(NT + 1):
                        if t < NT:
                            if t >= 2:
                                S.wait(gp, "TEN", mm_at[t - 2][DC - 1])
                            for ic in range(IC):
                                if t * IC + ic >= 2:
                                    pj = t * IC + ic - 2
                                    S.wait(gp, "VEC", ov_at[pj // IC][pj % IC])
                                src = (slice(ic * 128, (ic + 1) * 128),
                                       slice(t * TT, (t + 1) * TT))
                                k = gp.dma_start(out=slo(ic, 0)[:],
                                                 in_=ap["uq16"][src[0], src[1]])
                                GC.issue(k, (ic % 2) * 2, oldu_at[t][ic])
                                k = gp.dma_start(out=slo(ic, 1)[:],
                                                 in_=ap["g2q16"][src[0], src[1]])
                                GC.issue(k, (ic % 2) * 2 + 1, oldg_at[t][ic])
                        if t >= 1:
                            for h in range(DC):
                                S.wait(gp, "VEC", xn_at[t - 1][h])
                                k = gp.dma_start(
                                    out=ap[x_dst][h * 128:(h + 1) * 128,
                                                  (t - 1) * TT:t * TT],
                                    in_=TSL(14 + h % 2)[:])
                                GS.issue(k, h % 2, st_at[t - 1][h])
                    GS.wait_all(gp)

                @bvec
                def _(vector):
                    vector.tensor_scalar(SCs("h5"), SCs("s5"), 0.5, None,
                                         AL.mult)
                    S.cinc(vector, "VEC")
                    for t in range(NT + 1):
                        if t < NT:
                            for ic in range(IC):
                                GC.wait(vector, oldu_at[t][ic])
                                GC.wait(vector, oldg_at[t][ic])
                                s0, s1 = slo(ic, 0), slo(ic, 1)
                                tmp = TSL(22)
                                tmp2 = TSL(23)
                                vector.scalar_tensor_tensor(
                                    tmp, s1, SCs("s42"), s0, AL.mult, AL.mult)
                                vector.tensor_scalar(tmp, tmp, SCs("i5"), MAGIC,
                                                     AL.mult, AL.add)
                                vector.tensor_scalar(tmp2, tmp, MAGIC, 32767.0,
                                                     AL.subtract, AL.min)
                                vector.tensor_copy(obh(t % 2, ic), tmp2)
                                vector.tensor_tensor(obl(t % 2, ic), tmp2,
                                                     obh(t % 2, ic),
                                                     AL.subtract)
                                assert S.cinc(vector, "VEC") == ov_at[t][ic]
                        if t >= 1:
                            for h in range(DC):
                                j = (t - 1) * DC + h
                                S.wait(vector, "TEN", mm_at[t - 1][h])
                                GL.wait(vector, xo_at[t - 1][h])
                                if j >= 2:
                                    GS.wait(vector,
                                            st_at[(j - 2) // DC][(j - 2) % DC])
                                vector.scalar_tensor_tensor(
                                    TSL(14 + h % 2), PS[4 + j % 2][:, 0:TT],
                                    SCs("h5"), TSL(12 + h % 2), AL.mult, AL.add)
                                assert S.cinc(vector, "VEC") == xn_at[t - 1][h]

                @block.sync
                def _(sync):
                    for t in range(NT):
                        for h in range(DC):
                            j = t * DC + h
                            if j >= 2:
                                S.wait(sync, "TEN",
                                       mm_at[(j - 2) // DC][(j - 2) % DC])
                            k = ld3(sync, wdsl(j, 0),
                                    ap["wdTh"][:, h * 128:(h + 1) * 128])
                            GL.issue(k, 2 + j % 2, wld_at[t][h][0])
                            k = ld3(sync, wdsl(j, 1),
                                    ap["wdTl"][:, h * 128:(h + 1) * 128])
                            GL.issue(k, 4 + j % 2, wld_at[t][h][1])
                            if j >= 2:
                                S.wait(sync, "VEC",
                                       xn_at[(j - 2) // DC][(j - 2) % DC])
                            k = sync.dma_start(
                                out=TSL(12 + h % 2)[:],
                                in_=ap[x_src][h * 128:(h + 1) * 128,
                                              t * TT:(t + 1) * TT])
                            GL.issue(k, 6 + h % 2, xo_at[t][h])

                @block.tensor
                def _(tensor):
                    for t in range(NT):
                        S.wait(tensor, "VEC", ov_at[t][IC - 1])
                        for h in range(DC):
                            j = t * DC + h
                            GL.wait(tensor, wld_at[t][h][0])
                            GL.wait(tensor, wld_at[t][h][1])
                            if j >= 2:
                                S.wait(tensor, "VEC",
                                       xn_at[(j - 2) // DC][(j - 2) % DC])
                            for ic in range(IC):
                                wh = wdsl(j, 0)[:, ic * 128:(ic + 1) * 128]
                                wl = wdsl(j, 1)[:, ic * 128:(ic + 1) * 128]
                                tensor.matmul(PS[4 + j % 2][:, 0:TT], wh,
                                              obh(t % 2, ic),
                                              start=(ic == 0), stop=False)
                                tensor.matmul(PS[4 + j % 2][:, 0:TT], wh,
                                              obl(t % 2, ic),
                                              start=False, stop=False)
                                k = tensor.matmul(PS[4 + j % 2][:, 0:TT], wl,
                                                  obh(t % 2, ic), start=False,
                                                  stop=(ic == IC - 1))
                            assert S.inc(k, "TEN") == mm_at[t][h]

                my_barrier()

        def emit_a2_tile(t, x_src):
            XT0 = POOL[:, 0:4096]
            SQB = POOL[:, 8192:12288]
            LH = ARENA[:, 8192:10240].bitcast(BF16)
            LL = ARENA[:, 10240:12288].bitcast(BF16)
            QB = AR_(12288, 16384)
            KB = AR_(16384, 20480)
            WVB = AR_(20480, 24576).bitcast(BF16)
            WQK = POOL[:, 4096:8192].bitcast(BF16)
            T2B = XT0
            T1B = SQB
            VO = [TSL(0), TSL(1)]

            def lh_c(c):
                return LH[:, c * TT:(c + 1) * TT]

            def ll_c(c):
                return LL[:, c * TT:(c + 1) * TT]

            b = {n: S.cnt[n] for n in ("TEN", "SCA", "VEC")}
            cl = GL.plan_counters()
            ld_x = DmaGroup.plan(cl, t % 2, GL.K)
            sq_d = b["SCA"] + 1
            sd_d = b["SCA"] + 2
            st_d = b["TEN"] + 1
            bc_d = b["TEN"] + 2
            row_d = b["VEC"] + 1
            rr_d = b["VEC"] + 2
            ln_d = b["VEC"] + 2 + DC

            @block.sync
            def _(sync):
                i = ld3(sync, XT0[:], ap[x_src][:, t * TT:(t + 1) * TT])
                GL.issue(i, t % 2, ld_x)

            @bsca
            def _(scalar):
                GL.wait(scalar, ld_x)
                scalar.activation(SQB[:], XT0[:], AF.Square)
                assert S.cinc(scalar, "SCA") == sq_d
                S.wait(scalar, "VEC", row_d)
                scalar.activation(sd_row, var_row, AF.Sqrt,
                                  bias=EPS[0:1, 0:1], scale=1.0)
                assert S.cinc(scalar, "SCA") == sd_d

            @block.tensor
            def _(tensor):
                GL.wait(tensor, ld_x)
                S.wait(tensor, "SCA", sq_d)
                for c in range(DC):
                    tensor.matmul(PS[0][0:1, 0:TT], CST["ones128"][:],
                                  XT0[:, c * TT:(c + 1) * TT],
                                  start=(c == 0), stop=(c == DC - 1))
                for c in range(DC):
                    i = tensor.matmul(PS[1][0:1, 0:TT], CST["ones128"][:],
                                      SQB[:, c * TT:(c + 1) * TT],
                                      start=(c == 0), stop=(c == DC - 1))
                assert S.inc(i, "TEN") == st_d
                S.wait(tensor, "VEC", rr_d)
                tensor.matmul(PS[2][:, 0:TT], CST["ones1x"][:],
                              murow, start=True, stop=True)
                i = tensor.matmul(PS[3][:, 0:TT], CST["ones1x"][:],
                                  rrow, start=True, stop=True)
                assert S.inc(i, "TEN") == bc_d

            @bvec
            def _(vector):
                S.wait(vector, "TEN", st_d)
                vector.tensor_scalar(murow, PS[0][0:1, 0:TT],
                                     1.0 / float(D), None, AL.mult)
                vector.tensor_scalar(var_row, PS[1][0:1, 0:TT],
                                     1.0 / float(D), None, AL.mult)
                vector.tensor_tensor(sd_row, murow, murow, AL.mult)
                vector.tensor_tensor(var_row, var_row, sd_row, AL.subtract)
                assert S.cinc(vector, "VEC") == row_d
                S.wait(vector, "SCA", sd_d)
                vector.reciprocal(rrow, sd_row)
                assert S.cinc(vector, "VEC") == rr_d
                S.wait(vector, "TEN", bc_d)
                tmp = POOL[:, 8192:8704]
                tmp2 = POOL[:, 8704:9216]
                for c in range(DC):
                    vector.tensor_tensor(tmp, XT0[:, c * TT:(c + 1) * TT],
                                         PS[2][:, 0:TT], AL.subtract)
                    vector.tensor_tensor(tmp2, tmp, PS[3][:, 0:TT], AL.mult)
                    vector.tensor_copy(lh_c(c), tmp2)
                    vector.tensor_tensor(ll_c(c), tmp2, lh_c(c), AL.subtract)
                    S.cinc(vector, "VEC")
                assert S.cnt["VEC"] == ln_d

            b2 = {n: S.cnt[n] for n in ("TEN", "SCA", "VEC")}
            cl2, cs2 = GL.plan_counters(), GS.plan_counters()
            NU = 16
            wld = [None] * NU
            for u in range(NU):
                wld[u] = (DmaGroup.plan(cl2, 4 + (u % 2) * 2, GL.K),
                          DmaGroup.plan(cl2, 5 + (u % 2) * 2, GL.K))
            pmm_at = [b2["TEN"] + u + 1 for u in range(NU)]
            pev_at = [b2["SCA"] + u + 1 for u in range(NU)]

            def wqk_h(u):
                return WQK[:, (u % 2) * 4096:(u % 2) * 4096 + 1024]

            def wqk_l(u):
                return WQK[:, (u % 2) * 4096 + 1024:(u % 2) * 4096 + 2048]

            @block.sync
            def _(sync):
                for u in range(NU):
                    s_, o = u // 8, u % 8
                    nm = ("wqT", "wkT")[s_]
                    if u >= 2:
                        S.wait(sync, "TEN", pmm_at[u - 2])
                    i = ld3(sync, wqk_h(u),
                            ap[nm + "h"][:, o * 128:(o + 1) * 128])
                    GL.issue(i, 4 + (u % 2) * 2, wld[u][0])
                    i = ld3(sync, wqk_l(u),
                            ap[nm + "l"][:, o * 128:(o + 1) * 128])
                    GL.issue(i, 5 + (u % 2) * 2, wld[u][1])

            @block.tensor
            def _(tensor):
                S.wait(tensor, "VEC", ln_d)
                for u in range(NU):
                    s_, o = u // 8, u % 8
                    GL.wait(tensor, wld[u][0])
                    GL.wait(tensor, wld[u][1])
                    if u >= 2:
                        S.wait(tensor, "SCA", pev_at[u - 2])
                    for c in range(DC):
                        wh = wqk_h(u)[:, c * 128:(c + 1) * 128]
                        wl = wqk_l(u)[:, c * 128:(c + 1) * 128]
                        tensor.matmul(PS[u % 2][:, 0:TT], wh, lh_c(c),
                                      start=(c == 0), stop=False)
                        tensor.matmul(PS[u % 2][:, 0:TT], wh, ll_c(c),
                                      start=False, stop=False)
                        i = tensor.matmul(PS[u % 2][:, 0:TT], wl, lh_c(c),
                                          start=False, stop=(c == DC - 1))
                    assert S.inc(i, "TEN") == pmm_at[u]

            @bsca
            def _(scalar):
                for u in range(NU):
                    s_, o = u // 8, u % 8
                    S.wait(scalar, "TEN", pmm_at[u])
                    buf = QB if s_ == 0 else KB
                    scalar.copy(buf[:, o * TT:(o + 1) * TT], PS[u % 2][:, 0:TT])
                    assert S.cinc(scalar, "SCA") == pev_at[u]

            b3 = {n: S.cnt[n] for n in ("TEN", "SCA")}
            cl3, cs3 = GL.plan_counters(), GS.plan_counters()
            wvld = [(DmaGroup.plan(cl3, 2, GL.K), DmaGroup.plan(cl3, 3, GL.K))
                    for _ in range(2)]
            vst = [DmaGroup.plan(cs3, 2 + u % 2, GS.K) for u in range(8)]
            vmm = [b3["TEN"] + u + 1 for u in range(8)]
            vev = [b3["SCA"] + u + 1 for u in range(8)]

            @block.sync
            def _(sync):
                for o2 in range(2):
                    if o2 >= 1:
                        S.wait(sync, "TEN", vmm[3])
                    i = ld3(sync, WVB[:, 0:4096],
                            ap["wvTh"][:, o2 * TT:(o2 + 1) * TT])
                    GL.issue(i, 2, wvld[o2][0])
                    i = ld3(sync, WVB[:, 4096:8192],
                            ap["wvTl"][:, o2 * TT:(o2 + 1) * TT])
                    GL.issue(i, 3, wvld[o2][1])

            @block.tensor
            def _(tensor):
                for u in range(8):
                    o2, ts4 = u // 4, u % 4
                    GL.wait(tensor, wvld[o2][0])
                    GL.wait(tensor, wvld[o2][1])
                    if u < 2:
                        S.wait(tensor, "SCA", pev_at[NU - 2 + u])
                    else:
                        S.wait(tensor, "SCA", vev[u - 2])
                    for c in range(DC):
                        xh = LH[:, c * TT + ts4 * 128:c * TT + (ts4 + 1) * 128]
                        xl = LL[:, c * TT + ts4 * 128:c * TT + (ts4 + 1) * 128]
                        whm = WVB[:, c * 512:(c + 1) * 512]
                        wlm = WVB[:, 4096 + c * 512:4096 + (c + 1) * 512]
                        tensor.matmul(PS[u % 2][:, 0:TT], xh, whm,
                                      start=(c == 0), stop=False)
                        tensor.matmul(PS[u % 2][:, 0:TT], xl, whm,
                                      start=False, stop=False)
                        i = tensor.matmul(PS[u % 2][:, 0:TT], xh, wlm,
                                          start=False, stop=(c == DC - 1))
                    assert S.inc(i, "TEN") == vmm[u]

            @bsca
            def _(scalar):
                for u in range(8):
                    S.wait(scalar, "TEN", vmm[u])
                    if u >= 2:
                        GS.wait(scalar, vst[u - 2])
                    scalar.copy(VO[u % 2][:], PS[u % 2][:, 0:TT])
                    assert S.cinc(scalar, "SCA") == vev[u]

            @block.gpsimd
            def _(gp):
                for u in range(8):
                    o2, ts4 = u // 4, u % 4
                    S.wait(gp, "SCA", vev[u])
                    r0 = t * TT + ts4 * 128
                    i = gp.dma_start(
                        out=ap["vtm"][r0:r0 + 128, o2 * TT:(o2 + 1) * TT],
                        in_=VO[u % 2][:])
                    GS.issue(i, 2 + u % 2, vst[u])
                GS.wait_all(gp)

            rsl = slice(t * TT, (t + 1) * TT)
            prev_vqr = [None]
            for s_, buf in ((0, QB), (1, KB)):
                b4 = {n: S.cnt[n] for n in ("TEN", "SCA", "VEC")}
                cs4 = GS.plan_counters()
                sq2_d = b4["SCA"] + 1
                sd4 = [b4["SCA"] + 1 + c + 1 for c in range(DC)]
                rmm, rbb, pmm2 = {}, {}, {}
                tc = b4["TEN"]
                for st_ in range(DC + 2):
                    if st_ < DC:
                        tc += 1
                        rmm[st_] = tc
                    if 1 <= st_ <= DC:
                        tc += 1
                        rbb[st_ - 1] = tc
                    if st_ >= 2:
                        tc += 1
                        pmm2[st_ - 2] = tc
                vme, vrc, vqn, vt2 = {}, {}, {}, {}
                vc = b4["VEC"]
                for st_ in range(DC + 2):
                    if st_ < DC:
                        vc += 1
                        vme[st_] = vc
                        vc += 1
                        vrc[st_] = vc
                    if 1 <= st_ <= DC:
                        vc += 1
                        vqn[st_ - 1] = vc
                    if st_ >= 2:
                        vc += 1
                        vt2[st_ - 2] = vc
                vt1 = vc + 1
                vqr = vc + 2
                stq = DmaGroup.plan(cs4, s_, GS.K)

                def rslot(c):
                    return RMSW[0:2, (c % 4) * 512:(c % 4 + 1) * 512]

                @bsca
                def _(scalar, s_=s_, buf=buf, sq2_d=sq2_d, sd4=sd4,
                      vme=vme, pq=prev_vqr[0], rslot=rslot):
                    if s_ == 0:
                        GS.wait(scalar, vst[6])
                        GS.wait(scalar, vst[7])
                    else:
                        S.wait(scalar, "VEC", pq)
                    scalar.activation(T2B[:], buf[:], AF.Square)
                    assert S.cinc(scalar, "SCA") == sq2_d
                    for c in range(DC):
                        S.wait(scalar, "VEC", vme[c])
                        scalar.activation(rslot(c), rslot(c), AF.Sqrt,
                                          bias=EPS[0:2, 1:2], scale=1.0)
                        assert S.cinc(scalar, "SCA") == sd4[c]

                @block.tensor
                def _(tensor, s_=s_, buf=buf, sq2_d=sq2_d, sd4=sd4, rmm=rmm,
                      rbb=rbb, pmm2=pmm2, vme=vme, vrc=vrc, vqn=vqn, vt2=vt2,
                      rslot=rslot):
                    for st_ in range(DC + 2):
                        if st_ < DC:
                            c = st_
                            if c == 0:
                                S.wait(tensor, "SCA", sq2_d)
                            if c >= 4:
                                S.wait(tensor, "VEC", vt2[c - 4])
                            i = tensor.matmul(PS[4 + c % 4][0:2, 0:TT],
                                              CST["rmsmask"][:],
                                              T2B[:, c * TT:(c + 1) * TT],
                                              start=True, stop=True)
                            assert S.inc(i, "TEN") == rmm[c]
                        if 1 <= st_ <= DC:
                            c = st_ - 1
                            S.wait(tensor, "VEC", vrc[c])
                            if c >= 4:
                                S.wait(tensor, "VEC", vqn[c - 4])
                            stat = (GBC[0:2, c * 128:(c + 1) * 128]
                                    if s_ == 0 else CST["rmsbc"][:])
                            i = tensor.matmul(PS[c % 4][:, 0:TT], stat,
                                              rslot(c), start=True, stop=True)
                            assert S.inc(i, "TEN") == rbb[c]
                        if st_ >= 2:
                            c = st_ - 2
                            S.wait(tensor, "VEC", vqn[c])
                            i = tensor.matmul(PS[4 + c % 4][:, 0:TT],
                                              CST["perm32"][:],
                                              buf[:, c * TT:(c + 1) * TT],
                                              start=True, stop=True)
                            assert S.inc(i, "TEN") == pmm2[c]

                @bvec
                def _(vector, s_=s_, buf=buf, rmm=rmm, rbb=rbb, pmm2=pmm2,
                      vme=vme, vrc=vrc, vqn=vqn, vt2=vt2, vt1=vt1, vqr=vqr,
                      sd4=sd4, rslot=rslot):
                    for st_ in range(DC + 2):
                        if st_ < DC:
                            c = st_
                            S.wait(vector, "TEN", rmm[c])
                            vector.tensor_scalar(rslot(c),
                                                 PS[4 + c % 4][0:2, 0:TT],
                                                 1.0 / float(HD), None,
                                                 AL.mult)
                            assert S.cinc(vector, "VEC") == vme[c]
                            S.wait(vector, "SCA", sd4[c])
                            vector.reciprocal(rslot(c), rslot(c))
                            assert S.cinc(vector, "VEC") == vrc[c]
                        if 1 <= st_ <= DC:
                            c = st_ - 1
                            S.wait(vector, "TEN", rbb[c])
                            vector.tensor_tensor(buf[:, c * TT:(c + 1) * TT],
                                                 buf[:, c * TT:(c + 1) * TT],
                                                 PS[c % 4][:, 0:TT], AL.mult)
                            assert S.cinc(vector, "VEC") == vqn[c]
                        if st_ >= 2:
                            c = st_ - 2
                            S.wait(vector, "TEN", pmm2[c])
                            vector.tensor_tensor(T2B[:, c * TT:(c + 1) * TT],
                                                 PS[4 + c % 4][:, 0:TT],
                                                 ROPE_B[:, rsl], AL.mult)
                            assert S.cinc(vector, "VEC") == vt2[c]
                    rab = ROPE_A[:, rsl].rearrange(
                        "p (o k) -> p o k", o=1).broadcast_to([128, DC, TT])
                    vector.tensor_tensor(
                        T1B[:].rearrange("p (c k) -> p c k", c=DC),
                        buf[:].rearrange("p (c k) -> p c k", c=DC),
                        rab, AL.mult)
                    assert S.cinc(vector, "VEC") == vt1
                    vector.tensor_tensor(buf[:], T1B[:], T2B[:], AL.add)
                    assert S.cinc(vector, "VEC") == vqr

                prev_vqr[0] = vqr

                @block.gpsimd
                def _(gp, s_=s_, buf=buf, vqr=vqr, stq=stq):
                    S.wait(gp, "VEC", vqr)
                    i = gp.dma_start(
                        out=ap[("qrot", "krot")[s_]][:, t * TT:(t + 1) * TT]
                            .rearrange("(c p) k -> p c k", p=128),
                        in_=buf[:].rearrange("p (c k) -> p c k", c=DC))
                    GS.issue(i, s_, stq)
                    GS.wait(gp, stq)

            my_barrier()

        def emit_a3_wp(wp, x_src, x_dst):
            Q = AR_(8192, 12288)
            K_ = AR_(12288, 16384)
            V4 = AR_(16384, 20480)
            Y = AR_(20480, 24576)
            YH = V4[:, 0:2048].bitcast(BF16)
            YL = V4[:, 2048:4096].bitcast(BF16)
            WOB = POOL[:, 4096:6144].bitcast(BF16)
            XOL = [TSL(12), TSL(13)]
            SM3 = [TSL(16), TSL(17), TSL(0)]
            EN3 = [TSL(20), TSL(21), TSL(22)]
            ET3 = [[TSL(x)[:, ks * WIN:(ks + 1) * WIN] for ks in range(2)]
                   for x in (18, 19, 23)]

            cl, cs_ = GL.plan_counters(), GS.plan_counters()
            ldq = DmaGroup.plan(cl, 0, GL.K)
            ldk = DmaGroup.plan(cl, 1, GL.K)
            ldv = DmaGroup.plan(cl, 2, GL.K)
            wo_at = [(DmaGroup.plan(cl, 4 + h % 2, GL.K),
                      DmaGroup.plan(cl, 6 + h % 2, GL.K)) for h in range(DC)]
            cgc = GC.plan_counters()
            xo_at = [DmaGroup.plan(cgc, h % 2, GC.K) for h in range(DC)]
            st_at = [DmaGroup.plan(cs_, h % 2, GS.K) for h in range(DC)]

            b = {n: S.cnt[n] for n in ("TEN", "SCA", "VEC")}
            units = [(w, h) for w in range(2) for h in range(NH)]
            NU = len(units)
            sc_at, tp_at, pv_at = {}, {}, {}
            ms_at, nm_at, ex_at, et_at, ye_at = {}, {}, {}, {}, {}
            tc = b["TEN"]
            for u in range(3):
                for qs in range(2):
                    tc += 1
                    sc_at[units[u] + (qs,)] = tc
            for u in range(NU):
                if u + 3 < NU:
                    for qs in range(2):
                        tc += 1
                        sc_at[units[u + 3] + (qs,)] = tc
                for qs in range(2):
                    for ks in range(2):
                        tc += 1
                        tp_at[units[u] + (qs, ks)] = tc
                tc += 1
                pv_at[units[u]] = tc
            womm_at = [tc + h + 1 for h in range(DC)]
            vc2 = b["VEC"]
            for u in range(NU + 1):
                if u < NU:
                    for qs in range(2):
                        vc2 += 1
                        ms_at[units[u] + (qs,)] = vc2
                if u >= 1:
                    for qs in range(2):
                        vc2 += 1
                        nm_at[units[u - 1] + (qs,)] = vc2
            ysp = vc2 + 1
            vc2 += 1
            xn_at = [vc2 + h + 1 for h in range(DC)]
            sc2 = b["SCA"]
            for u in range(NU + 1):
                if u < NU:
                    for qs in range(2):
                        sc2 += 1
                        ex_at[units[u] + (qs,)] = sc2
                if u >= 1:
                    w_, h_ = units[u - 1]
                    sc2 += 1
                    et_at[(w_, h_)] = sc2
                    if h_ % 2 == 1:
                        sc2 += 1
                        ye_at[(w_, h_ // 2)] = sc2

            def scb(u):
                return PS[u % 3]

            def tpb(u):
                return PS[3 + u % 3]

            @block.sync
            def _(sync):
                i = ld3(sync, Q[:], ap["qrot"][:, wp * TT:(wp + 1) * TT])
                GL.issue(i, 0, ldq)
                i = ld3(sync, K_[:], ap["krot"][:, wp * TT:(wp + 1) * TT])
                GL.issue(i, 1, ldk)
                i = ld3(sync, V4[:], ap["vtm"][wp * TT:(wp + 1) * TT, :])
                GL.issue(i, 2, ldv)
                for h in range(DC):
                    if h >= 2:
                        S.wait(sync, "TEN", womm_at[h - 2])
                    i = ld3(sync, WOB[:, (h % 2) * 2048:(h % 2) * 2048 + 1024],
                            ap["woTh"][:, h * 128:(h + 1) * 128])
                    GL.issue(i, 4 + h % 2, wo_at[h][0])
                    i = ld3(sync, WOB[:, (h % 2) * 2048 + 1024:(h % 2) * 2048 + 2048],
                            ap["woTl"][:, h * 128:(h + 1) * 128])
                    GL.issue(i, 6 + h % 2, wo_at[h][1])

            def emit_sc(tensor, u):
                w, h = units[u]
                hc, off = h // 2, (h % 2) * 64
                kw = K_[off:off + 64,
                        hc * TT + w * WIN:hc * TT + (w + 1) * WIN]
                if u >= 3:
                    S.wait(tensor, "VEC", ms_at[units[u - 3] + (1,)])
                for qs in range(2):
                    q0 = hc * TT + w * WIN + qs * 128
                    i = tensor.matmul(scb(u)[:, qs * WIN:(qs + 1) * WIN],
                                      Q[off:off + 64, q0:q0 + 128],
                                      kw, start=True, stop=True)
                    assert S.inc(i, "TEN") == sc_at[(w, h, qs)]

            @block.tensor
            def _(tensor):
                GL.wait(tensor, ldq)
                GL.wait(tensor, ldk)
                GL.wait(tensor, ldv)
                for u in range(3):
                    emit_sc(tensor, u)
                for u, (w, h) in enumerate(units):
                    if u + 3 < NU:
                        emit_sc(tensor, u + 3)
                    if u >= 3:
                        S.wait(tensor, "SCA", et_at[units[u - 3]])
                    for qs in range(2):
                        S.wait(tensor, "VEC", nm_at[(w, h, qs)])
                        for ks in range(2):
                            i = tensor.transpose(
                                tpb(u)[:, ks * WIN + qs * 128:
                                       ks * WIN + (qs + 1) * 128],
                                EN3[u % 3][:, qs * WIN + ks * 128:
                                           qs * WIN + (ks + 1) * 128],
                                CST["ident"][:])
                            assert S.inc(i, "TEN") == tp_at[(w, h, qs, ks)]
                    S.wait(tensor, "SCA", et_at[(w, h)])
                    if h >= 2:
                        S.wait(tensor, "SCA", ye_at[(w, h // 2 - 1)])
                    off_y = (h % 2) * 64
                    for ks in range(2):
                        s4 = (w * 2 + ks) * 1024
                        i = tensor.matmul(
                            PS[6 + w][off_y:off_y + 64, 0:WIN],
                            V4[:, s4 + h * 64:s4 + (h + 1) * 64],
                            ET3[u % 3][ks][:, 0:WIN],
                            start=(ks == 0), stop=(ks == 1))
                    assert S.inc(i, "TEN") == pv_at[(w, h)]
                S.wait(tensor, "VEC", ysp)
                for h in range(DC):
                    GL.wait(tensor, wo_at[h][0])
                    GL.wait(tensor, wo_at[h][1])
                    if h >= 2:
                        S.wait(tensor, "VEC", xn_at[h - 2])
                    for c in range(DC):
                        wh = WOB[:, (h % 2) * 2048 + c * 128:
                                 (h % 2) * 2048 + (c + 1) * 128]
                        wl = WOB[:, (h % 2) * 2048 + 1024 + c * 128:
                                 (h % 2) * 2048 + 1024 + (c + 1) * 128]
                        yh = YH[:, c * TT:(c + 1) * TT]
                        yl = YL[:, c * TT:(c + 1) * TT]
                        tensor.matmul(PS[h % 2][:, 0:TT], wh, yh,
                                      start=(c == 0), stop=False)
                        tensor.matmul(PS[h % 2][:, 0:TT], wh, yl,
                                      start=False, stop=False)
                        i = tensor.matmul(PS[h % 2][:, 0:TT], wl, yh,
                                          start=False, stop=(c == DC - 1))
                    assert S.inc(i, "TEN") == womm_at[h]

            @bvec
            def _(vector):
                for u in range(NU + 1):
                    if u < NU:
                        w, h = units[u]
                        for qs in range(2):
                            if qs == 0:
                                S.wait(vector, "TEN", sc_at[(w, h, 1)])
                                if u >= 3:
                                    S.wait(vector, "SCA",
                                           ex_at[units[u - 3] + (1,)])
                            sm = SM3[u % 3][:, qs * WIN:(qs + 1) * WIN]
                            vector.scalar_tensor_tensor(
                                sm, scb(u)[:, qs * WIN:(qs + 1) * WIN], 0.125,
                                CST[f"maskq{qs}"][:], AL.mult, AL.min)
                            vector.tensor_reduce(
                                NMC[:, (u % 3) * 2 + qs:(u % 3) * 2 + qs + 1],
                                sm, AX, AL.max, negate=True)
                            assert S.cinc(vector, "VEC") == ms_at[(w, h, qs)]
                    if u >= 1:
                        w, h = units[u - 1]
                        l = (u - 1) % 3
                        en_f = EN3[l]
                        for qs in range(2):
                            S.wait(vector, "SCA", ex_at[(w, h, qs)])
                            en = en_f[:, qs * WIN:(qs + 1) * WIN]
                            vector.reciprocal(
                                NMC[:, 12 + l * 2 + qs:13 + l * 2 + qs],
                                NMC[:, 6 + l * 2 + qs:7 + l * 2 + qs])
                            vector.tensor_scalar(
                                en, en,
                                NMC[:, 12 + l * 2 + qs:13 + l * 2 + qs],
                                None, AL.mult)
                            assert S.cinc(vector, "VEC") == nm_at[(w, h, qs)]
                S.wait(vector, "SCA", ye_at[(1, NH // 2 - 1)])
                vector.tensor_copy(YH[:], Y[:])
                vector.tensor_tensor(YL[:], Y[:], YH[:], AL.subtract)
                assert S.cinc(vector, "VEC") == ysp
                for h in range(DC):
                    S.wait(vector, "TEN", womm_at[h])
                    GC.wait(vector, xo_at[h])
                    if h >= 2:
                        GS.wait(vector, st_at[h - 2])
                    vector.tensor_tensor(TSL(14 + h % 2)[:],
                                         PS[h % 2][:, 0:TT],
                                         XOL[h % 2][:], AL.add)
                    assert S.cinc(vector, "VEC") == xn_at[h]

            @bsca
            def _(scalar):
                for u in range(NU + 1):
                    if u < NU:
                        w, h = units[u]
                        l = u % 3
                        for qs in range(2):
                            S.wait(scalar, "VEC", ms_at[(w, h, qs)])
                            if qs == 0 and u >= 3:
                                S.wait(scalar, "TEN",
                                       tp_at[units[u - 3] + (1, 1)])
                            scalar.activation(
                                EN3[l][:, qs * WIN:(qs + 1) * WIN],
                                SM3[l][:, qs * WIN:(qs + 1) * WIN], AF.Exp,
                                bias=NMC[:, l * 2 + qs:l * 2 + qs + 1],
                                scale=1.0,
                                accum_out=NMC[:, 6 + l * 2 + qs:
                                              7 + l * 2 + qs])
                            assert S.cinc(scalar, "SCA") == ex_at[(w, h, qs)]
                    if u >= 1:
                        w, h = units[u - 1]
                        lp = (u - 1) % 3
                        S.wait(scalar, "TEN", tp_at[(w, h, 1, 1)])
                        scalar.copy(ET3[lp][0][:, 0:WIN],
                                    tpb(u - 1)[:, 0:WIN])
                        scalar.copy(ET3[lp][1][:, 0:WIN],
                                    tpb(u - 1)[:, WIN:2 * WIN])
                        assert S.cinc(scalar, "SCA") == et_at[(w, h)]
                        if h % 2 == 1:
                            S.wait(scalar, "TEN", pv_at[(w, h)])
                            hp = h // 2
                            scalar.copy(Y[:, hp * TT + w * WIN:
                                          hp * TT + (w + 1) * WIN],
                                        PS[6 + w][:, 0:WIN])
                            assert S.cinc(scalar, "SCA") == ye_at[(w, hp)]

            @block.gpsimd
            def _(gp):
                for h in range(DC + 1):
                    if h < DC:
                        if h >= 2:
                            S.wait(gp, "VEC", xn_at[h - 2])
                        i = gp.dma_start(
                            out=XOL[h % 2][:],
                            in_=ap[x_src][h * 128:(h + 1) * 128,
                                          wp * TT:(wp + 1) * TT])
                        GC.issue(i, h % 2, xo_at[h])
                    if h >= 1:
                        S.wait(gp, "VEC", xn_at[h - 1])
                        i = gp.dma_start(
                            out=ap[x_dst][(h - 1) * 128:h * 128,
                                          wp * TT:(wp + 1) * TT],
                            in_=TSL(14 + (h - 1) % 2)[:])
                        GS.issue(i, (h - 1) % 2, st_at[h - 1])
                GS.wait_all(gp)

            my_barrier()

        def emit_attn(x_src, x_dst):
            if active():
                cl = GL.plan_counters()
                la = DmaGroup.plan(cl, 0, GL.K)
                lb = DmaGroup.plan(cl, 1, GL.K)

                @block.sync
                def _(sync):
                    i = sync.dma_start(out=ROPE_A[:], in_=ap["cropeA"])
                    GL.issue(i, 0, la)
                    i = sync.dma_start(out=ROPE_B[:], in_=ap["cropeB"])
                    GL.issue(i, 1, lb)
                    GL.wait(sync, la)
                    GL.wait(sync, lb)

                @bvec
                def _(vector):
                    for c in range(DC):
                        vector.tensor_scalar(GBC[0:2, c * 128:(c + 1) * 128],
                                             CST["rmsbc"][0:2, 0:128],
                                             QG[0:2, c:c + 1], None, AL.mult)
                        S.cinc(vector, "VEC")

                my_barrier()
                for t in range(NT):
                    emit_a2_tile(t, x_src)
            if active():
                for wp in range(NT):
                    emit_a3_wp(wp, x_src, x_dst)

        emit_ffn("x", "xmid", 0)
        emit_attn("xmid", "xmid2")
        emit_ffn("xmid2", "y", 5)

        @block.gpsimd
        def _(gp):
            GS.wait_all(gp)

    return nc


def kernel(**inputs):
    x = np.ascontiguousarray(inputs["L0_attn_x"])
    per_core = []
    for b in range(N_CORES):
        m = {
            "x": np.ascontiguousarray(x[b].T),
            "wq": np.ascontiguousarray(inputs["L0_attn_wq"]),
            "wk": np.ascontiguousarray(inputs["L0_attn_wk"]),
            "wv": np.ascontiguousarray(inputs["L0_attn_wv"]),
            "wo": np.ascontiguousarray(inputs["L0_attn_wo"]),
            "wg": np.ascontiguousarray(inputs["L0_ffn_w_gate"]),
            "wu": np.ascontiguousarray(inputs["L0_ffn_w_up"]),
            "wd": np.ascontiguousarray(inputs["L0_ffn_w_down"]),
            "qg": np.ascontiguousarray(
                inputs["L0_attn_q_gain"].reshape(NH // 2, 2).T),
        }
        for n, v in _CONSTS.items():
            m["c" + n] = v
        per_core.append(m)
    nc = build_nc()
    res = run_bass_kernel_spmd(nc, per_core, core_ids=list(range(N_CORES)))
    out = np.empty((N_CORES, T, D), np.float32)
    for b in range(N_CORES):
        out[b] = res.results[b]["y"].T
    return out


# version 09
# speedup vs baseline: 1.0454x; 1.0454x over previous
import contextlib
import os

import numpy as np

import concourse.bass as bass
import concourse.mybir as mybir
from concourse.bass_utils import run_bass_kernel_spmd

F32 = mybir.dt.float32
BF16 = mybir.dt.bfloat16
I16 = mybir.dt.int16
U16 = mybir.dt.uint16
I8 = mybir.dt.int8
AL = mybir.AluOpType
AF = mybir.ActivationFunctionType
AX = mybir.AxisListType.X

D = 1024
T = 4096
FI = 3072
NH = 16
HD = 64
WIN = 256
DC = D // 128
IC = FI // 128
TT = 512
NT = T // TT
NW = T // WIN

MAGIC = float(np.float32(12582912.0))
FMIN = float(np.finfo(np.float32).min)
FMAX = float(np.finfo(np.float32).max)
LN_EPS = 1e-5
RMS_EPS = 1.1920928955078125e-07
N_CORES = 8

_DBG = os.environ.get("MK_DBG", "") == "1"
_NPH = int(os.environ.get("MK_PHASES", "99"))


def _host_constants():
    c = {}
    c["ident"] = np.eye(128, dtype=np.float32)
    perm = np.zeros((128, 128), np.float32)
    for p in range(128):
        perm[p, p ^ 32] = 1.0
    c["perm32"] = perm
    half = HD // 2
    inv_freq = (1.0 / (10000.0 ** (np.arange(0, HD, 2, dtype=np.float32) / np.float32(HD)))).astype(np.float32)
    freqs = np.outer(np.arange(T, dtype=np.float32), inv_freq).astype(np.float32)
    cosv, sinv = np.cos(freqs).astype(np.float32), np.sin(freqs).astype(np.float32)
    A = np.zeros((128, T), np.float32)
    B = np.zeros((128, T), np.float32)
    for p in range(128):
        j = p % 64
        jj = j if j < half else j - half
        A[p] = cosv[:, jj]
        B[p] = sinv[:, jj] if j < half else -sinv[:, jj]
    c["ropeA"], c["ropeB"] = A, B
    for s in range(2):
        m = np.full((128, WIN), FMIN, np.float32)
        for r in range(128):
            m[r, : s * 128 + r + 1] = FMAX
        c[f"maskq{s}"] = m
    c["ones128"] = np.ones((128, 1), np.float32)
    c["ones1x"] = np.ones((1, 128), np.float32)
    rmsmask = np.zeros((128, 2), np.float32)
    rmsmask[:64, 0] = 1.0
    rmsmask[64:, 1] = 1.0
    c["rmsmask"] = rmsmask
    rmsbc = np.zeros((2, 128), np.float32)
    rmsbc[0, :64] = 1.0
    rmsbc[1, 64:] = 1.0
    c["rmsbc"] = rmsbc
    return c


_CONSTS = _host_constants()


class Sems:

    def __init__(self):
        self.sem = {}
        self.cnt = {}

    def add(self, nc, ctx, names):
        for n in names:
            self.sem[n] = ctx.enter_context(nc.semaphore(n))
            self.cnt[n] = 0

    def inc(self, instr, name, n=1):
        instr.then_inc(self.sem[name], n)
        self.cnt[name] += n
        return self.cnt[name]

    def wait(self, eng, name, v):
        if v > 0:
            eng.wait_ge(self.sem[name], v)

    def cinc(self, chain, name, n=1):
        chain.flush()
        chain.e.sem_inc(self.sem[name], n)
        self.cnt[name] += n
        return self.cnt[name]


class Chain:

    EMITS = {"tensor_scalar", "tensor_tensor", "scalar_tensor_tensor",
             "tensor_reduce", "tensor_copy", "tensor_scalar_mul",
             "tensor_add", "tensor_sub", "tensor_mul", "reciprocal",
             "memset", "activation", "copy", "sqrt", "mul", "add",
             "square"}

    def __init__(self, eng, S, sem):
        self.e = eng
        self.S = S
        self.semname = sem
        self.waited = S.cnt[sem]

    def flush(self):
        if self.S.cnt[self.semname] > self.waited:
            self.e.wait_ge(self.S.sem[self.semname], self.S.cnt[self.semname])
            self.waited = self.S.cnt[self.semname]

    def __getattr__(self, n):
        f = getattr(self.e, n)
        if n not in Chain.EMITS:
            return f

        def g(*a, **k):
            self.flush()
            i = f(*a, **k)
            self.S.cnt[self.semname] += 1
            i.then_inc(self.S.sem[self.semname], 1)
            return i
        return g


class DmaGroup:

    def __init__(self, nc, ctx, name, K):
        self.K = K
        self.sems = [ctx.enter_context(nc.semaphore(f"{name}{k}"))
                     for k in range(K)]
        self.cnt = [0] * K

    def plan_counters(self):
        return list(self.cnt)

    @staticmethod
    def plan(counters, slot, K):
        k = slot % K
        counters[k] += 16
        return (k, counters[k])

    def issue(self, instr, slot, expect=None):
        k = slot % self.K
        instr.then_inc(self.sems[k], 16)
        self.cnt[k] += 16
        tok = (k, self.cnt[k])
        if expect is not None:
            assert tok == expect, (tok, expect)
        return tok

    def wait(self, eng, tok):
        if tok is not None:
            eng.wait_ge(self.sems[tok[0]], tok[1])

    def wait_all(self, eng):
        for k, c in enumerate(self.cnt):
            if c:
                eng.wait_ge(self.sems[k], c)


def build_nc():
    nc = bass.Bass("TRN2", target_bir_lowering=False)
    ap = {}

    def din(name, shape, dt=F32):
        ap[name] = nc.dram_tensor(name, shape, dt, kind="ExternalInput").ap()

    def dint(name, shape, dt=F32, dbg=False):
        kind = "ExternalOutput" if (dbg and _DBG) else "Internal"
        ap[name] = nc.dram_tensor(name, shape, dt, kind=kind).ap()

    din("x", [D, T])
    for n, sh in (("wq", [D, D]), ("wk", [D, D]), ("wv", [D, D]), ("wo", [D, D]),
                  ("wg", [FI, D]), ("wu", [FI, D]), ("wd", [D, FI])):
        din(n, sh)
    din("qg", [2, NH // 2])
    for n, v in _CONSTS.items():
        din("c" + n, list(v.shape))
    ap["y"] = nc.dram_tensor("y", [D, T], F32, kind="ExternalOutput").ap()

    for n, sh in (("wqT", [D, D]), ("wkT", [D, D]), ("wvT", [D, D]),
                  ("woT", [D, D]), ("wgT", [D, FI]), ("wuT", [D, FI]),
                  ("wdT", [FI, D])):
        dint(n + "h", sh, BF16, dbg=(n == "wgT"))
        dint(n + "l", sh, BF16)
    for n in ("wg", "wu", "wd"):
        dint("q8" + n, list(ap[n].shape), I8)
    dint("up_raw", [FI, T], dbg=True)
    dint("gate_raw", [FI, T])
    dint("uq16", [FI, T], I16)
    dint("gq16", [FI, T], I16)
    dint("sq16", [FI, T], U16)
    dint("g2q16", [FI, T], I16, dbg=True)
    dint("xmid", [D, T], dbg=True)
    dint("xmid2", [D, T], dbg=True)
    dint("qrot", [D, T], dbg=True)
    dint("krot", [D, T])
    dint("vtm", [T, D], dbg=True)
    for k in range(10):
        dint(f"cci{k}", [1, 1])
        dint(f"cco{k}", [1, 1])
        dint(f"colb{k}", [128, 1])
        dint(f"sdr{k}", [1, 2])

    with contextlib.ExitStack() as ctx:
        ent = ctx.enter_context
        ARENA = ent(nc.sbuf_tensor("ARENA", [128, 32768], F32))
        POOL = ent(nc.sbuf_tensor("POOL", [128, 12288], F32))
        STAT = ent(nc.sbuf_tensor("STAT", [128, 768], F32))
        SCAL = ent(nc.sbuf_tensor("SCAL", [128, 16], F32))
        ROWS = ent(nc.sbuf_tensor("ROWS", [1, 4 * TT], F32))
        RMSW = ent(nc.sbuf_tensor("RMSW", [2, 4 * TT], F32))
        CST = {n: ent(nc.sbuf_tensor("cs" + n, list(v.shape), F32))
               for n, v in _CONSTS.items() if n not in ("ropeA", "ropeB")}
        QG = ent(nc.sbuf_tensor("qg_sb", [2, NH // 2], F32))
        EPS = ent(nc.sbuf_tensor("eps_sb", [128, 3], F32))
        PS = [ent(nc.psum_tensor(f"ps{k}", [128, 512], F32)) for k in range(8)]

        S = Sems()
        S.add(nc, ctx, ["TEN", "VEC", "SCA", "CCS", "VE2", "SC2"])
        GL = DmaGroup(nc, ctx, "GL", 8)
        GS = DmaGroup(nc, ctx, "GS", 8)
        GC = DmaGroup(nc, ctx, "GC", 8)
        GA = DmaGroup(nc, ctx, "GA", 1)
        BAR = ctx.enter_context(nc.semaphore("BAR"))

        block = ent(nc.Block())

        def bvec(f):
            block.vector(lambda v: f(Chain(v, S, "VE2")))

        def bsca(f):
            block.scalar(lambda s2: f(Chain(s2, S, "SC2")))

        barctr = [0]

        def my_barrier():
            barctr[0] += 1
            tgt = 5 * barctr[0]
            for name in ("sync", "scalar", "tensor", "vector", "gpsimd"):
                def f(eng):
                    eng.drain().then_inc(BAR, 1)
                    eng.wait_ge(BAR, tgt)
                getattr(block, name)(f)

        def ld3(eng, out2d, src2d):
            C = src2d.shape[0] // 128
            return eng.dma_start(
                out=out2d.rearrange("p (c t) -> p c t", c=C),
                in_=src2d.rearrange("(c p) t -> p c t", p=128))

        def TSL(k):
            return POOL[:, k * 512:(k + 1) * 512]

        def AR_(a, b):
            return ARENA[:, a:b]

        SLOT = {"s1": 0, "i1": 1, "s2": 2, "i2": 3, "s3": 4, "i3": 5,
                "s4": 6, "i4": 7, "s5": 8, "i5": 9, "s42": 10, "s3c": 11, "h5": 12}

        def SCs(name):
            return SCAL[:, SLOT[name]:SLOT[name] + 1]

        M0 = STAT[:, 0:192]
        M1 = STAT[:, 192:384]
        PMcol = STAT[:, 384:385]
        Grow = STAT[0:1, 385:386]
        SROW = STAT[0:1, 386:388]
        ROWB = STAT[0:1, 388:516]
        NMC = STAT[:, 520:538]

        murow = ROWS[0:1, 0:TT]
        rrow = ROWS[0:1, TT:2 * TT]
        var_row = ROWS[0:1, 2 * TT:3 * TT]
        sd_row = ROWS[0:1, 3 * TT:4 * TT]

        ROPE_A = AR_(24576, 28672)
        ROPE_B = AR_(28672, 32768)
        GBC = ARENA[0:2, 0:1024]

        ph = [0]

        def active():
            ph[0] += 1
            return ph[0] <= _NPH

        @block.sync
        def _(sync):
            names = [n for n in _CONSTS if n not in ("ropeA", "ropeB")]
            for idx, n in enumerate(names):
                i = sync.dma_start(out=CST[n][:], in_=ap["c" + n])
                GL.issue(i, idx)
            i = sync.dma_start(out=QG[:], in_=ap["qg"])
            GL.issue(i, len(names))
            GL.wait_all(sync)

        @bvec
        def _(vector):
            vector.memset(EPS[:, 0:1], LN_EPS)
            vector.memset(EPS[:, 1:2], RMS_EPS)
            vector.memset(EPS[:, 2:3], MAGIC)

        my_barrier()

        def emit_wprep():
            plans = [("wq", D, D, False), ("wk", D, D, False),
                     ("wv", D, D, False), ("wo", D, D, False),
                     ("wg", FI, D, True), ("wu", FI, D, True),
                     ("wd", D, FI, True)]
            loads = []
            for (n, rows, cols, quant) in plans:
                for oc in range(rows // 128):
                    loads.append((n, oc, cols, quant))
            NL = len(loads)

            wbuf = [AR_(0, 3072), AR_(3072, 6144)]
            qf = [AR_(6144, 9216), AR_(9216, 12288)]
            dq = [AR_(12288, 15360), AR_(15360, 18432)]
            sA = [M0[:, 0:96], M0[:, 96:192]]
            sR = [M1[:, 0:96], M1[:, 96:192]]
            EV = AR_(18432, 18944)
            EVH = AR_(18944, 19200).bitcast(BF16)
            EVL = AR_(19200, 19456).bitcast(BF16)

            b_vec, b_ten, b_sca = S.cnt["VEC"], S.cnt["TEN"], S.cnt["SCA"]
            cl, cc_, cs_ = (GL.plan_counters(), GC.plan_counters(),
                            GS.plan_counters())
            ld_at = [None] * NL
            q8st_at = [None] * NL
            q8ld_at = [None] * NL
            qrdy_at = [0] * NL
            deq_at = [0] * NL
            ntp = [0] * NL
            tp0 = [0] * NL
            g = 0
            for idx, (n, oc, cols, quant) in enumerate(loads):
                ld_at[idx] = DmaGroup.plan(cl, idx % 2, GL.K)
                if quant:
                    q8st_at[idx] = DmaGroup.plan(cc_, idx % 2, GC.K)
                    q8ld_at[idx] = DmaGroup.plan(cc_, idx % 2, GC.K)
                ntp[idx] = cols // 128
                tp0[idx] = g
                g += ntp[idx]
            NG = g
            isq = [None] * NG
            for idx in range(NL):
                for k in range(ntp[idx]):
                    isq[tp0[idx] + k] = idx
            vec = b_vec
            vsp_at = [0] * NG
            for idx in range(NL + 1):
                if idx < NL:
                    if loads[idx][3]:
                        vec += 1
                        qrdy_at[idx] = vec
                    vec += 1
                    deq_at[idx] = vec
                if idx >= 1:
                    for k in range(ntp[idx - 1]):
                        vec += 1
                        vsp_at[tp0[idx - 1] + k] = vec
            ten_at = [b_ten + j + 1 for j in range(NG)]
            sca_at = [b_sca + j + 1 for j in range(NG)]
            sth_at = [None] * NG
            stl_at = [None] * NG
            st_at = [None] * NG
            for idx in range(NL + 1):
                if idx >= 1:
                    for k in range(ntp[idx - 1]):
                        j = tp0[idx - 1] + k
                        sth_at[j] = DmaGroup.plan(cs_, j % 4, GS.K)
                        stl_at[j] = DmaGroup.plan(cs_, 4 + j % 4, GS.K)

            @block.sync
            def _(sync):
                for idx, (n, oc, cols, quant) in enumerate(loads):
                    b = idx % 2
                    if idx >= 2:
                        S.wait(sync, "TEN", ten_at[tp0[idx - 2] + ntp[idx - 2] - 1])
                    i = sync.dma_start(out=wbuf[b][:, 0:cols],
                                       in_=ap[n][oc * 128:(oc + 1) * 128, :])
                    GL.issue(i, idx % 2, ld_at[idx])

            @bvec
            def _(vector):
                for idx in range(NL + 1):
                    if idx < NL:
                        n, oc, cols, quant = loads[idx]
                        b = idx % 2
                        GL.wait(vector, ld_at[idx])
                        if quant:
                            nb = cols // 32
                            w3 = wbuf[b][:, 0:cols].rearrange("p (b k) -> p b k", k=32)
                            vector.tensor_reduce(
                                sA[b][:, 0:nb].rearrange("p (b o) -> p b o", o=1),
                                w3, AX, AL.max, apply_absolute_value=True)
                            vector.tensor_scalar(sA[b][:, 0:nb], sA[b][:, 0:nb],
                                                 1.0 / 7.0, 1e-12, AL.mult, AL.max)
                            vector.reciprocal(sR[b][:, 0:nb], sA[b][:, 0:nb])
                            rbc = sR[b][:, 0:nb].rearrange("p (b o) -> p b o", o=1).broadcast_to([128, nb, 32])
                            vector.tensor_tensor(
                                qf[b][:, 0:cols].rearrange("p (b k) -> p b k", k=32),
                                w3, rbc, AL.mult)
                            vector.tensor_scalar(qf[b][:, 0:cols], qf[b][:, 0:cols],
                                                 1.0, MAGIC, AL.mult, AL.add)
                            vector.tensor_scalar(qf[b][:, 0:cols], qf[b][:, 0:cols],
                                                 MAGIC, 7.0, AL.subtract, AL.min)
                            assert S.cinc(vector, "VEC") == qrdy_at[idx]
                            GC.wait(vector, q8ld_at[idx])
                            sbc = sA[b][:, 0:nb].rearrange("p (b o) -> p b o", o=1).broadcast_to([128, nb, 32])
                            vector.tensor_tensor(
                                dq[b][:, 0:cols].rearrange("p (b k) -> p b k", k=32),
                                qf[b][:, 0:cols].rearrange("p (b k) -> p b k", k=32),
                                sbc, AL.mult)
                            assert S.cinc(vector, "VEC") == deq_at[idx]
                        else:
                            vector.tensor_copy(dq[b][:, 0:cols], wbuf[b][:, 0:cols])
                            assert S.cinc(vector, "VEC") == deq_at[idx]
                    if idx >= 1:
                        for k in range(ntp[idx - 1]):
                            j = tp0[idx - 1] + k
                            S.wait(vector, "SCA", sca_at[j])
                            if j >= 4:
                                GS.wait(vector, sth_at[j - 4])
                                GS.wait(vector, stl_at[j - 4])
                            sl_ = slice((j % 4) * 128, (j % 4 + 1) * 128)
                            vector.tensor_copy(EVH[:, sl_], EV[:, sl_])
                            vector.tensor_tensor(EVL[:, sl_], EV[:, sl_],
                                                 EVH[:, sl_], AL.subtract)
                            assert S.cinc(vector, "VEC") == vsp_at[j]

            @block.gpsimd
            def _(gp):
                for idx in range(NL + 1):
                    if idx < NL:
                        n, oc, cols, quant = loads[idx]
                        b = idx % 2
                        if quant:
                            S.wait(gp, "VEC", qrdy_at[idx])
                            i = gp.dma_start(
                                out=ap["q8" + n][oc * 128:(oc + 1) * 128, :],
                                in_=qf[b][:, 0:cols])
                            GC.issue(i, b, q8st_at[idx])
                            GC.wait(gp, q8st_at[idx])
                            i = gp.dma_start(
                                out=qf[b][:, 0:cols],
                                in_=ap["q8" + n][oc * 128:(oc + 1) * 128, :])
                            GC.issue(i, b, q8ld_at[idx])
                    if idx >= 1:
                        n, oc, cols, quant = loads[idx - 1]
                        for k in range(ntp[idx - 1]):
                            j = tp0[idx - 1] + k
                            sl_ = slice((j % 4) * 128, (j % 4 + 1) * 128)
                            S.wait(gp, "VEC", vsp_at[j])
                            i = gp.dma_start(
                                out=ap[n + "Th"][k * 128:(k + 1) * 128,
                                                 oc * 128:(oc + 1) * 128],
                                in_=EVH[:, sl_])
                            GS.issue(i, j % 4, sth_at[j])
                            i = gp.dma_start(
                                out=ap[n + "Tl"][k * 128:(k + 1) * 128,
                                                 oc * 128:(oc + 1) * 128],
                                in_=EVL[:, sl_])
                            GS.issue(i, 4 + j % 4, stl_at[j])
                GS.wait_all(gp)

            @block.tensor
            def _(tensor):
                for idx, (n, oc, cols, quant) in enumerate(loads):
                    b = idx % 2
                    S.wait(tensor, "VEC", deq_at[idx])
                    for k in range(ntp[idx]):
                        j = tp0[idx] + k
                        if j >= 4:
                            S.wait(tensor, "SCA", sca_at[j - 4])
                        i = tensor.transpose(PS[4 + j % 4][:, 0:128],
                                             dq[b][:, k * 128:(k + 1) * 128],
                                             CST["ident"][:])
                        assert S.inc(i, "TEN") == ten_at[j]

            @bsca
            def _(scalar):
                for j in range(NG):
                    S.wait(scalar, "TEN", ten_at[j])
                    if j >= 4:
                        S.wait(scalar, "VEC", vsp_at[j - 4])
                    i = scalar.copy(EV[:, (j % 4) * 128:(j % 4 + 1) * 128],
                                    PS[4 + j % 4][:, 0:128])
                    assert S.cinc(scalar, "SCA") == sca_at[j]

        if active():
            emit_wprep()
            my_barrier()

        def emit_allreduce(k, statr, ncols, s_slot, i_slot):
            b_vec, b_ccs = S.cnt["VEC"], S.cnt["CCS"]
            v1, v2, v3 = b_vec + 1, b_vec + 2, b_vec + 3
            ca = GA.plan_counters()
            g = [DmaGroup.plan(ca, 0, 1) for _ in range(7)]
            cc1 = b_ccs + 1

            @bvec
            def _(vector):
                vector.tensor_reduce(PMcol[:], statr[:, 0:ncols], AX, AL.max)
                assert S.cinc(vector, "VEC") == v1
                GA.wait(vector, g[1])
                vector.tensor_reduce(Grow[:], ROWB[:], AX, AL.max)
                assert S.cinc(vector, "VEC") == v2
                GA.wait(vector, g[3])
                vector.tensor_scalar(SROW[0:1, 0:1], Grow[:],
                                         1.0 / 32767.0, 1e-12,
                                         AL.mult, AL.max)
                vector.reciprocal(SROW[0:1, 1:2], SROW[0:1, 0:1])
                assert S.cinc(vector, "VEC") == v3

            @block.gpsimd
            def _(gp):
                S.wait(gp, "VEC", v1)
                i = gp.dma_start(out=ap[f"colb{k}"], in_=PMcol[:])
                GA.issue(i, 0, g[0])
                GA.wait(gp, g[0])
                i = gp.dma_start(out=ROWB[:], in_=ap[f"colb{k}"].transpose([1, 0]))
                GA.issue(i, 0, g[1])
                S.wait(gp, "VEC", v2)
                i = gp.dma_start(out=ap[f"cci{k}"], in_=Grow[:])
                GA.issue(i, 0, g[2])
                GA.wait(gp, g[2])
                i = gp.collective_compute(
                    "AllReduce", AL.max, replica_groups=[list(range(N_CORES))],
                    ins=[ap[f"cci{k}"]], outs=[ap[f"cco{k}"]])
                assert S.inc(i, "CCS") == cc1
                S.wait(gp, "CCS", cc1)
                i = gp.dma_start(out=Grow[:], in_=ap[f"cco{k}"])
                GA.issue(i, 0, g[3])
                S.wait(gp, "VEC", v3)
                i = gp.dma_start(out=ap[f"sdr{k}"], in_=SROW[:])
                GA.issue(i, 0, g[4])
                GA.wait(gp, g[4])
                i = gp.dma_start(out=SCs(s_slot),
                                 in_=ap[f"sdr{k}"][0:1, 0:1].broadcast_to([128, 1]))
                GA.issue(i, 0, g[5])
                GA.wait(gp, g[5])
                i = gp.dma_start(out=SCs(i_slot),
                                 in_=ap[f"sdr{k}"][0:1, 1:2].broadcast_to([128, 1]))
                GA.issue(i, 0, g[6])
                GA.wait(gp, g[6])

            my_barrier()

        def emit_ln(x_src, consume, ninc_per_chunk):
            XT2 = [POOL[:, 0:4096], POOL[:, 4096:8192]]
            SQB = POOL[:, 8192:12288]
            b_ten, b_sca, b_vec = S.cnt["TEN"], S.cnt["SCA"], S.cnt["VEC"]
            cl = GL.plan_counters()
            ld_at = [DmaGroup.plan(cl, t % 2, GL.K) for t in range(NT)]
            sq_at = [b_sca + 2 * t + 1 for t in range(NT)]
            sd_at = [b_sca + 2 * t + 2 for t in range(NT)]
            stats_at = [b_ten + 2 * t + 1 for t in range(NT)]
            bcast_at = [b_ten + 2 * t + 2 for t in range(NT)]
            vpt = 2 + DC * ninc_per_chunk
            row_at = [b_vec + vpt * t + 1 for t in range(NT)]
            rr_at = [b_vec + vpt * t + 2 for t in range(NT)]
            tile_at = [b_vec + vpt * (t + 1) for t in range(NT)]

            @block.sync
            def _(sync):
                for t in range(NT):
                    if t >= 2:
                        S.wait(sync, "VEC", tile_at[t - 2])
                    i = ld3(sync, XT2[t % 2][:],
                            ap[x_src][:, t * TT:(t + 1) * TT])
                    GL.issue(i, t % 2, ld_at[t])

            @bsca
            def _(scalar):
                for t in range(NT):
                    GL.wait(scalar, ld_at[t])
                    if t >= 1:
                        S.wait(scalar, "TEN", stats_at[t - 1])
                        S.wait(scalar, "VEC", tile_at[t - 1])
                    i = scalar.activation(SQB[:], XT2[t % 2][:], AF.Square)
                    assert S.cinc(scalar, "SCA") == sq_at[t]
                    S.wait(scalar, "VEC", row_at[t])
                    i = scalar.activation(sd_row, var_row, AF.Sqrt,
                                          bias=EPS[0:1, 0:1], scale=1.0)
                    assert S.cinc(scalar, "SCA") == sd_at[t]

            @block.tensor
            def _(tensor):
                for t in range(NT):
                    GL.wait(tensor, ld_at[t])
                    S.wait(tensor, "SCA", sq_at[t])
                    if t >= 1:
                        S.wait(tensor, "VEC", row_at[t - 1])
                        S.wait(tensor, "VEC", tile_at[t - 1])
                    for c in range(DC):
                        tensor.matmul(PS[0][0:1, 0:TT], CST["ones128"][:],
                                      XT2[t % 2][:, c * TT:(c + 1) * TT],
                                      start=(c == 0), stop=(c == DC - 1))
                    for c in range(DC):
                        i = tensor.matmul(PS[1][0:1, 0:TT], CST["ones128"][:],
                                          SQB[:, c * TT:(c + 1) * TT],
                                          start=(c == 0), stop=(c == DC - 1))
                    assert S.inc(i, "TEN") == stats_at[t]
                    S.wait(tensor, "VEC", rr_at[t])
                    tensor.matmul(PS[2][:, 0:TT], CST["ones1x"][:],
                                  murow, start=True, stop=True)
                    i = tensor.matmul(PS[3][:, 0:TT], CST["ones1x"][:],
                                      rrow, start=True, stop=True)
                    assert S.inc(i, "TEN") == bcast_at[t]

            @bvec
            def _(vector):
                for t in range(NT):
                    S.wait(vector, "TEN", stats_at[t])
                    vector.tensor_scalar(murow, PS[0][0:1, 0:TT],
                                         1.0 / float(D), None, AL.mult)
                    vector.tensor_scalar(var_row, PS[1][0:1, 0:TT],
                                         1.0 / float(D), None, AL.mult)
                    vector.tensor_tensor(sd_row, murow, murow, AL.mult)
                    vector.tensor_tensor(var_row, var_row, sd_row,
                                             AL.subtract)
                    assert S.cinc(vector, "VEC") == row_at[t]
                    S.wait(vector, "SCA", sd_at[t])
                    vector.reciprocal(rrow, sd_row)
                    assert S.cinc(vector, "VEC") == rr_at[t]
                    S.wait(vector, "TEN", bcast_at[t])
                    for c in range(DC):
                        consume(vector, t, c, XT2[t % 2][:, c * TT:(c + 1) * TT])
                    assert S.cnt["VEC"] == tile_at[t]

        def emit_ffn(x_src, x_dst, ccb):
            def p0a_consume(vector, t, c, xc):
                tmp = AR_(0, 512)
                tmp2 = AR_(512, 1024)
                vector.tensor_tensor(tmp, xc, PS[2][:, 0:TT], AL.subtract)
                vector.tensor_tensor(tmp2, tmp, PS[3][:, 0:TT], AL.mult)
                vector.tensor_reduce(M0[:, t * DC + c:t * DC + c + 1],
                                         tmp2, AX, AL.max,
                                         apply_absolute_value=True)
                S.cinc(vector, "VEC")

            if active():
                emit_ln(x_src, p0a_consume, 1)
                my_barrier()
                emit_allreduce(ccb + 0, M0, 64, "s1", "i1")

            XQH = ARENA[:, 0:16384].bitcast(BF16)
            XQL = ARENA[:, 16384:32768].bitcast(BF16)

            def xqh_ct(c, t):
                return XQH[:, c * 4096 + t * TT:c * 4096 + (t + 1) * TT]

            def xql_ct(c, t):
                return XQL[:, c * 4096 + t * TT:c * 4096 + (t + 1) * TT]

            def p0b_consume(vector, t, c, xc):
                tmp = POOL[:, 8192:8704]
                tmp2 = POOL[:, 8704:9216]
                vector.tensor_tensor(tmp, xc, PS[2][:, 0:TT], AL.subtract)
                vector.tensor_tensor(tmp2, tmp, PS[3][:, 0:TT], AL.mult)
                vector.tensor_scalar(tmp, tmp2, SCs("i1"), MAGIC,
                                     AL.mult, AL.add)
                vector.tensor_scalar(tmp2, tmp, MAGIC, 32767.0,
                                     AL.subtract, AL.min)
                vector.tensor_copy(xqh_ct(c, t), tmp2)
                vector.tensor_tensor(xql_ct(c, t), tmp2, xqh_ct(c, t),
                                     AL.subtract)
                S.cinc(vector, "VEC")

            if active():
                emit_ln(x_src, p0b_consume, 1)
                my_barrier()

            def xq_c(c):
                return AR_(c * 4096, (c + 1) * 4096)

            if active():
                WST = [POOL[:, 8192:10240], POOL[:, 10240:12288]]
                b_ten, b_sca, b_vec = S.cnt["TEN"], S.cnt["SCA"], S.cnt["VEC"]
                cl, cs_ = GL.plan_counters(), GS.plan_counters()
                NJ = IC * NT
                ldu_at = [None] * IC
                ldg_at = [None] * IC
                for i in range(IC):
                    ldu_at[i] = (DmaGroup.plan(cl, i % 2, GL.K),
                                 DmaGroup.plan(cl, 2 + i % 2, GL.K))
                    ldg_at[i] = (DmaGroup.plan(cl, 4 + i % 2, GL.K),
                                 DmaGroup.plan(cl, 6 + i % 2, GL.K))
                mmu_at = [b_ten + 2 * j + 1 for j in range(NJ)]
                mmg_at = [b_ten + 2 * j + 2 for j in range(NJ)]
                evu_at = [b_sca + 2 * j + 1 for j in range(NJ)]
                evg_at = [b_sca + 2 * j + 2 for j in range(NJ)]
                mxu_at = [b_vec + 2 * j + 1 for j in range(NJ)]
                mxg_at = [b_vec + 2 * j + 2 for j in range(NJ)]
                stu_at = [None] * NJ
                stg_at = [None] * NJ
                cs2 = cs_
                for j in range(NJ):
                    stu_at[j] = DmaGroup.plan(cs2, j % 4, GS.K)
                    stg_at[j] = DmaGroup.plan(cs2, 4 + j % 4, GS.K)

                WB = POOL[:, 8192:12288].bitcast(BF16)

                def wsl(i, plane):
                    base = (i % 2) * 4096 + plane * 1024
                    return WB[:, base:base + 1024]

                @block.sync
                def _(sync):
                    for i in range(IC):
                        if i >= 2:
                            S.wait(sync, "TEN", mmg_at[(i - 1) * NT - 1])
                        for pl, (nm, tok) in enumerate(
                                (("wuTh", ldu_at[i][0]), ("wuTl", ldu_at[i][1]),
                                 ("wgTh", ldg_at[i][0]), ("wgTl", ldg_at[i][1]))):
                            k = ld3(sync, wsl(i, pl),
                                    ap[nm][:, i * 128:(i + 1) * 128])
                            GL.issue(k, 2 * pl + i % 2, tok)

                @block.tensor
                def _(tensor):
                    for i in range(IC):
                        for tok in ldu_at[i] + ldg_at[i]:
                            GL.wait(tensor, tok)
                        for t in range(NT):
                            j = i * NT + t
                            if j >= 2:
                                S.wait(tensor, "SCA", evu_at[j - 2])
                            for c in range(DC):
                                wh = wsl(i, 0)[:, c * 128:(c + 1) * 128]
                                wl = wsl(i, 1)[:, c * 128:(c + 1) * 128]
                                tensor.matmul(PS[j % 2][:, 0:TT], wh,
                                              xqh_ct(c, t),
                                              start=(c == 0), stop=False)
                                tensor.matmul(PS[j % 2][:, 0:TT], wh,
                                              xql_ct(c, t),
                                              start=False, stop=False)
                                k = tensor.matmul(PS[j % 2][:, 0:TT], wl,
                                                  xqh_ct(c, t), start=False,
                                                  stop=(c == DC - 1))
                            assert S.inc(k, "TEN") == mmu_at[j]
                            if j >= 2:
                                S.wait(tensor, "SCA", evg_at[j - 2])
                            for c in range(DC):
                                wh = wsl(i, 2)[:, c * 128:(c + 1) * 128]
                                wl = wsl(i, 3)[:, c * 128:(c + 1) * 128]
                                tensor.matmul(PS[2 + j % 2][:, 0:TT], wh,
                                              xqh_ct(c, t),
                                              start=(c == 0), stop=False)
                                tensor.matmul(PS[2 + j % 2][:, 0:TT], wh,
                                              xql_ct(c, t),
                                              start=False, stop=False)
                                k = tensor.matmul(PS[2 + j % 2][:, 0:TT], wl,
                                                  xqh_ct(c, t), start=False,
                                                  stop=(c == DC - 1))
                            assert S.inc(k, "TEN") == mmg_at[j]

                @bsca
                def _(scalar):
                    for j in range(NJ):
                        S.wait(scalar, "TEN", mmu_at[j])
                        if j >= 4:
                            S.wait(scalar, "VEC", mxu_at[j - 4])
                            GS.wait(scalar, stu_at[j - 4])
                        scalar.activation(TSL(j % 4), PS[j % 2][:, 0:TT],
                                          AF.Copy, scale=SCs("s1"))
                        assert S.cinc(scalar, "SCA") == evu_at[j]
                        S.wait(scalar, "TEN", mmg_at[j])
                        if j >= 4:
                            S.wait(scalar, "VEC", mxg_at[j - 4])
                            GS.wait(scalar, stg_at[j - 4])
                        scalar.activation(TSL(4 + j % 4), PS[2 + j % 2][:, 0:TT],
                                          AF.Copy, scale=SCs("s1"))
                        assert S.cinc(scalar, "SCA") == evg_at[j]

                @bvec
                def _(vector):
                    for j in range(NJ):
                        S.wait(vector, "SCA", evu_at[j])
                        vector.tensor_reduce(M0[:, j:j + 1], TSL(j % 4),
                                                 AX, AL.max,
                                                 apply_absolute_value=True)
                        assert S.cinc(vector, "VEC") == mxu_at[j]
                        S.wait(vector, "SCA", evg_at[j])
                        vector.tensor_reduce(M1[:, j:j + 1], TSL(4 + j % 4),
                                                 AX, AL.max,
                                                 apply_absolute_value=True)
                        assert S.cinc(vector, "VEC") == mxg_at[j]

                @block.gpsimd
                def _(gp):
                    for j in range(NJ):
                        i, t = j // NT, j % NT
                        S.wait(gp, "SCA", evu_at[j])
                        k = gp.dma_start(
                            out=ap["up_raw"][i * 128:(i + 1) * 128,
                                             t * TT:(t + 1) * TT],
                            in_=TSL(j % 4)[:])
                        GS.issue(k, j % 4, stu_at[j])
                        S.wait(gp, "SCA", evg_at[j])
                        k = gp.dma_start(
                            out=ap["gate_raw"][i * 128:(i + 1) * 128,
                                               t * TT:(t + 1) * TT],
                            in_=TSL(4 + j % 4)[:])
                        GS.issue(k, 4 + j % 4, stg_at[j])
                    GS.wait_all(gp)

                my_barrier()
                emit_allreduce(ccb + 1, M0, NJ, "s2", "i2")
                emit_allreduce(ccb + 2, M1, NJ, "s3", "i3")

            if active():
                NJ = IC * NT
                b_vec, b_sca = S.cnt["VEC"], S.cnt["SCA"]
                cl, cs_ = GL.plan_counters(), GS.plan_counters()
                cc2 = GC.plan_counters()
                ldu_at = [None] * NJ
                ldg_at = [None] * NJ
                stu_at = [None] * NJ
                stg_at = [None] * NJ
                sts_at = [None] * NJ
                for j in range(NJ):
                    ldu_at[j] = DmaGroup.plan(cl, (j % 3) * 2, GL.K)
                    ldg_at[j] = DmaGroup.plan(cl, (j % 3) * 2 + 1, GL.K)
                    stu_at[j] = DmaGroup.plan(cs_, j % 3, GS.K)
                    stg_at[j] = DmaGroup.plan(cs_, 3 + j % 3, GS.K)
                    sts_at[j] = DmaGroup.plan(cc2, j % 3, GC.K)
                sig_at, w1s_at = {}, {}
                sc2 = b_sca
                for j in range(NJ):
                    sc2 += 1
                    sig_at[j] = sc2
                    sc2 += 1
                    w1s_at[j] = sc2
                uq_at, gq_at, sq_at, tl_at = {}, {}, {}, {}
                vc2 = b_vec + 1
                for j in range(NJ + 1):
                    if j < NJ:
                        vc2 += 1
                        uq_at[j] = vc2
                        vc2 += 1
                        gq_at[j] = vc2
                    if j >= 1:
                        vc2 += 1
                        sq_at[j - 1] = vc2
                        vc2 += 1
                        tl_at[j - 1] = vc2

                def sl(j, k):
                    return TSL((j % 3) * 8 + k)

                @block.sync
                def _(sync):
                    for j in range(NJ):
                        i, t = j // NT, j % NT
                        if j >= 3:
                            S.wait(sync, "VEC", uq_at[j - 3])
                        k = sync.dma_start(
                            out=sl(j, 0)[:],
                            in_=ap["up_raw"][i * 128:(i + 1) * 128,
                                             t * TT:(t + 1) * TT])
                        GL.issue(k, (j % 3) * 2, ldu_at[j])
                        if j >= 3:
                            S.wait(sync, "VEC", gq_at[j - 3])
                        k = sync.dma_start(
                            out=sl(j, 1)[:],
                            in_=ap["gate_raw"][i * 128:(i + 1) * 128,
                                               t * TT:(t + 1) * TT])
                        GL.issue(k, (j % 3) * 2 + 1, ldg_at[j])

                @bsca
                def _(scalar):
                    for j in range(NJ):
                        S.wait(scalar, "VEC", gq_at[j])
                        scalar.activation(sl(j, 6), sl(j, 4), AF.Sigmoid,
                                          scale=SCs("s3"))
                        assert S.cinc(scalar, "SCA") == sig_at[j]
                        scalar.activation(sl(j, 2), sl(j, 6), AF.Copy,
                                          scale=65536.0, bias=MAGIC)
                        assert S.cinc(scalar, "SCA") == w1s_at[j]

                @bvec
                def _(vector):
                    vector.tensor_scalar(SCs("s3c"), SCs("s3"),
                                             float(2.0 ** -16), None, AL.mult)
                    S.cinc(vector, "VEC")
                    for j in range(NJ + 1):
                        if j < NJ:
                            GL.wait(vector, ldu_at[j])
                            if j >= 3:
                                GS.wait(vector, stu_at[j - 3])
                            vector.tensor_scalar(sl(j, 2), sl(j, 0),
                                                 SCs("i2"), MAGIC,
                                                 AL.mult, AL.add)
                            vector.tensor_scalar(sl(j, 3), sl(j, 2), MAGIC,
                                                     32767.0,
                                                     AL.subtract, AL.min)
                            assert S.cinc(vector, "VEC") == uq_at[j]
                            GL.wait(vector, ldg_at[j])
                            if j >= 3:
                                GS.wait(vector, stg_at[j - 3])
                            vector.tensor_scalar(sl(j, 2), sl(j, 1),
                                                 SCs("i3"), MAGIC,
                                                 AL.mult, AL.add)
                            vector.tensor_scalar(sl(j, 4), sl(j, 2), MAGIC,
                                                     32767.0,
                                                     AL.subtract, AL.min)
                            assert S.cinc(vector, "VEC") == gq_at[j]
                        if j >= 1:
                            jj = j - 1
                            S.wait(vector, "SCA", w1s_at[jj])
                            if jj >= 3:
                                GC.wait(vector, sts_at[jj - 3])
                            vector.tensor_scalar(sl(jj, 7), sl(jj, 2), MAGIC,
                                                 65535.0,
                                                 AL.subtract, AL.min)
                            assert S.cinc(vector, "VEC") == sq_at[jj]
                            vector.scalar_tensor_tensor(sl(jj, 5), sl(jj, 7),
                                                            SCs("s3c"),
                                                            sl(jj, 4),
                                                            AL.mult, AL.mult)
                            vector.tensor_reduce(M0[:, jj:jj + 1], sl(jj, 5),
                                                     AX, AL.max,
                                                     apply_absolute_value=True)
                            assert S.cinc(vector, "VEC") == tl_at[jj]

                @block.gpsimd
                def _(gp):
                    for j in range(NJ):
                        i, t = j // NT, j % NT
                        dst = (slice(i * 128, (i + 1) * 128),
                               slice(t * TT, (t + 1) * TT))
                        S.wait(gp, "VEC", uq_at[j])
                        k = gp.dma_start(out=ap["uq16"][dst[0], dst[1]],
                                         in_=sl(j, 3)[:])
                        GS.issue(k, j % 3, stu_at[j])
                        S.wait(gp, "VEC", gq_at[j])
                        k = gp.dma_start(out=ap["gq16"][dst[0], dst[1]],
                                         in_=sl(j, 4)[:])
                        GS.issue(k, 3 + j % 3, stg_at[j])
                        S.wait(gp, "VEC", sq_at[j])
                        k = gp.dma_start(out=ap["sq16"][dst[0], dst[1]],
                                         in_=sl(j, 7)[:])
                        GC.issue(k, j % 3, sts_at[j])
                    GS.wait_all(gp)
                    GC.wait_all(gp)

                my_barrier()
                emit_allreduce(ccb + 3, M0, NJ, "s4", "i4")

            if active():
                NJ = IC * NT
                b_vec, b_sca = S.cnt["VEC"], S.cnt["SCA"]
                cc_, cl, cs_ = (GC.plan_counters(), GL.plan_counters(),
                                GS.plan_counters())
                ldu_at = [None] * NJ
                ldg_at = [None] * NJ
                lds_at = [None] * NJ
                st_at = [None] * NJ
                for j in range(NJ):
                    ldu_at[j] = DmaGroup.plan(cc_, (j % 3) * 2, GC.K)
                    ldg_at[j] = DmaGroup.plan(cc_, (j % 3) * 2 + 1, GC.K)
                    lds_at[j] = DmaGroup.plan(cs_, 3 + j % 3, GS.K)
                    st_at[j] = DmaGroup.plan(cs_, j % 3, GS.K)
                sc_at = [b_sca + j + 1 for j in range(NJ)]
                st4_at, g2_at, tl_at = {}, {}, {}
                vc2 = b_vec + 1
                for j in range(NJ + 1):
                    if j < NJ:
                        vc2 += 1
                        st4_at[j] = vc2
                    if j >= 1:
                        vc2 += 1
                        g2_at[j - 1] = vc2
                        vc2 += 1
                        tl_at[j - 1] = vc2

                def sl(j, k):
                    return TSL((j % 3) * 8 + k)

                @block.gpsimd
                def _(gp):
                    for j in range(NJ + 1):
                        if j < NJ:
                            i, t = j // NT, j % NT
                            src = (slice(i * 128, (i + 1) * 128),
                                   slice(t * TT, (t + 1) * TT))
                            if j >= 3:
                                S.wait(gp, "VEC", tl_at[j - 3])
                            k = gp.dma_start(out=sl(j, 0)[:],
                                             in_=ap["uq16"][src[0], src[1]])
                            GC.issue(k, (j % 3) * 2, ldu_at[j])
                            k = gp.dma_start(out=sl(j, 1)[:],
                                             in_=ap["gq16"][src[0], src[1]])
                            GC.issue(k, (j % 3) * 2 + 1, ldg_at[j])
                            k = gp.dma_start(out=sl(j, 2)[:],
                                             in_=ap["sq16"][src[0], src[1]])
                            GS.issue(k, 3 + j % 3, lds_at[j])
                        if j >= 1:
                            jj = j - 1
                            i, t = jj // NT, jj % NT
                            S.wait(gp, "VEC", g2_at[jj])
                            k = gp.dma_start(
                                out=ap["g2q16"][i * 128:(i + 1) * 128,
                                                t * TT:(t + 1) * TT],
                                in_=sl(jj, 4)[:])
                            GS.issue(k, jj % 3, st_at[jj])
                    GS.wait_all(gp)
                    GC.wait_all(gp)

                @bsca
                def _(scalar):
                    for j in range(NJ):
                        S.wait(scalar, "VEC", st4_at[j])
                        if j >= 3:
                            S.wait(scalar, "VEC", g2_at[j - 3])
                        scalar.activation(sl(j, 6), sl(j, 3), AF.Copy,
                                          scale=SCs("i4"), bias=MAGIC)
                        assert S.cinc(scalar, "SCA") == sc_at[j]

                @bvec
                def _(vector):
                    vector.tensor_tensor(SCs("s42"), SCs("s4"), SCs("s2"),
                                             AL.mult)
                    S.cinc(vector, "VEC")
                    for j in range(NJ + 1):
                        if j < NJ:
                            GC.wait(vector, ldg_at[j])
                            GS.wait(vector, lds_at[j])
                            if j >= 3:
                                S.wait(vector, "SCA", sc_at[j - 3])
                            vector.scalar_tensor_tensor(sl(j, 3), sl(j, 2),
                                                        SCs("s3c"), sl(j, 1),
                                                        AL.mult, AL.mult)
                            assert S.cinc(vector, "VEC") == st4_at[j]
                        if j >= 1:
                            jj = j - 1
                            S.wait(vector, "SCA", sc_at[jj])
                            if jj >= 3:
                                GS.wait(vector, st_at[jj - 3])
                            vector.tensor_scalar(sl(jj, 4), sl(jj, 6), MAGIC,
                                                     32767.0,
                                                     AL.subtract, AL.min)
                            assert S.cinc(vector, "VEC") == g2_at[jj]
                            GC.wait(vector, ldu_at[jj])
                            vector.scalar_tensor_tensor(sl(jj, 5), sl(jj, 4),
                                                            SCs("s42"),
                                                            sl(jj, 0),
                                                            AL.mult, AL.mult)
                            vector.tensor_reduce(M1[:, jj:jj + 1], sl(jj, 5),
                                                     AX, AL.max,
                                                     apply_absolute_value=True)
                            assert S.cinc(vector, "VEC") == tl_at[jj]

                my_barrier()
                emit_allreduce(ccb + 4, M1, NJ, "s5", "i5")

            if active():
                OBH = [AR_(0, 12288).bitcast(BF16),
                       AR_(12288, 24576).bitcast(BF16)]
                WDB = AR_(24576, 30720).bitcast(BF16)

                def obh(tb, ic):
                    return OBH[tb][:, ic * TT:(ic + 1) * TT]

                def obl(tb, ic):
                    return OBH[tb][:, 12288 + ic * TT:12288 + (ic + 1) * TT]

                def wdsl(j, plane):
                    base = (j % 2) * 6144 + plane * 3072
                    return WDB[:, base:base + 3072]
                b_vec, b_ten = S.cnt["VEC"], S.cnt["TEN"]
                cc_, cl, cs_ = (GC.plan_counters(), GL.plan_counters(),
                                GS.plan_counters())
                oldu_at = [[None] * IC for _ in range(NT)]
                oldg_at = [[None] * IC for _ in range(NT)]
                wld_at = [[None] * DC for _ in range(NT)]
                xo_at = [[None] * DC for _ in range(NT)]
                st_at = [[None] * DC for _ in range(NT)]
                for t in range(NT):
                    for ic in range(IC):
                        oldu_at[t][ic] = DmaGroup.plan(cc_, (ic % 2) * 2, GC.K)
                        oldg_at[t][ic] = DmaGroup.plan(cc_, (ic % 2) * 2 + 1, GC.K)
                    for h in range(DC):
                        wld_at[t][h] = (
                            DmaGroup.plan(cl, 2 + (t * DC + h) % 2, GL.K),
                            DmaGroup.plan(cl, 4 + (t * DC + h) % 2, GL.K))
                        xo_at[t][h] = DmaGroup.plan(cl, 6 + h % 2, GL.K)
                        st_at[t][h] = DmaGroup.plan(cs_, h % 2, GS.K)
                mm_at = [[b_ten + t * DC + h + 1 for h in range(DC)]
                         for t in range(NT)]
                ov_at = [[0] * IC for _ in range(NT)]
                xn_at = [[0] * DC for _ in range(NT)]
                vp = b_vec + 1
                for t in range(NT + 1):
                    if t < NT:
                        for ic in range(IC):
                            vp += 1
                            ov_at[t][ic] = vp
                    if t >= 1:
                        for h in range(DC):
                            vp += 1
                            xn_at[t - 1][h] = vp

                def slo(ic, k):
                    return TSL(16 + (ic % 2) * 2 + k)

                @block.gpsimd
                def _(gp):
                    for t in range(NT + 1):
                        if t < NT:
                            if t >= 2:
                                S.wait(gp, "TEN", mm_at[t - 2][DC - 1])
                            for ic in range(IC):
                                if t * IC + ic >= 2:
                                    pj = t * IC + ic - 2
                                    S.wait(gp, "VEC", ov_at[pj // IC][pj % IC])
                                src = (slice(ic * 128, (ic + 1) * 128),
                                       slice(t * TT, (t + 1) * TT))
                                k = gp.dma_start(out=slo(ic, 0)[:],
                                                 in_=ap["uq16"][src[0], src[1]])
                                GC.issue(k, (ic % 2) * 2, oldu_at[t][ic])
                                k = gp.dma_start(out=slo(ic, 1)[:],
                                                 in_=ap["g2q16"][src[0], src[1]])
                                GC.issue(k, (ic % 2) * 2 + 1, oldg_at[t][ic])
                        if t >= 1:
                            for h in range(DC):
                                S.wait(gp, "VEC", xn_at[t - 1][h])
                                k = gp.dma_start(
                                    out=ap[x_dst][h * 128:(h + 1) * 128,
                                                  (t - 1) * TT:t * TT],
                                    in_=TSL(14 + h % 2)[:])
                                GS.issue(k, h % 2, st_at[t - 1][h])
                    GS.wait_all(gp)

                @bvec
                def _(vector):
                    vector.tensor_scalar(SCs("h5"), SCs("s5"), 0.5, None,
                                         AL.mult)
                    S.cinc(vector, "VEC")
                    for t in range(NT + 1):
                        if t < NT:
                            for ic in range(IC):
                                GC.wait(vector, oldu_at[t][ic])
                                GC.wait(vector, oldg_at[t][ic])
                                s0, s1 = slo(ic, 0), slo(ic, 1)
                                tmp = TSL(22)
                                tmp2 = TSL(23)
                                vector.scalar_tensor_tensor(
                                    tmp, s1, SCs("s42"), s0, AL.mult, AL.mult)
                                vector.tensor_scalar(tmp, tmp, SCs("i5"), MAGIC,
                                                     AL.mult, AL.add)
                                vector.tensor_scalar(tmp2, tmp, MAGIC, 32767.0,
                                                     AL.subtract, AL.min)
                                vector.tensor_copy(obh(t % 2, ic), tmp2)
                                vector.tensor_tensor(obl(t % 2, ic), tmp2,
                                                     obh(t % 2, ic),
                                                     AL.subtract)
                                assert S.cinc(vector, "VEC") == ov_at[t][ic]
                        if t >= 1:
                            for h in range(DC):
                                j = (t - 1) * DC + h
                                S.wait(vector, "TEN", mm_at[t - 1][h])
                                GL.wait(vector, xo_at[t - 1][h])
                                if j >= 2:
                                    GS.wait(vector,
                                            st_at[(j - 2) // DC][(j - 2) % DC])
                                vector.scalar_tensor_tensor(
                                    TSL(14 + h % 2), PS[4 + j % 2][:, 0:TT],
                                    SCs("h5"), TSL(12 + h % 2), AL.mult, AL.add)
                                assert S.cinc(vector, "VEC") == xn_at[t - 1][h]

                @block.sync
                def _(sync):
                    for t in range(NT):
                        for h in range(DC):
                            j = t * DC + h
                            if j >= 2:
                                S.wait(sync, "TEN",
                                       mm_at[(j - 2) // DC][(j - 2) % DC])
                            k = ld3(sync, wdsl(j, 0),
                                    ap["wdTh"][:, h * 128:(h + 1) * 128])
                            GL.issue(k, 2 + j % 2, wld_at[t][h][0])
                            k = ld3(sync, wdsl(j, 1),
                                    ap["wdTl"][:, h * 128:(h + 1) * 128])
                            GL.issue(k, 4 + j % 2, wld_at[t][h][1])
                            if j >= 2:
                                S.wait(sync, "VEC",
                                       xn_at[(j - 2) // DC][(j - 2) % DC])
                            k = sync.dma_start(
                                out=TSL(12 + h % 2)[:],
                                in_=ap[x_src][h * 128:(h + 1) * 128,
                                              t * TT:(t + 1) * TT])
                            GL.issue(k, 6 + h % 2, xo_at[t][h])

                @block.tensor
                def _(tensor):
                    for t in range(NT):
                        S.wait(tensor, "VEC", ov_at[t][IC - 1])
                        for h in range(DC):
                            j = t * DC + h
                            GL.wait(tensor, wld_at[t][h][0])
                            GL.wait(tensor, wld_at[t][h][1])
                            if j >= 2:
                                S.wait(tensor, "VEC",
                                       xn_at[(j - 2) // DC][(j - 2) % DC])
                            for ic in range(IC):
                                wh = wdsl(j, 0)[:, ic * 128:(ic + 1) * 128]
                                wl = wdsl(j, 1)[:, ic * 128:(ic + 1) * 128]
                                tensor.matmul(PS[4 + j % 2][:, 0:TT], wh,
                                              obh(t % 2, ic),
                                              start=(ic == 0), stop=False)
                                tensor.matmul(PS[4 + j % 2][:, 0:TT], wh,
                                              obl(t % 2, ic),
                                              start=False, stop=False)
                                k = tensor.matmul(PS[4 + j % 2][:, 0:TT], wl,
                                                  obh(t % 2, ic), start=False,
                                                  stop=(ic == IC - 1))
                            assert S.inc(k, "TEN") == mm_at[t][h]

                my_barrier()

        def emit_a2_tile(t, x_src):
            XT0 = POOL[:, 0:4096]
            SQB = POOL[:, 8192:12288]
            LH = ARENA[:, 8192:10240].bitcast(BF16)
            LL = ARENA[:, 10240:12288].bitcast(BF16)
            QB = AR_(12288, 16384)
            KB = AR_(16384, 20480)
            WVB = AR_(20480, 24576).bitcast(BF16)
            WQK = POOL[:, 4096:8192].bitcast(BF16)
            T2B = XT0
            T1B = SQB
            VO = [TSL(0), TSL(1)]

            def lh_c(c):
                return LH[:, c * TT:(c + 1) * TT]

            def ll_c(c):
                return LL[:, c * TT:(c + 1) * TT]

            b = {n: S.cnt[n] for n in ("TEN", "SCA", "VEC")}
            cl = GL.plan_counters()
            ld_x = DmaGroup.plan(cl, t % 2, GL.K)
            sq_d = b["SCA"] + 1
            sd_d = b["SCA"] + 2
            st_d = b["TEN"] + 1
            bc_d = b["TEN"] + 2
            row_d = b["VEC"] + 1
            rr_d = b["VEC"] + 2
            ln_d = b["VEC"] + 2 + DC

            @block.sync
            def _(sync):
                i = ld3(sync, XT0[:], ap[x_src][:, t * TT:(t + 1) * TT])
                GL.issue(i, t % 2, ld_x)

            @bsca
            def _(scalar):
                GL.wait(scalar, ld_x)
                scalar.activation(SQB[:], XT0[:], AF.Square)
                assert S.cinc(scalar, "SCA") == sq_d
                S.wait(scalar, "VEC", row_d)
                scalar.activation(sd_row, var_row, AF.Sqrt,
                                  bias=EPS[0:1, 0:1], scale=1.0)
                assert S.cinc(scalar, "SCA") == sd_d

            @block.tensor
            def _(tensor):
                GL.wait(tensor, ld_x)
                S.wait(tensor, "SCA", sq_d)
                for c in range(DC):
                    tensor.matmul(PS[0][0:1, 0:TT], CST["ones128"][:],
                                  XT0[:, c * TT:(c + 1) * TT],
                                  start=(c == 0), stop=(c == DC - 1))
                for c in range(DC):
                    i = tensor.matmul(PS[1][0:1, 0:TT], CST["ones128"][:],
                                      SQB[:, c * TT:(c + 1) * TT],
                                      start=(c == 0), stop=(c == DC - 1))
                assert S.inc(i, "TEN") == st_d
                S.wait(tensor, "VEC", rr_d)
                tensor.matmul(PS[2][:, 0:TT], CST["ones1x"][:],
                              murow, start=True, stop=True)
                i = tensor.matmul(PS[3][:, 0:TT], CST["ones1x"][:],
                                  rrow, start=True, stop=True)
                assert S.inc(i, "TEN") == bc_d

            @bvec
            def _(vector):
                S.wait(vector, "TEN", st_d)
                vector.tensor_scalar(murow, PS[0][0:1, 0:TT],
                                     1.0 / float(D), None, AL.mult)
                vector.tensor_scalar(var_row, PS[1][0:1, 0:TT],
                                     1.0 / float(D), None, AL.mult)
                vector.tensor_tensor(sd_row, murow, murow, AL.mult)
                vector.tensor_tensor(var_row, var_row, sd_row, AL.subtract)
                assert S.cinc(vector, "VEC") == row_d
                S.wait(vector, "SCA", sd_d)
                vector.reciprocal(rrow, sd_row)
                assert S.cinc(vector, "VEC") == rr_d
                S.wait(vector, "TEN", bc_d)
                tmp = POOL[:, 8192:8704]
                tmp2 = POOL[:, 8704:9216]
                for c in range(DC):
                    vector.tensor_tensor(tmp, XT0[:, c * TT:(c + 1) * TT],
                                         PS[2][:, 0:TT], AL.subtract)
                    vector.tensor_tensor(tmp2, tmp, PS[3][:, 0:TT], AL.mult)
                    vector.tensor_copy(lh_c(c), tmp2)
                    vector.tensor_tensor(ll_c(c), tmp2, lh_c(c), AL.subtract)
                    S.cinc(vector, "VEC")
                assert S.cnt["VEC"] == ln_d

            b2 = {n: S.cnt[n] for n in ("TEN", "SCA", "VEC")}
            cl2, cs2 = GL.plan_counters(), GS.plan_counters()
            NU = 16
            wld = [None] * NU
            for u in range(NU):
                wld[u] = (DmaGroup.plan(cl2, 4 + (u % 2) * 2, GL.K),
                          DmaGroup.plan(cl2, 5 + (u % 2) * 2, GL.K))
            pmm_at = [b2["TEN"] + u + 1 for u in range(NU)]
            pev_at = [b2["SCA"] + u + 1 for u in range(NU)]

            def wqk_h(u):
                return WQK[:, (u % 2) * 4096:(u % 2) * 4096 + 1024]

            def wqk_l(u):
                return WQK[:, (u % 2) * 4096 + 1024:(u % 2) * 4096 + 2048]

            @block.sync
            def _(sync):
                for u in range(NU):
                    s_, o = u // 8, u % 8
                    nm = ("wqT", "wkT")[s_]
                    if u >= 2:
                        S.wait(sync, "TEN", pmm_at[u - 2])
                    i = ld3(sync, wqk_h(u),
                            ap[nm + "h"][:, o * 128:(o + 1) * 128])
                    GL.issue(i, 4 + (u % 2) * 2, wld[u][0])
                    i = ld3(sync, wqk_l(u),
                            ap[nm + "l"][:, o * 128:(o + 1) * 128])
                    GL.issue(i, 5 + (u % 2) * 2, wld[u][1])

            @block.tensor
            def _(tensor):
                S.wait(tensor, "VEC", ln_d)
                for u in range(NU):
                    s_, o = u // 8, u % 8
                    GL.wait(tensor, wld[u][0])
                    GL.wait(tensor, wld[u][1])
                    if u >= 2:
                        S.wait(tensor, "SCA", pev_at[u - 2])
                    for c in range(DC):
                        wh = wqk_h(u)[:, c * 128:(c + 1) * 128]
                        wl = wqk_l(u)[:, c * 128:(c + 1) * 128]
                        tensor.matmul(PS[u % 2][:, 0:TT], wh, lh_c(c),
                                      start=(c == 0), stop=False)
                        tensor.matmul(PS[u % 2][:, 0:TT], wh, ll_c(c),
                                      start=False, stop=False)
                        i = tensor.matmul(PS[u % 2][:, 0:TT], wl, lh_c(c),
                                          start=False, stop=(c == DC - 1))
                    assert S.inc(i, "TEN") == pmm_at[u]

            @bsca
            def _(scalar):
                for u in range(NU):
                    s_, o = u // 8, u % 8
                    S.wait(scalar, "TEN", pmm_at[u])
                    buf = QB if s_ == 0 else KB
                    scalar.copy(buf[:, o * TT:(o + 1) * TT], PS[u % 2][:, 0:TT])
                    assert S.cinc(scalar, "SCA") == pev_at[u]

            b3 = {n: S.cnt[n] for n in ("TEN", "SCA")}
            cl3, cs3 = GL.plan_counters(), GS.plan_counters()
            wvld = [(DmaGroup.plan(cl3, 2, GL.K), DmaGroup.plan(cl3, 3, GL.K))
                    for _ in range(2)]
            vst = [DmaGroup.plan(cs3, 2 + u % 2, GS.K) for u in range(8)]
            vmm = [b3["TEN"] + u + 1 for u in range(8)]
            vev = [b3["SCA"] + u + 1 for u in range(8)]

            @block.sync
            def _(sync):
                for o2 in range(2):
                    if o2 >= 1:
                        S.wait(sync, "TEN", vmm[3])
                    i = ld3(sync, WVB[:, 0:4096],
                            ap["wvTh"][:, o2 * TT:(o2 + 1) * TT])
                    GL.issue(i, 2, wvld[o2][0])
                    i = ld3(sync, WVB[:, 4096:8192],
                            ap["wvTl"][:, o2 * TT:(o2 + 1) * TT])
                    GL.issue(i, 3, wvld[o2][1])

            @block.tensor
            def _(tensor):
                for u in range(8):
                    o2, ts4 = u // 4, u % 4
                    GL.wait(tensor, wvld[o2][0])
                    GL.wait(tensor, wvld[o2][1])
                    if u < 2:
                        S.wait(tensor, "SCA", pev_at[NU - 2 + u])
                    else:
                        S.wait(tensor, "SCA", vev[u - 2])
                    for c in range(DC):
                        xh = LH[:, c * TT + ts4 * 128:c * TT + (ts4 + 1) * 128]
                        xl = LL[:, c * TT + ts4 * 128:c * TT + (ts4 + 1) * 128]
                        whm = WVB[:, c * 512:(c + 1) * 512]
                        wlm = WVB[:, 4096 + c * 512:4096 + (c + 1) * 512]
                        tensor.matmul(PS[u % 2][:, 0:TT], xh, whm,
                                      start=(c == 0), stop=False)
                        tensor.matmul(PS[u % 2][:, 0:TT], xl, whm,
                                      start=False, stop=False)
                        i = tensor.matmul(PS[u % 2][:, 0:TT], xh, wlm,
                                          start=False, stop=(c == DC - 1))
                    assert S.inc(i, "TEN") == vmm[u]

            @bsca
            def _(scalar):
                for u in range(8):
                    S.wait(scalar, "TEN", vmm[u])
                    if u >= 2:
                        GS.wait(scalar, vst[u - 2])
                    scalar.copy(VO[u % 2][:], PS[u % 2][:, 0:TT])
                    assert S.cinc(scalar, "SCA") == vev[u]

            @block.gpsimd
            def _(gp):
                for u in range(8):
                    o2, ts4 = u // 4, u % 4
                    S.wait(gp, "SCA", vev[u])
                    r0 = t * TT + ts4 * 128
                    i = gp.dma_start(
                        out=ap["vtm"][r0:r0 + 128, o2 * TT:(o2 + 1) * TT],
                        in_=VO[u % 2][:])
                    GS.issue(i, 2 + u % 2, vst[u])
                GS.wait_all(gp)

            rsl = slice(t * TT, (t + 1) * TT)
            prev_vqr = [None]
            for s_, buf in ((0, QB), (1, KB)):
                b4 = {n: S.cnt[n] for n in ("TEN", "SCA", "VEC")}
                cs4 = GS.plan_counters()
                sq2_d = b4["SCA"] + 1
                sd4 = [b4["SCA"] + 1 + c + 1 for c in range(DC)]
                rmm, rbb, pmm2 = {}, {}, {}
                tc = b4["TEN"]
                for st_ in range(DC + 2):
                    if st_ < DC:
                        tc += 1
                        rmm[st_] = tc
                    if 1 <= st_ <= DC:
                        tc += 1
                        rbb[st_ - 1] = tc
                    if st_ >= 2:
                        tc += 1
                        pmm2[st_ - 2] = tc
                vme, vrc, vqn, vt2 = {}, {}, {}, {}
                vc = b4["VEC"]
                for st_ in range(DC + 2):
                    if st_ < DC:
                        vc += 1
                        vme[st_] = vc
                        vc += 1
                        vrc[st_] = vc
                    if 1 <= st_ <= DC:
                        vc += 1
                        vqn[st_ - 1] = vc
                    if st_ >= 2:
                        vc += 1
                        vt2[st_ - 2] = vc
                vt1 = vc + 1
                vqr = vc + 2
                stq = DmaGroup.plan(cs4, s_, GS.K)

                def rslot(c):
                    return RMSW[0:2, (c % 4) * 512:(c % 4 + 1) * 512]

                @bsca
                def _(scalar, s_=s_, buf=buf, sq2_d=sq2_d, sd4=sd4,
                      vme=vme, pq=prev_vqr[0], rslot=rslot):
                    if s_ == 0:
                        GS.wait(scalar, vst[6])
                        GS.wait(scalar, vst[7])
                    else:
                        S.wait(scalar, "VEC", pq)
                    scalar.activation(T2B[:], buf[:], AF.Square)
                    assert S.cinc(scalar, "SCA") == sq2_d
                    for c in range(DC):
                        S.wait(scalar, "VEC", vme[c])
                        scalar.activation(rslot(c), rslot(c), AF.Sqrt,
                                          bias=EPS[0:2, 1:2], scale=1.0)
                        assert S.cinc(scalar, "SCA") == sd4[c]

                @block.tensor
                def _(tensor, s_=s_, buf=buf, sq2_d=sq2_d, sd4=sd4, rmm=rmm,
                      rbb=rbb, pmm2=pmm2, vme=vme, vrc=vrc, vqn=vqn, vt2=vt2,
                      rslot=rslot):
                    for st_ in range(DC + 2):
                        if st_ < DC:
                            c = st_
                            if c == 0:
                                S.wait(tensor, "SCA", sq2_d)
                            if c >= 4:
                                S.wait(tensor, "VEC", vt2[c - 4])
                            i = tensor.matmul(PS[4 + c % 4][0:2, 0:TT],
                                              CST["rmsmask"][:],
                                              T2B[:, c * TT:(c + 1) * TT],
                                              start=True, stop=True)
                            assert S.inc(i, "TEN") == rmm[c]
                        if 1 <= st_ <= DC:
                            c = st_ - 1
                            S.wait(tensor, "VEC", vrc[c])
                            if c >= 4:
                                S.wait(tensor, "VEC", vqn[c - 4])
                            stat = (GBC[0:2, c * 128:(c + 1) * 128]
                                    if s_ == 0 else CST["rmsbc"][:])
                            i = tensor.matmul(PS[c % 4][:, 0:TT], stat,
                                              rslot(c), start=True, stop=True)
                            assert S.inc(i, "TEN") == rbb[c]
                        if st_ >= 2:
                            c = st_ - 2
                            S.wait(tensor, "VEC", vqn[c])
                            i = tensor.matmul(PS[4 + c % 4][:, 0:TT],
                                              CST["perm32"][:],
                                              buf[:, c * TT:(c + 1) * TT],
                                              start=True, stop=True)
                            assert S.inc(i, "TEN") == pmm2[c]

                @bvec
                def _(vector, s_=s_, buf=buf, rmm=rmm, rbb=rbb, pmm2=pmm2,
                      vme=vme, vrc=vrc, vqn=vqn, vt2=vt2, vt1=vt1, vqr=vqr,
                      sd4=sd4, rslot=rslot):
                    for st_ in range(DC + 2):
                        if st_ < DC:
                            c = st_
                            S.wait(vector, "TEN", rmm[c])
                            vector.tensor_scalar(rslot(c),
                                                 PS[4 + c % 4][0:2, 0:TT],
                                                 1.0 / float(HD), None,
                                                 AL.mult)
                            assert S.cinc(vector, "VEC") == vme[c]
                            S.wait(vector, "SCA", sd4[c])
                            vector.reciprocal(rslot(c), rslot(c))
                            assert S.cinc(vector, "VEC") == vrc[c]
                        if 1 <= st_ <= DC:
                            c = st_ - 1
                            S.wait(vector, "TEN", rbb[c])
                            vector.tensor_tensor(buf[:, c * TT:(c + 1) * TT],
                                                 buf[:, c * TT:(c + 1) * TT],
                                                 PS[c % 4][:, 0:TT], AL.mult)
                            assert S.cinc(vector, "VEC") == vqn[c]
                        if st_ >= 2:
                            c = st_ - 2
                            S.wait(vector, "TEN", pmm2[c])
                            vector.tensor_tensor(T2B[:, c * TT:(c + 1) * TT],
                                                 PS[4 + c % 4][:, 0:TT],
                                                 ROPE_B[:, rsl], AL.mult)
                            assert S.cinc(vector, "VEC") == vt2[c]
                    rab = ROPE_A[:, rsl].rearrange(
                        "p (o k) -> p o k", o=1).broadcast_to([128, DC, TT])
                    vector.tensor_tensor(
                        T1B[:].rearrange("p (c k) -> p c k", c=DC),
                        buf[:].rearrange("p (c k) -> p c k", c=DC),
                        rab, AL.mult)
                    assert S.cinc(vector, "VEC") == vt1
                    vector.tensor_tensor(buf[:], T1B[:], T2B[:], AL.add)
                    assert S.cinc(vector, "VEC") == vqr

                prev_vqr[0] = vqr

                @block.gpsimd
                def _(gp, s_=s_, buf=buf, vqr=vqr, stq=stq):
                    S.wait(gp, "VEC", vqr)
                    i = gp.dma_start(
                        out=ap[("qrot", "krot")[s_]][:, t * TT:(t + 1) * TT]
                            .rearrange("(c p) k -> p c k", p=128),
                        in_=buf[:].rearrange("p (c k) -> p c k", c=DC))
                    GS.issue(i, s_, stq)
                    GS.wait(gp, stq)

            my_barrier()

        def emit_a3_wp(wp, x_src, x_dst):
            Q = AR_(8192, 12288)
            K_ = AR_(12288, 16384)
            V4 = AR_(16384, 20480)
            Y = AR_(20480, 24576)
            YH = V4[:, 0:2048].bitcast(BF16)
            YL = V4[:, 2048:4096].bitcast(BF16)
            WOB = POOL[:, 4096:6144].bitcast(BF16)
            XOL = [TSL(12), TSL(13)]
            SM3 = [TSL(16), TSL(17), TSL(0)]
            EN3 = [TSL(20), TSL(21), TSL(22)]
            ET3 = [[TSL(x)[:, ks * WIN:(ks + 1) * WIN] for ks in range(2)]
                   for x in (18, 19, 23)]

            cl, cs_ = GL.plan_counters(), GS.plan_counters()
            ldq = DmaGroup.plan(cl, 0, GL.K)
            ldk = DmaGroup.plan(cl, 1, GL.K)
            ldv = DmaGroup.plan(cl, 2, GL.K)
            wo_at = [(DmaGroup.plan(cl, 4 + h % 2, GL.K),
                      DmaGroup.plan(cl, 6 + h % 2, GL.K)) for h in range(DC)]
            cgc = GC.plan_counters()
            xo_at = [DmaGroup.plan(cgc, h % 2, GC.K) for h in range(DC)]
            st_at = [DmaGroup.plan(cs_, h % 2, GS.K) for h in range(DC)]

            b = {n: S.cnt[n] for n in ("TEN", "SCA", "VEC")}
            units = [(w, h) for w in range(2) for h in range(NH)]
            NU = len(units)
            sc_at, tp_at, pv_at = {}, {}, {}
            ms_at, nm_at, ex_at, et_at, ye_at = {}, {}, {}, {}, {}
            tc = b["TEN"]
            for u in range(3):
                for qs in range(2):
                    tc += 1
                    sc_at[units[u] + (qs,)] = tc
            for u in range(NU):
                if u + 3 < NU:
                    for qs in range(2):
                        tc += 1
                        sc_at[units[u + 3] + (qs,)] = tc
                for qs in range(2):
                    for ks in range(2):
                        tc += 1
                        tp_at[units[u] + (qs, ks)] = tc
                tc += 1
                pv_at[units[u]] = tc
            womm_at = [tc + h + 1 for h in range(DC)]
            vc2 = b["VEC"]
            for u in range(NU + 1):
                if u < NU:
                    for qs in range(2):
                        vc2 += 1
                        ms_at[units[u] + (qs,)] = vc2
                if u >= 1:
                    for qs in range(2):
                        vc2 += 1
                        nm_at[units[u - 1] + (qs,)] = vc2
            ysp = vc2 + 1
            vc2 += 1
            xn_at = [vc2 + h + 1 for h in range(DC)]
            sc2 = b["SCA"]
            for u in range(NU + 1):
                if u < NU:
                    for qs in range(2):
                        sc2 += 1
                        ex_at[units[u] + (qs,)] = sc2
                if u >= 1:
                    w_, h_ = units[u - 1]
                    sc2 += 1
                    et_at[(w_, h_)] = sc2
                    if h_ % 2 == 1:
                        sc2 += 1
                        ye_at[(w_, h_ // 2)] = sc2

            def scb(u):
                return PS[u % 3]

            def tpb(u):
                return PS[3 + u % 3]

            @block.sync
            def _(sync):
                i = ld3(sync, Q[:], ap["qrot"][:, wp * TT:(wp + 1) * TT])
                GL.issue(i, 0, ldq)
                i = ld3(sync, K_[:], ap["krot"][:, wp * TT:(wp + 1) * TT])
                GL.issue(i, 1, ldk)
                i = ld3(sync, V4[:], ap["vtm"][wp * TT:(wp + 1) * TT, :])
                GL.issue(i, 2, ldv)
                for h in range(DC):
                    if h >= 2:
                        S.wait(sync, "TEN", womm_at[h - 2])
                    i = ld3(sync, WOB[:, (h % 2) * 2048:(h % 2) * 2048 + 1024],
                            ap["woTh"][:, h * 128:(h + 1) * 128])
                    GL.issue(i, 4 + h % 2, wo_at[h][0])
                    i = ld3(sync, WOB[:, (h % 2) * 2048 + 1024:(h % 2) * 2048 + 2048],
                            ap["woTl"][:, h * 128:(h + 1) * 128])
                    GL.issue(i, 6 + h % 2, wo_at[h][1])

            def emit_sc(tensor, u):
                w, h = units[u]
                hc, off = h // 2, (h % 2) * 64
                kw = K_[off:off + 64,
                        hc * TT + w * WIN:hc * TT + (w + 1) * WIN]
                if u >= 3:
                    S.wait(tensor, "VEC", ms_at[units[u - 3] + (1,)])
                for qs in range(2):
                    q0 = hc * TT + w * WIN + qs * 128
                    i = tensor.matmul(scb(u)[:, qs * WIN:(qs + 1) * WIN],
                                      Q[off:off + 64, q0:q0 + 128],
                                      kw, start=True, stop=True)
                    assert S.inc(i, "TEN") == sc_at[(w, h, qs)]

            @block.tensor
            def _(tensor):
                GL.wait(tensor, ldq)
                GL.wait(tensor, ldk)
                GL.wait(tensor, ldv)
                for u in range(3):
                    emit_sc(tensor, u)
                for u, (w, h) in enumerate(units):
                    if u + 3 < NU:
                        emit_sc(tensor, u + 3)
                    if u >= 3:
                        S.wait(tensor, "SCA", et_at[units[u - 3]])
                    for qs in range(2):
                        S.wait(tensor, "VEC", nm_at[(w, h, qs)])
                        for ks in range(2):
                            i = tensor.transpose(
                                tpb(u)[:, ks * WIN + qs * 128:
                                       ks * WIN + (qs + 1) * 128],
                                EN3[u % 3][:, qs * WIN + ks * 128:
                                           qs * WIN + (ks + 1) * 128],
                                CST["ident"][:])
                            assert S.inc(i, "TEN") == tp_at[(w, h, qs, ks)]
                    S.wait(tensor, "SCA", et_at[(w, h)])
                    if h >= 2:
                        S.wait(tensor, "SCA", ye_at[(w, h // 2 - 1)])
                    off_y = (h % 2) * 64
                    for ks in range(2):
                        s4 = (w * 2 + ks) * 1024
                        i = tensor.matmul(
                            PS[6 + w][off_y:off_y + 64, 0:WIN],
                            V4[:, s4 + h * 64:s4 + (h + 1) * 64],
                            ET3[u % 3][ks][:, 0:WIN],
                            start=(ks == 0), stop=(ks == 1))
                    assert S.inc(i, "TEN") == pv_at[(w, h)]
                S.wait(tensor, "VEC", ysp)
                for h in range(DC):
                    GL.wait(tensor, wo_at[h][0])
                    GL.wait(tensor, wo_at[h][1])
                    if h >= 2:
                        S.wait(tensor, "VEC", xn_at[h - 2])
                    for c in range(DC):
                        wh = WOB[:, (h % 2) * 2048 + c * 128:
                                 (h % 2) * 2048 + (c + 1) * 128]
                        wl = WOB[:, (h % 2) * 2048 + 1024 + c * 128:
                                 (h % 2) * 2048 + 1024 + (c + 1) * 128]
                        yh = YH[:, c * TT:(c + 1) * TT]
                        yl = YL[:, c * TT:(c + 1) * TT]
                        tensor.matmul(PS[h % 2][:, 0:TT], wh, yh,
                                      start=(c == 0), stop=False)
                        tensor.matmul(PS[h % 2][:, 0:TT], wh, yl,
                                      start=False, stop=False)
                        i = tensor.matmul(PS[h % 2][:, 0:TT], wl, yh,
                                          start=False, stop=(c == DC - 1))
                    assert S.inc(i, "TEN") == womm_at[h]

            @bvec
            def _(vector):
                for u in range(NU + 1):
                    if u < NU:
                        w, h = units[u]
                        for qs in range(2):
                            if qs == 0:
                                S.wait(vector, "TEN", sc_at[(w, h, 1)])
                                if u >= 3:
                                    S.wait(vector, "SCA",
                                           ex_at[units[u - 3] + (1,)])
                            sm = SM3[u % 3][:, qs * WIN:(qs + 1) * WIN]
                            vector.scalar_tensor_tensor(
                                sm, scb(u)[:, qs * WIN:(qs + 1) * WIN], 0.125,
                                CST[f"maskq{qs}"][:], AL.mult, AL.min)
                            vector.tensor_reduce(
                                NMC[:, (u % 3) * 2 + qs:(u % 3) * 2 + qs + 1],
                                sm, AX, AL.max, negate=True)
                            assert S.cinc(vector, "VEC") == ms_at[(w, h, qs)]
                    if u >= 1:
                        w, h = units[u - 1]
                        l = (u - 1) % 3
                        en_f = EN3[l]
                        for qs in range(2):
                            S.wait(vector, "SCA", ex_at[(w, h, qs)])
                            en = en_f[:, qs * WIN:(qs + 1) * WIN]
                            vector.reciprocal(
                                NMC[:, 12 + l * 2 + qs:13 + l * 2 + qs],
                                NMC[:, 6 + l * 2 + qs:7 + l * 2 + qs])
                            vector.tensor_scalar(
                                en, en,
                                NMC[:, 12 + l * 2 + qs:13 + l * 2 + qs],
                                None, AL.mult)
                            assert S.cinc(vector, "VEC") == nm_at[(w, h, qs)]
                S.wait(vector, "SCA", ye_at[(1, NH // 2 - 1)])
                vector.tensor_copy(YH[:], Y[:])
                vector.tensor_tensor(YL[:], Y[:], YH[:], AL.subtract)
                assert S.cinc(vector, "VEC") == ysp
                for h in range(DC):
                    S.wait(vector, "TEN", womm_at[h])
                    GC.wait(vector, xo_at[h])
                    if h >= 2:
                        GS.wait(vector, st_at[h - 2])
                    vector.tensor_tensor(TSL(14 + h % 2)[:],
                                         PS[h % 2][:, 0:TT],
                                         XOL[h % 2][:], AL.add)
                    assert S.cinc(vector, "VEC") == xn_at[h]

            @bsca
            def _(scalar):
                for u in range(NU + 1):
                    if u < NU:
                        w, h = units[u]
                        l = u % 3
                        for qs in range(2):
                            S.wait(scalar, "VEC", ms_at[(w, h, qs)])
                            if qs == 0 and u >= 3:
                                S.wait(scalar, "TEN",
                                       tp_at[units[u - 3] + (1, 1)])
                            scalar.activation(
                                EN3[l][:, qs * WIN:(qs + 1) * WIN],
                                SM3[l][:, qs * WIN:(qs + 1) * WIN], AF.Exp,
                                bias=NMC[:, l * 2 + qs:l * 2 + qs + 1],
                                scale=1.0,
                                accum_out=NMC[:, 6 + l * 2 + qs:
                                              7 + l * 2 + qs])
                            assert S.cinc(scalar, "SCA") == ex_at[(w, h, qs)]
                    if u >= 1:
                        w, h = units[u - 1]
                        lp = (u - 1) % 3
                        S.wait(scalar, "TEN", tp_at[(w, h, 1, 1)])
                        scalar.copy(ET3[lp][0][:, 0:WIN],
                                    tpb(u - 1)[:, 0:WIN])
                        scalar.copy(ET3[lp][1][:, 0:WIN],
                                    tpb(u - 1)[:, WIN:2 * WIN])
                        assert S.cinc(scalar, "SCA") == et_at[(w, h)]
                        if h % 2 == 1:
                            S.wait(scalar, "TEN", pv_at[(w, h)])
                            hp = h // 2
                            scalar.copy(Y[:, hp * TT + w * WIN:
                                          hp * TT + (w + 1) * WIN],
                                        PS[6 + w][:, 0:WIN])
                            assert S.cinc(scalar, "SCA") == ye_at[(w, hp)]

            @block.gpsimd
            def _(gp):
                for h in range(DC + 1):
                    if h < DC:
                        if h >= 2:
                            S.wait(gp, "VEC", xn_at[h - 2])
                        i = gp.dma_start(
                            out=XOL[h % 2][:],
                            in_=ap[x_src][h * 128:(h + 1) * 128,
                                          wp * TT:(wp + 1) * TT])
                        GC.issue(i, h % 2, xo_at[h])
                    if h >= 1:
                        S.wait(gp, "VEC", xn_at[h - 1])
                        i = gp.dma_start(
                            out=ap[x_dst][(h - 1) * 128:h * 128,
                                          wp * TT:(wp + 1) * TT],
                            in_=TSL(14 + (h - 1) % 2)[:])
                        GS.issue(i, (h - 1) % 2, st_at[h - 1])
                GS.wait_all(gp)

            my_barrier()

        def emit_attn(x_src, x_dst):
            if active():
                cl = GL.plan_counters()
                la = DmaGroup.plan(cl, 0, GL.K)
                lb = DmaGroup.plan(cl, 1, GL.K)

                @block.sync
                def _(sync):
                    i = sync.dma_start(out=ROPE_A[:], in_=ap["cropeA"])
                    GL.issue(i, 0, la)
                    i = sync.dma_start(out=ROPE_B[:], in_=ap["cropeB"])
                    GL.issue(i, 1, lb)
                    GL.wait(sync, la)
                    GL.wait(sync, lb)

                @bvec
                def _(vector):
                    for c in range(DC):
                        vector.tensor_scalar(GBC[0:2, c * 128:(c + 1) * 128],
                                             CST["rmsbc"][0:2, 0:128],
                                             QG[0:2, c:c + 1], None, AL.mult)
                        S.cinc(vector, "VEC")

                my_barrier()
                for t in range(NT):
                    emit_a2_tile(t, x_src)
            if active():
                for wp in range(NT):
                    emit_a3_wp(wp, x_src, x_dst)

        emit_ffn("x", "xmid", 0)
        emit_attn("xmid", "xmid2")
        emit_ffn("xmid2", "y", 5)

        @block.gpsimd
        def _(gp):
            GS.wait_all(gp)

    return nc


def kernel(**inputs):
    x = np.ascontiguousarray(inputs["L0_attn_x"])
    per_core = []
    for b in range(N_CORES):
        m = {
            "x": np.ascontiguousarray(x[b].T),
            "wq": np.ascontiguousarray(inputs["L0_attn_wq"]),
            "wk": np.ascontiguousarray(inputs["L0_attn_wk"]),
            "wv": np.ascontiguousarray(inputs["L0_attn_wv"]),
            "wo": np.ascontiguousarray(inputs["L0_attn_wo"]),
            "wg": np.ascontiguousarray(inputs["L0_ffn_w_gate"]),
            "wu": np.ascontiguousarray(inputs["L0_ffn_w_up"]),
            "wd": np.ascontiguousarray(inputs["L0_ffn_w_down"]),
            "qg": np.ascontiguousarray(
                inputs["L0_attn_q_gain"].reshape(NH // 2, 2).T),
        }
        for n, v in _CONSTS.items():
            m["c" + n] = v
        per_core.append(m)
    nc = build_nc()
    res = run_bass_kernel_spmd(nc, per_core, core_ids=list(range(N_CORES)))
    out = np.empty((N_CORES, T, D), np.float32)
    for b in range(N_CORES):
        out[b] = res.results[b]["y"].T
    return out


# version 10
# speedup vs baseline: 1.1091x; 1.0609x over previous
import contextlib
import os

import numpy as np

import concourse.bass as bass
import concourse.mybir as mybir
from concourse.bass_utils import run_bass_kernel_spmd

F32 = mybir.dt.float32
BF16 = mybir.dt.bfloat16
I16 = mybir.dt.int16
U16 = mybir.dt.uint16
I8 = mybir.dt.int8
AL = mybir.AluOpType
AF = mybir.ActivationFunctionType
AX = mybir.AxisListType.X

D = 1024
T = 4096
FI = 3072
NH = 16
HD = 64
WIN = 256
DC = D // 128
IC = FI // 128
TT = 512
NT = T // TT
NW = T // WIN

MAGIC = float(np.float32(12582912.0))
FMIN = float(np.finfo(np.float32).min)
FMAX = float(np.finfo(np.float32).max)
LN_EPS = 1e-5
RMS_EPS = 1.1920928955078125e-07
N_CORES = 8

_DBG = os.environ.get("MK_DBG", "") == "1"
_NPH = int(os.environ.get("MK_PHASES", "99"))


def _host_constants():
    c = {}
    c["ident"] = np.eye(128, dtype=np.float32)
    perm = np.zeros((128, 128), np.float32)
    for p in range(128):
        perm[p, p ^ 32] = 1.0
    c["perm32"] = perm
    half = HD // 2
    inv_freq = (1.0 / (10000.0 ** (np.arange(0, HD, 2, dtype=np.float32) / np.float32(HD)))).astype(np.float32)
    freqs = np.outer(np.arange(T, dtype=np.float32), inv_freq).astype(np.float32)
    cosv, sinv = np.cos(freqs).astype(np.float32), np.sin(freqs).astype(np.float32)
    A = np.zeros((128, T), np.float32)
    B = np.zeros((128, T), np.float32)
    for p in range(128):
        j = p % 64
        jj = j if j < half else j - half
        A[p] = cosv[:, jj]
        B[p] = sinv[:, jj] if j < half else -sinv[:, jj]
    c["ropeA"], c["ropeB"] = A, B
    for s in range(2):
        m = np.full((128, WIN), FMIN, np.float32)
        for r in range(128):
            m[r, : s * 128 + r + 1] = FMAX
        c[f"maskq{s}"] = m
    c["ones128"] = np.ones((128, 1), np.float32)
    c["ones1x"] = np.ones((1, 128), np.float32)
    rmsmask = np.zeros((128, 2), np.float32)
    rmsmask[:64, 0] = 1.0
    rmsmask[64:, 1] = 1.0
    c["rmsmask"] = rmsmask
    rmsbc = np.zeros((2, 128), np.float32)
    rmsbc[0, :64] = 1.0
    rmsbc[1, 64:] = 1.0
    c["rmsbc"] = rmsbc
    return c


_CONSTS = _host_constants()


class Sems:

    def __init__(self):
        self.sem = {}
        self.cnt = {}

    def add(self, nc, ctx, names):
        for n in names:
            self.sem[n] = ctx.enter_context(nc.semaphore(n))
            self.cnt[n] = 0

    def inc(self, instr, name, n=1):
        instr.then_inc(self.sem[name], n)
        self.cnt[name] += n
        return self.cnt[name]

    def wait(self, eng, name, v):
        if v > 0:
            eng.wait_ge(self.sem[name], v)

    def cinc(self, chain, name, n=1):
        chain.flush()
        chain.e.sem_inc(self.sem[name], n)
        self.cnt[name] += n
        return self.cnt[name]


class Chain:

    EMITS = {"tensor_scalar", "tensor_tensor", "scalar_tensor_tensor",
             "tensor_reduce", "tensor_copy", "tensor_scalar_mul",
             "tensor_add", "tensor_sub", "tensor_mul", "reciprocal",
             "memset", "activation", "copy", "sqrt", "mul", "add",
             "square"}

    def __init__(self, eng, S, sem):
        self.e = eng
        self.S = S
        self.semname = sem
        self.waited = S.cnt[sem]

    def flush(self):
        if self.S.cnt[self.semname] > self.waited:
            self.e.wait_ge(self.S.sem[self.semname], self.S.cnt[self.semname])
            self.waited = self.S.cnt[self.semname]

    def __getattr__(self, n):
        f = getattr(self.e, n)
        if n not in Chain.EMITS:
            return f

        def g(*a, **k):
            self.flush()
            i = f(*a, **k)
            self.S.cnt[self.semname] += 1
            i.then_inc(self.S.sem[self.semname], 1)
            return i
        return g


class DmaGroup:

    def __init__(self, nc, ctx, name, K):
        self.K = K
        self.sems = [ctx.enter_context(nc.semaphore(f"{name}{k}"))
                     for k in range(K)]
        self.cnt = [0] * K

    def plan_counters(self):
        return list(self.cnt)

    @staticmethod
    def plan(counters, slot, K):
        k = slot % K
        counters[k] += 16
        return (k, counters[k])

    def issue(self, instr, slot, expect=None):
        k = slot % self.K
        instr.then_inc(self.sems[k], 16)
        self.cnt[k] += 16
        tok = (k, self.cnt[k])
        if expect is not None:
            assert tok == expect, (tok, expect)
        return tok

    def wait(self, eng, tok):
        if tok is not None:
            eng.wait_ge(self.sems[tok[0]], tok[1])

    def wait_all(self, eng):
        for k, c in enumerate(self.cnt):
            if c:
                eng.wait_ge(self.sems[k], c)


def build_nc():
    nc = bass.Bass("TRN2", target_bir_lowering=False)
    ap = {}

    def din(name, shape, dt=F32):
        ap[name] = nc.dram_tensor(name, shape, dt, kind="ExternalInput").ap()

    def dint(name, shape, dt=F32, dbg=False):
        kind = "ExternalOutput" if (dbg and _DBG) else "Internal"
        ap[name] = nc.dram_tensor(name, shape, dt, kind=kind).ap()

    din("x", [D, T])
    for n, sh in (("wq", [D, D]), ("wk", [D, D]), ("wv", [D, D]), ("wo", [D, D]),
                  ("wg", [FI, D]), ("wu", [FI, D]), ("wd", [D, FI])):
        din(n, sh)
    din("qg", [2, NH // 2])
    for n, v in _CONSTS.items():
        din("c" + n, list(v.shape))
    ap["y"] = nc.dram_tensor("y", [D, T], F32, kind="ExternalOutput").ap()

    for n, sh in (("wqT", [D, D]), ("wkT", [D, D]), ("wvT", [D, D]),
                  ("woT", [D, D]), ("wgT", [D, FI]), ("wuT", [D, FI]),
                  ("wdT", [FI, D])):
        dint(n + "h", sh, BF16, dbg=(n == "wgT"))
        dint(n + "l", sh, BF16)
    for n in ("wg", "wu", "wd"):
        dint("q8" + n, list(ap[n].shape), I8)
    dint("up_raw", [FI, T], dbg=True)
    dint("gate_raw", [FI, T])
    dint("uq16", [FI, T], I16)
    dint("gq16", [FI, T], I16)
    dint("sq16", [FI, T], U16)
    dint("g2q16", [FI, T], I16, dbg=True)
    dint("xmid", [D, T], dbg=True)
    dint("xmid2", [D, T], dbg=True)
    dint("qrot", [D, T], dbg=True)
    dint("krot", [D, T])
    dint("vtm", [T, D], dbg=True)
    for k in range(10):
        dint(f"cci{k}", [1, 1])
        dint(f"cco{k}", [1, 1])
        dint(f"colb{k}", [128, 1])
        dint(f"sdr{k}", [1, 2])

    with contextlib.ExitStack() as ctx:
        ent = ctx.enter_context
        ARENA = ent(nc.sbuf_tensor("ARENA", [128, 32768], F32))
        POOL = ent(nc.sbuf_tensor("POOL", [128, 12288], F32))
        STAT = ent(nc.sbuf_tensor("STAT", [128, 768], F32))
        SCAL = ent(nc.sbuf_tensor("SCAL", [128, 16], F32))
        ROWS = ent(nc.sbuf_tensor("ROWS", [1, 4 * TT], F32))
        RMSW = ent(nc.sbuf_tensor("RMSW", [2, 4 * TT], F32))
        CST = {n: ent(nc.sbuf_tensor("cs" + n, list(v.shape), F32))
               for n, v in _CONSTS.items() if n not in ("ropeA", "ropeB")}
        QG = ent(nc.sbuf_tensor("qg_sb", [2, NH // 2], F32))
        EPS = ent(nc.sbuf_tensor("eps_sb", [128, 3], F32))
        PS = [ent(nc.psum_tensor(f"ps{k}", [128, 512], F32)) for k in range(8)]

        S = Sems()
        S.add(nc, ctx, ["TEN", "VEC", "SCA", "CCS", "VE2", "SC2"])
        GL = DmaGroup(nc, ctx, "GL", 8)
        GS = DmaGroup(nc, ctx, "GS", 8)
        GC = DmaGroup(nc, ctx, "GC", 8)
        GA = DmaGroup(nc, ctx, "GA", 1)
        BAR = ctx.enter_context(nc.semaphore("BAR"))

        block = ent(nc.Block())

        def bvec(f):
            block.vector(lambda v: f(Chain(v, S, "VE2")))

        def bsca(f):
            block.scalar(lambda s2: f(Chain(s2, S, "SC2")))

        barctr = [0]

        def my_barrier():
            barctr[0] += 1
            tgt = 5 * barctr[0]
            for name in ("sync", "scalar", "tensor", "vector", "gpsimd"):
                def f(eng):
                    eng.drain().then_inc(BAR, 1)
                    eng.wait_ge(BAR, tgt)
                getattr(block, name)(f)

        def ld3(eng, out2d, src2d):
            C = src2d.shape[0] // 128
            return eng.dma_start(
                out=out2d.rearrange("p (c t) -> p c t", c=C),
                in_=src2d.rearrange("(c p) t -> p c t", p=128))

        def TSL(k):
            return POOL[:, k * 512:(k + 1) * 512]

        def AR_(a, b):
            return ARENA[:, a:b]

        SLOT = {"s1": 0, "i1": 1, "s2": 2, "i2": 3, "s3": 4, "i3": 5,
                "s4": 6, "i4": 7, "s5": 8, "i5": 9, "s42": 10, "s3c": 11, "h5": 12}

        def SCs(name):
            return SCAL[:, SLOT[name]:SLOT[name] + 1]

        M0 = STAT[:, 0:192]
        M1 = STAT[:, 192:384]
        PMcol = STAT[:, 384:385]
        Grow = STAT[0:1, 385:386]
        SROW = STAT[0:1, 386:388]
        ROWB = STAT[0:1, 388:516]
        NMC = STAT[:, 520:538]

        murow = ROWS[0:1, 0:TT]
        rrow = ROWS[0:1, TT:2 * TT]
        var_row = ROWS[0:1, 2 * TT:3 * TT]
        sd_row = ROWS[0:1, 3 * TT:4 * TT]

        ROPE_A = AR_(24576, 28672)
        ROPE_B = AR_(28672, 32768)
        GBC = ARENA[0:2, 0:1024]

        ph = [0]

        def active():
            ph[0] += 1
            return ph[0] <= _NPH

        @block.sync
        def _(sync):
            names = [n for n in _CONSTS if n not in ("ropeA", "ropeB")]
            for idx, n in enumerate(names):
                i = sync.dma_start(out=CST[n][:], in_=ap["c" + n])
                GL.issue(i, idx)
            i = sync.dma_start(out=QG[:], in_=ap["qg"])
            GL.issue(i, len(names))
            GL.wait_all(sync)

        @bvec
        def _(vector):
            vector.memset(EPS[:, 0:1], LN_EPS)
            vector.memset(EPS[:, 1:2], RMS_EPS)
            vector.memset(EPS[:, 2:3], MAGIC)

        my_barrier()

        def emit_wprep():
            plans = [("wq", D, D, False), ("wk", D, D, False),
                     ("wv", D, D, False), ("wo", D, D, False),
                     ("wg", FI, D, True), ("wu", FI, D, True),
                     ("wd", D, FI, True)]
            loads = []
            for (n, rows, cols, quant) in plans:
                for oc in range(rows // 128):
                    loads.append((n, oc, cols, quant))
            NL = len(loads)

            wbuf = [AR_(0, 3072), AR_(3072, 6144)]
            qf = [AR_(6144, 9216), AR_(9216, 12288)]
            dq = [AR_(12288, 15360), AR_(15360, 18432)]
            sA = [M0[:, 0:96], M0[:, 96:192]]
            sR = [M1[:, 0:96], M1[:, 96:192]]
            EV = AR_(18432, 18944)
            EVH = AR_(18944, 19200).bitcast(BF16)
            EVL = AR_(19200, 19456).bitcast(BF16)

            b_vec, b_ten, b_sca = S.cnt["VEC"], S.cnt["TEN"], S.cnt["SCA"]
            cl, cc_, cs_ = (GL.plan_counters(), GC.plan_counters(),
                            GS.plan_counters())
            ld_at = [None] * NL
            q8st_at = [None] * NL
            q8ld_at = [None] * NL
            qrdy_at = [0] * NL
            deq_at = [0] * NL
            ntp = [0] * NL
            tp0 = [0] * NL
            g = 0
            for idx, (n, oc, cols, quant) in enumerate(loads):
                ld_at[idx] = DmaGroup.plan(cl, idx % 2, GL.K)
                if quant:
                    q8st_at[idx] = DmaGroup.plan(cc_, idx % 2, GC.K)
                    q8ld_at[idx] = DmaGroup.plan(cc_, idx % 2, GC.K)
                ntp[idx] = cols // 128
                tp0[idx] = g
                g += ntp[idx]
            NG = g
            isq = [None] * NG
            for idx in range(NL):
                for k in range(ntp[idx]):
                    isq[tp0[idx] + k] = idx
            vec = b_vec
            vsp_at = [0] * NG
            for idx in range(NL + 1):
                if idx < NL:
                    if loads[idx][3]:
                        vec += 1
                        qrdy_at[idx] = vec
                    vec += 1
                    deq_at[idx] = vec
                if idx >= 1:
                    for k in range(ntp[idx - 1]):
                        vec += 1
                        vsp_at[tp0[idx - 1] + k] = vec
            ten_at = [b_ten + j + 1 for j in range(NG)]
            sca_at = [b_sca + j + 1 for j in range(NG)]
            sth_at = [None] * NG
            stl_at = [None] * NG
            st_at = [None] * NG
            for idx in range(NL + 1):
                if idx >= 1:
                    for k in range(ntp[idx - 1]):
                        j = tp0[idx - 1] + k
                        sth_at[j] = DmaGroup.plan(cs_, j % 4, GS.K)
                        stl_at[j] = DmaGroup.plan(cs_, 4 + j % 4, GS.K)

            @block.sync
            def _(sync):
                for idx, (n, oc, cols, quant) in enumerate(loads):
                    b = idx % 2
                    if idx >= 2:
                        S.wait(sync, "TEN", ten_at[tp0[idx - 2] + ntp[idx - 2] - 1])
                    i = sync.dma_start(out=wbuf[b][:, 0:cols],
                                       in_=ap[n][oc * 128:(oc + 1) * 128, :])
                    GL.issue(i, idx % 2, ld_at[idx])

            @bvec
            def _(vector):
                for idx in range(NL + 1):
                    if idx < NL:
                        n, oc, cols, quant = loads[idx]
                        b = idx % 2
                        GL.wait(vector, ld_at[idx])
                        if quant:
                            nb = cols // 32
                            w3 = wbuf[b][:, 0:cols].rearrange("p (b k) -> p b k", k=32)
                            vector.tensor_reduce(
                                sA[b][:, 0:nb].rearrange("p (b o) -> p b o", o=1),
                                w3, AX, AL.max, apply_absolute_value=True)
                            vector.tensor_scalar(sA[b][:, 0:nb], sA[b][:, 0:nb],
                                                 1.0 / 7.0, 1e-12, AL.mult, AL.max)
                            vector.reciprocal(sR[b][:, 0:nb], sA[b][:, 0:nb])
                            rbc = sR[b][:, 0:nb].rearrange("p (b o) -> p b o", o=1).broadcast_to([128, nb, 32])
                            vector.tensor_tensor(
                                qf[b][:, 0:cols].rearrange("p (b k) -> p b k", k=32),
                                w3, rbc, AL.mult)
                            vector.tensor_scalar(qf[b][:, 0:cols], qf[b][:, 0:cols],
                                                 1.0, MAGIC, AL.mult, AL.add)
                            vector.tensor_scalar(qf[b][:, 0:cols], qf[b][:, 0:cols],
                                                 MAGIC, 7.0, AL.subtract, AL.min)
                            assert S.cinc(vector, "VEC") == qrdy_at[idx]
                            GC.wait(vector, q8ld_at[idx])
                            sbc = sA[b][:, 0:nb].rearrange("p (b o) -> p b o", o=1).broadcast_to([128, nb, 32])
                            vector.tensor_tensor(
                                dq[b][:, 0:cols].rearrange("p (b k) -> p b k", k=32),
                                qf[b][:, 0:cols].rearrange("p (b k) -> p b k", k=32),
                                sbc, AL.mult)
                            assert S.cinc(vector, "VEC") == deq_at[idx]
                        else:
                            vector.tensor_copy(dq[b][:, 0:cols], wbuf[b][:, 0:cols])
                            assert S.cinc(vector, "VEC") == deq_at[idx]
                    if idx >= 1:
                        for k in range(ntp[idx - 1]):
                            j = tp0[idx - 1] + k
                            S.wait(vector, "SCA", sca_at[j])
                            if j >= 4:
                                GS.wait(vector, sth_at[j - 4])
                                GS.wait(vector, stl_at[j - 4])
                            sl_ = slice((j % 4) * 128, (j % 4 + 1) * 128)
                            vector.tensor_copy(EVH[:, sl_], EV[:, sl_])
                            vector.tensor_tensor(EVL[:, sl_], EV[:, sl_],
                                                 EVH[:, sl_], AL.subtract)
                            assert S.cinc(vector, "VEC") == vsp_at[j]

            @block.gpsimd
            def _(gp):
                for idx in range(NL + 1):
                    if idx < NL:
                        n, oc, cols, quant = loads[idx]
                        b = idx % 2
                        if quant:
                            S.wait(gp, "VEC", qrdy_at[idx])
                            i = gp.dma_start(
                                out=ap["q8" + n][oc * 128:(oc + 1) * 128, :],
                                in_=qf[b][:, 0:cols])
                            GC.issue(i, b, q8st_at[idx])
                            GC.wait(gp, q8st_at[idx])
                            i = gp.dma_start(
                                out=qf[b][:, 0:cols],
                                in_=ap["q8" + n][oc * 128:(oc + 1) * 128, :])
                            GC.issue(i, b, q8ld_at[idx])
                    if idx >= 1:
                        n, oc, cols, quant = loads[idx - 1]
                        for k in range(ntp[idx - 1]):
                            j = tp0[idx - 1] + k
                            sl_ = slice((j % 4) * 128, (j % 4 + 1) * 128)
                            S.wait(gp, "VEC", vsp_at[j])
                            i = gp.dma_start(
                                out=ap[n + "Th"][k * 128:(k + 1) * 128,
                                                 oc * 128:(oc + 1) * 128],
                                in_=EVH[:, sl_])
                            GS.issue(i, j % 4, sth_at[j])
                            i = gp.dma_start(
                                out=ap[n + "Tl"][k * 128:(k + 1) * 128,
                                                 oc * 128:(oc + 1) * 128],
                                in_=EVL[:, sl_])
                            GS.issue(i, 4 + j % 4, stl_at[j])
                GS.wait_all(gp)

            @block.tensor
            def _(tensor):
                for idx, (n, oc, cols, quant) in enumerate(loads):
                    b = idx % 2
                    S.wait(tensor, "VEC", deq_at[idx])
                    for k in range(ntp[idx]):
                        j = tp0[idx] + k
                        if j >= 4:
                            S.wait(tensor, "SCA", sca_at[j - 4])
                        i = tensor.transpose(PS[4 + j % 4][:, 0:128],
                                             dq[b][:, k * 128:(k + 1) * 128],
                                             CST["ident"][:])
                        assert S.inc(i, "TEN") == ten_at[j]

            @bsca
            def _(scalar):
                for j in range(NG):
                    S.wait(scalar, "TEN", ten_at[j])
                    if j >= 4:
                        S.wait(scalar, "VEC", vsp_at[j - 4])
                    i = scalar.copy(EV[:, (j % 4) * 128:(j % 4 + 1) * 128],
                                    PS[4 + j % 4][:, 0:128])
                    assert S.cinc(scalar, "SCA") == sca_at[j]

        if active():
            emit_wprep()
            my_barrier()

        def emit_allreduce(k, statr, ncols, s_slot, i_slot):
            b_vec, b_ccs = S.cnt["VEC"], S.cnt["CCS"]
            v1, v2, v3 = b_vec + 1, b_vec + 2, b_vec + 3
            ca = GA.plan_counters()
            g = [DmaGroup.plan(ca, 0, 1) for _ in range(7)]
            cc1 = b_ccs + 1

            @bvec
            def _(vector):
                vector.tensor_reduce(PMcol[:], statr[:, 0:ncols], AX, AL.max)
                assert S.cinc(vector, "VEC") == v1
                GA.wait(vector, g[1])
                vector.tensor_reduce(Grow[:], ROWB[:], AX, AL.max)
                assert S.cinc(vector, "VEC") == v2
                GA.wait(vector, g[3])
                vector.tensor_scalar(SROW[0:1, 0:1], Grow[:],
                                         1.0 / 32767.0, 1e-12,
                                         AL.mult, AL.max)
                vector.reciprocal(SROW[0:1, 1:2], SROW[0:1, 0:1])
                assert S.cinc(vector, "VEC") == v3

            @block.gpsimd
            def _(gp):
                S.wait(gp, "VEC", v1)
                i = gp.dma_start(out=ap[f"colb{k}"], in_=PMcol[:])
                GA.issue(i, 0, g[0])
                GA.wait(gp, g[0])
                i = gp.dma_start(out=ROWB[:], in_=ap[f"colb{k}"].transpose([1, 0]))
                GA.issue(i, 0, g[1])
                S.wait(gp, "VEC", v2)
                i = gp.dma_start(out=ap[f"cci{k}"], in_=Grow[:])
                GA.issue(i, 0, g[2])
                GA.wait(gp, g[2])
                i = gp.collective_compute(
                    "AllReduce", AL.max, replica_groups=[list(range(N_CORES))],
                    ins=[ap[f"cci{k}"]], outs=[ap[f"cco{k}"]])
                assert S.inc(i, "CCS") == cc1
                S.wait(gp, "CCS", cc1)
                i = gp.dma_start(out=Grow[:], in_=ap[f"cco{k}"])
                GA.issue(i, 0, g[3])
                S.wait(gp, "VEC", v3)
                i = gp.dma_start(out=ap[f"sdr{k}"], in_=SROW[:])
                GA.issue(i, 0, g[4])
                GA.wait(gp, g[4])
                i = gp.dma_start(out=SCs(s_slot),
                                 in_=ap[f"sdr{k}"][0:1, 0:1].broadcast_to([128, 1]))
                GA.issue(i, 0, g[5])
                GA.wait(gp, g[5])
                i = gp.dma_start(out=SCs(i_slot),
                                 in_=ap[f"sdr{k}"][0:1, 1:2].broadcast_to([128, 1]))
                GA.issue(i, 0, g[6])
                GA.wait(gp, g[6])

            my_barrier()

        def emit_ln(x_src, consume, ninc_per_chunk):
            XT2 = [POOL[:, 0:4096], POOL[:, 4096:8192]]
            SQB = POOL[:, 8192:12288]
            b_ten, b_sca, b_vec = S.cnt["TEN"], S.cnt["SCA"], S.cnt["VEC"]
            cl = GL.plan_counters()
            ld_at = [DmaGroup.plan(cl, t % 2, GL.K) for t in range(NT)]
            sq_at = [b_sca + 2 * t + 1 for t in range(NT)]
            sd_at = [b_sca + 2 * t + 2 for t in range(NT)]
            stats_at = [b_ten + 2 * t + 1 for t in range(NT)]
            bcast_at = [b_ten + 2 * t + 2 for t in range(NT)]
            vpt = 2 + DC * ninc_per_chunk
            row_at = [b_vec + vpt * t + 1 for t in range(NT)]
            rr_at = [b_vec + vpt * t + 2 for t in range(NT)]
            tile_at = [b_vec + vpt * (t + 1) for t in range(NT)]

            @block.sync
            def _(sync):
                for t in range(NT):
                    if t >= 2:
                        S.wait(sync, "VEC", tile_at[t - 2])
                    i = ld3(sync, XT2[t % 2][:],
                            ap[x_src][:, t * TT:(t + 1) * TT])
                    GL.issue(i, t % 2, ld_at[t])

            @bsca
            def _(scalar):
                for t in range(NT):
                    GL.wait(scalar, ld_at[t])
                    if t >= 1:
                        S.wait(scalar, "TEN", stats_at[t - 1])
                        S.wait(scalar, "VEC", tile_at[t - 1])
                    i = scalar.activation(SQB[:], XT2[t % 2][:], AF.Square)
                    assert S.cinc(scalar, "SCA") == sq_at[t]
                    S.wait(scalar, "VEC", row_at[t])
                    i = scalar.activation(sd_row, var_row, AF.Sqrt,
                                          bias=EPS[0:1, 0:1], scale=1.0)
                    assert S.cinc(scalar, "SCA") == sd_at[t]

            @block.tensor
            def _(tensor):
                for t in range(NT):
                    GL.wait(tensor, ld_at[t])
                    S.wait(tensor, "SCA", sq_at[t])
                    if t >= 1:
                        S.wait(tensor, "VEC", row_at[t - 1])
                        S.wait(tensor, "VEC", tile_at[t - 1])
                    for c in range(DC):
                        tensor.matmul(PS[0][0:1, 0:TT], CST["ones128"][:],
                                      XT2[t % 2][:, c * TT:(c + 1) * TT],
                                      start=(c == 0), stop=(c == DC - 1))
                    for c in range(DC):
                        i = tensor.matmul(PS[1][0:1, 0:TT], CST["ones128"][:],
                                          SQB[:, c * TT:(c + 1) * TT],
                                          start=(c == 0), stop=(c == DC - 1))
                    assert S.inc(i, "TEN") == stats_at[t]
                    S.wait(tensor, "VEC", rr_at[t])
                    tensor.matmul(PS[2][:, 0:TT], CST["ones1x"][:],
                                  murow, start=True, stop=True)
                    i = tensor.matmul(PS[3][:, 0:TT], CST["ones1x"][:],
                                      rrow, start=True, stop=True)
                    assert S.inc(i, "TEN") == bcast_at[t]

            @bvec
            def _(vector):
                for t in range(NT):
                    S.wait(vector, "TEN", stats_at[t])
                    vector.tensor_scalar(murow, PS[0][0:1, 0:TT],
                                         1.0 / float(D), None, AL.mult)
                    vector.tensor_scalar(var_row, PS[1][0:1, 0:TT],
                                         1.0 / float(D), None, AL.mult)
                    vector.tensor_tensor(sd_row, murow, murow, AL.mult)
                    vector.tensor_tensor(var_row, var_row, sd_row,
                                             AL.subtract)
                    assert S.cinc(vector, "VEC") == row_at[t]
                    S.wait(vector, "SCA", sd_at[t])
                    vector.reciprocal(rrow, sd_row)
                    assert S.cinc(vector, "VEC") == rr_at[t]
                    S.wait(vector, "TEN", bcast_at[t])
                    for c in range(DC):
                        consume(vector, t, c, XT2[t % 2][:, c * TT:(c + 1) * TT])
                    assert S.cnt["VEC"] == tile_at[t]

        def emit_ffn(x_src, x_dst, ccb):
            def p0a_consume(vector, t, c, xc):
                tmp = AR_(0, 512)
                tmp2 = AR_(512, 1024)
                vector.tensor_tensor(tmp, xc, PS[2][:, 0:TT], AL.subtract)
                vector.tensor_tensor(tmp2, tmp, PS[3][:, 0:TT], AL.mult)
                vector.tensor_reduce(M0[:, t * DC + c:t * DC + c + 1],
                                         tmp2, AX, AL.max,
                                         apply_absolute_value=True)
                S.cinc(vector, "VEC")

            if active():
                emit_ln(x_src, p0a_consume, 1)
                my_barrier()
                emit_allreduce(ccb + 0, M0, 64, "s1", "i1")

            XQH = ARENA[:, 0:16384].bitcast(BF16)
            XQL = ARENA[:, 16384:32768].bitcast(BF16)

            def xqh_ct(c, t):
                return XQH[:, c * 4096 + t * TT:c * 4096 + (t + 1) * TT]

            def xql_ct(c, t):
                return XQL[:, c * 4096 + t * TT:c * 4096 + (t + 1) * TT]

            def p0b_consume(vector, t, c, xc):
                tmp = POOL[:, 8192:8704]
                tmp2 = POOL[:, 8704:9216]
                vector.tensor_tensor(tmp, xc, PS[2][:, 0:TT], AL.subtract)
                vector.tensor_tensor(tmp2, tmp, PS[3][:, 0:TT], AL.mult)
                vector.tensor_scalar(tmp, tmp2, SCs("i1"), MAGIC,
                                     AL.mult, AL.add)
                vector.tensor_scalar(tmp2, tmp, MAGIC, 32767.0,
                                     AL.subtract, AL.min)
                vector.tensor_copy(xqh_ct(c, t), tmp2)
                vector.tensor_tensor(xql_ct(c, t), tmp2, xqh_ct(c, t),
                                     AL.subtract)
                S.cinc(vector, "VEC")

            if active():
                emit_ln(x_src, p0b_consume, 1)
                my_barrier()

            def xq_c(c):
                return AR_(c * 4096, (c + 1) * 4096)

            if active():
                WST = [POOL[:, 8192:10240], POOL[:, 10240:12288]]
                b_ten, b_sca, b_vec = S.cnt["TEN"], S.cnt["SCA"], S.cnt["VEC"]
                cl, cs_ = GL.plan_counters(), GS.plan_counters()
                NJ = IC * NT
                ldu_at = [None] * IC
                ldg_at = [None] * IC
                for i in range(IC):
                    ldu_at[i] = (DmaGroup.plan(cl, i % 2, GL.K),
                                 DmaGroup.plan(cl, 2 + i % 2, GL.K))
                    ldg_at[i] = (DmaGroup.plan(cl, 4 + i % 2, GL.K),
                                 DmaGroup.plan(cl, 6 + i % 2, GL.K))
                mmu_at = [b_ten + 2 * j + 1 for j in range(NJ)]
                mmg_at = [b_ten + 2 * j + 2 for j in range(NJ)]
                evu_at = [b_sca + 2 * j + 1 for j in range(NJ)]
                evg_at = [b_sca + 2 * j + 2 for j in range(NJ)]
                mxu_at = [b_vec + 2 * j + 1 for j in range(NJ)]
                mxg_at = [b_vec + 2 * j + 2 for j in range(NJ)]
                stu_at = [None] * NJ
                stg_at = [None] * NJ
                cs2 = cs_
                for j in range(NJ):
                    stu_at[j] = DmaGroup.plan(cs2, j % 4, GS.K)
                    stg_at[j] = DmaGroup.plan(cs2, 4 + j % 4, GS.K)

                WB = POOL[:, 8192:12288].bitcast(BF16)

                def wsl(i, plane):
                    base = (i % 2) * 4096 + plane * 1024
                    return WB[:, base:base + 1024]

                @block.sync
                def _(sync):
                    for i in range(IC):
                        if i >= 2:
                            S.wait(sync, "TEN", mmg_at[(i - 1) * NT - 1])
                        for pl, (nm, tok) in enumerate(
                                (("wuTh", ldu_at[i][0]), ("wuTl", ldu_at[i][1]),
                                 ("wgTh", ldg_at[i][0]), ("wgTl", ldg_at[i][1]))):
                            k = ld3(sync, wsl(i, pl),
                                    ap[nm][:, i * 128:(i + 1) * 128])
                            GL.issue(k, 2 * pl + i % 2, tok)

                @block.tensor
                def _(tensor):
                    for i in range(IC):
                        for tok in ldu_at[i] + ldg_at[i]:
                            GL.wait(tensor, tok)
                        for t in range(NT):
                            j = i * NT + t
                            if j >= 2:
                                S.wait(tensor, "SCA", evu_at[j - 2])
                            for c in range(DC):
                                wh = wsl(i, 0)[:, c * 128:(c + 1) * 128]
                                wl = wsl(i, 1)[:, c * 128:(c + 1) * 128]
                                tensor.matmul(PS[j % 2][:, 0:TT], wh,
                                              xqh_ct(c, t),
                                              start=(c == 0), stop=False)
                                tensor.matmul(PS[j % 2][:, 0:TT], wh,
                                              xql_ct(c, t),
                                              start=False, stop=False)
                                k = tensor.matmul(PS[j % 2][:, 0:TT], wl,
                                                  xqh_ct(c, t), start=False,
                                                  stop=(c == DC - 1))
                            assert S.inc(k, "TEN") == mmu_at[j]
                            if j >= 2:
                                S.wait(tensor, "SCA", evg_at[j - 2])
                            for c in range(DC):
                                wh = wsl(i, 2)[:, c * 128:(c + 1) * 128]
                                wl = wsl(i, 3)[:, c * 128:(c + 1) * 128]
                                tensor.matmul(PS[2 + j % 2][:, 0:TT], wh,
                                              xqh_ct(c, t),
                                              start=(c == 0), stop=False)
                                tensor.matmul(PS[2 + j % 2][:, 0:TT], wh,
                                              xql_ct(c, t),
                                              start=False, stop=False)
                                k = tensor.matmul(PS[2 + j % 2][:, 0:TT], wl,
                                                  xqh_ct(c, t), start=False,
                                                  stop=(c == DC - 1))
                            assert S.inc(k, "TEN") == mmg_at[j]

                @bsca
                def _(scalar):
                    for j in range(NJ):
                        S.wait(scalar, "TEN", mmu_at[j])
                        if j >= 4:
                            S.wait(scalar, "VEC", mxu_at[j - 4])
                            GS.wait(scalar, stu_at[j - 4])
                        scalar.activation(TSL(j % 4), PS[j % 2][:, 0:TT],
                                          AF.Copy, scale=SCs("s1"))
                        assert S.cinc(scalar, "SCA") == evu_at[j]
                        S.wait(scalar, "TEN", mmg_at[j])
                        if j >= 4:
                            S.wait(scalar, "VEC", mxg_at[j - 4])
                            GS.wait(scalar, stg_at[j - 4])
                        scalar.activation(TSL(4 + j % 4), PS[2 + j % 2][:, 0:TT],
                                          AF.Copy, scale=SCs("s1"))
                        assert S.cinc(scalar, "SCA") == evg_at[j]

                @bvec
                def _(vector):
                    for j in range(NJ):
                        S.wait(vector, "SCA", evu_at[j])
                        vector.tensor_reduce(M0[:, j:j + 1], TSL(j % 4),
                                                 AX, AL.max,
                                                 apply_absolute_value=True)
                        assert S.cinc(vector, "VEC") == mxu_at[j]
                        S.wait(vector, "SCA", evg_at[j])
                        vector.tensor_reduce(M1[:, j:j + 1], TSL(4 + j % 4),
                                                 AX, AL.max,
                                                 apply_absolute_value=True)
                        assert S.cinc(vector, "VEC") == mxg_at[j]

                @block.gpsimd
                def _(gp):
                    for j in range(NJ):
                        i, t = j // NT, j % NT
                        S.wait(gp, "SCA", evu_at[j])
                        k = gp.dma_start(
                            out=ap["up_raw"][i * 128:(i + 1) * 128,
                                             t * TT:(t + 1) * TT],
                            in_=TSL(j % 4)[:])
                        GS.issue(k, j % 4, stu_at[j])
                        S.wait(gp, "SCA", evg_at[j])
                        k = gp.dma_start(
                            out=ap["gate_raw"][i * 128:(i + 1) * 128,
                                               t * TT:(t + 1) * TT],
                            in_=TSL(4 + j % 4)[:])
                        GS.issue(k, 4 + j % 4, stg_at[j])
                    GS.wait_all(gp)

                my_barrier()
                emit_allreduce(ccb + 1, M0, NJ, "s2", "i2")
                emit_allreduce(ccb + 2, M1, NJ, "s3", "i3")

            if active():
                NJ = IC * NT
                b_vec, b_sca = S.cnt["VEC"], S.cnt["SCA"]
                cl, cs_ = GL.plan_counters(), GS.plan_counters()
                cc2 = GC.plan_counters()
                ldu_at = [None] * NJ
                ldg_at = [None] * NJ
                stu_at = [None] * NJ
                stg_at = [None] * NJ
                sts_at = [None] * NJ
                for j in range(NJ):
                    ldu_at[j] = DmaGroup.plan(cl, (j % 3) * 2, GL.K)
                    ldg_at[j] = DmaGroup.plan(cl, (j % 3) * 2 + 1, GL.K)
                    stu_at[j] = DmaGroup.plan(cs_, j % 3, GS.K)
                    stg_at[j] = DmaGroup.plan(cs_, 3 + j % 3, GS.K)
                    sts_at[j] = DmaGroup.plan(cc2, j % 3, GC.K)
                sig_at, w1s_at = {}, {}
                sc2 = b_sca
                for j in range(NJ):
                    sc2 += 1
                    sig_at[j] = sc2
                    sc2 += 1
                    w1s_at[j] = sc2
                uq_at, gq_at, sq_at, tl_at = {}, {}, {}, {}
                vc2 = b_vec + 1
                for j in range(NJ + 1):
                    if j < NJ:
                        vc2 += 1
                        uq_at[j] = vc2
                        vc2 += 1
                        gq_at[j] = vc2
                    if j >= 1:
                        vc2 += 1
                        sq_at[j - 1] = vc2
                        vc2 += 1
                        tl_at[j - 1] = vc2

                def sl(j, k):
                    return TSL((j % 3) * 8 + k)

                @block.sync
                def _(sync):
                    for j in range(NJ):
                        i, t = j // NT, j % NT
                        if j >= 3:
                            S.wait(sync, "VEC", uq_at[j - 3])
                        k = sync.dma_start(
                            out=sl(j, 0)[:],
                            in_=ap["up_raw"][i * 128:(i + 1) * 128,
                                             t * TT:(t + 1) * TT])
                        GL.issue(k, (j % 3) * 2, ldu_at[j])
                        if j >= 3:
                            S.wait(sync, "VEC", gq_at[j - 3])
                        k = sync.dma_start(
                            out=sl(j, 1)[:],
                            in_=ap["gate_raw"][i * 128:(i + 1) * 128,
                                               t * TT:(t + 1) * TT])
                        GL.issue(k, (j % 3) * 2 + 1, ldg_at[j])

                @bsca
                def _(scalar):
                    for j in range(NJ):
                        S.wait(scalar, "VEC", gq_at[j])
                        scalar.activation(sl(j, 6), sl(j, 4), AF.Sigmoid,
                                          scale=SCs("s3"))
                        assert S.cinc(scalar, "SCA") == sig_at[j]
                        scalar.activation(sl(j, 2), sl(j, 6), AF.Copy,
                                          scale=65536.0, bias=MAGIC)
                        assert S.cinc(scalar, "SCA") == w1s_at[j]

                @bvec
                def _(vector):
                    vector.tensor_scalar(SCs("s3c"), SCs("s3"),
                                             float(2.0 ** -16), None, AL.mult)
                    S.cinc(vector, "VEC")
                    for j in range(NJ + 1):
                        if j < NJ:
                            GL.wait(vector, ldu_at[j])
                            if j >= 3:
                                GS.wait(vector, stu_at[j - 3])
                            vector.tensor_scalar(sl(j, 2), sl(j, 0),
                                                 SCs("i2"), MAGIC,
                                                 AL.mult, AL.add)
                            vector.tensor_scalar(sl(j, 3), sl(j, 2), MAGIC,
                                                     32767.0,
                                                     AL.subtract, AL.min)
                            assert S.cinc(vector, "VEC") == uq_at[j]
                            GL.wait(vector, ldg_at[j])
                            if j >= 3:
                                GS.wait(vector, stg_at[j - 3])
                            vector.tensor_scalar(sl(j, 2), sl(j, 1),
                                                 SCs("i3"), MAGIC,
                                                 AL.mult, AL.add)
                            vector.tensor_scalar(sl(j, 4), sl(j, 2), MAGIC,
                                                     32767.0,
                                                     AL.subtract, AL.min)
                            assert S.cinc(vector, "VEC") == gq_at[j]
                        if j >= 1:
                            jj = j - 1
                            S.wait(vector, "SCA", w1s_at[jj])
                            if jj >= 3:
                                GC.wait(vector, sts_at[jj - 3])
                            vector.tensor_scalar(sl(jj, 7), sl(jj, 2), MAGIC,
                                                 65535.0,
                                                 AL.subtract, AL.min)
                            assert S.cinc(vector, "VEC") == sq_at[jj]
                            vector.scalar_tensor_tensor(sl(jj, 5), sl(jj, 7),
                                                            SCs("s3c"),
                                                            sl(jj, 4),
                                                            AL.mult, AL.mult)
                            vector.tensor_reduce(M0[:, jj:jj + 1], sl(jj, 5),
                                                     AX, AL.max,
                                                     apply_absolute_value=True)
                            assert S.cinc(vector, "VEC") == tl_at[jj]

                @block.gpsimd
                def _(gp):
                    for j in range(NJ):
                        i, t = j // NT, j % NT
                        dst = (slice(i * 128, (i + 1) * 128),
                               slice(t * TT, (t + 1) * TT))
                        S.wait(gp, "VEC", uq_at[j])
                        k = gp.dma_start(out=ap["uq16"][dst[0], dst[1]],
                                         in_=sl(j, 3)[:])
                        GS.issue(k, j % 3, stu_at[j])
                        S.wait(gp, "VEC", gq_at[j])
                        k = gp.dma_start(out=ap["gq16"][dst[0], dst[1]],
                                         in_=sl(j, 4)[:])
                        GS.issue(k, 3 + j % 3, stg_at[j])
                        S.wait(gp, "VEC", sq_at[j])
                        k = gp.dma_start(out=ap["sq16"][dst[0], dst[1]],
                                         in_=sl(j, 7)[:])
                        GC.issue(k, j % 3, sts_at[j])
                    GS.wait_all(gp)
                    GC.wait_all(gp)

                my_barrier()
                emit_allreduce(ccb + 3, M0, NJ, "s4", "i4")

            if active():
                NJ = IC * NT
                b_vec, b_sca = S.cnt["VEC"], S.cnt["SCA"]
                cc_, cl, cs_ = (GC.plan_counters(), GL.plan_counters(),
                                GS.plan_counters())
                ldu_at = [None] * NJ
                ldg_at = [None] * NJ
                lds_at = [None] * NJ
                st_at = [None] * NJ
                for j in range(NJ):
                    ldu_at[j] = DmaGroup.plan(cc_, (j % 3) * 2, GC.K)
                    ldg_at[j] = DmaGroup.plan(cc_, (j % 3) * 2 + 1, GC.K)
                    lds_at[j] = DmaGroup.plan(cs_, 3 + j % 3, GS.K)
                    st_at[j] = DmaGroup.plan(cs_, j % 3, GS.K)
                sc_at = [b_sca + j + 1 for j in range(NJ)]
                st4_at, g2_at, tl_at = {}, {}, {}
                vc2 = b_vec + 1
                for j in range(NJ + 1):
                    if j < NJ:
                        vc2 += 1
                        st4_at[j] = vc2
                    if j >= 1:
                        vc2 += 1
                        g2_at[j - 1] = vc2
                        vc2 += 1
                        tl_at[j - 1] = vc2

                def sl(j, k):
                    return TSL((j % 3) * 8 + k)

                @block.gpsimd
                def _(gp):
                    for j in range(NJ + 2):
                        if j < NJ:
                            i, t = j // NT, j % NT
                            src = (slice(i * 128, (i + 1) * 128),
                                   slice(t * TT, (t + 1) * TT))
                            if j >= 3:
                                S.wait(gp, "VEC", tl_at[j - 3])
                            k = gp.dma_start(out=sl(j, 0)[:],
                                             in_=ap["uq16"][src[0], src[1]])
                            GC.issue(k, (j % 3) * 2, ldu_at[j])
                            k = gp.dma_start(out=sl(j, 1)[:],
                                             in_=ap["gq16"][src[0], src[1]])
                            GC.issue(k, (j % 3) * 2 + 1, ldg_at[j])
                            k = gp.dma_start(out=sl(j, 2)[:],
                                             in_=ap["sq16"][src[0], src[1]])
                            GS.issue(k, 3 + j % 3, lds_at[j])
                        if j >= 2:
                            jj = j - 2
                            i, t = jj // NT, jj % NT
                            S.wait(gp, "VEC", g2_at[jj])
                            k = gp.dma_start(
                                out=ap["g2q16"][i * 128:(i + 1) * 128,
                                                t * TT:(t + 1) * TT],
                                in_=sl(jj, 4)[:])
                            GS.issue(k, jj % 3, st_at[jj])
                    GS.wait_all(gp)
                    GC.wait_all(gp)

                @bsca
                def _(scalar):
                    for j in range(NJ):
                        S.wait(scalar, "VEC", st4_at[j])
                        if j >= 3:
                            S.wait(scalar, "VEC", g2_at[j - 3])
                        scalar.activation(sl(j, 6), sl(j, 3), AF.Copy,
                                          scale=SCs("i4"), bias=MAGIC)
                        assert S.cinc(scalar, "SCA") == sc_at[j]

                @bvec
                def _(vector):
                    vector.tensor_tensor(SCs("s42"), SCs("s4"), SCs("s2"),
                                             AL.mult)
                    S.cinc(vector, "VEC")
                    for j in range(NJ + 1):
                        if j < NJ:
                            GC.wait(vector, ldg_at[j])
                            GS.wait(vector, lds_at[j])
                            if j >= 3:
                                S.wait(vector, "SCA", sc_at[j - 3])
                            vector.scalar_tensor_tensor(sl(j, 3), sl(j, 2),
                                                        SCs("s3c"), sl(j, 1),
                                                        AL.mult, AL.mult)
                            assert S.cinc(vector, "VEC") == st4_at[j]
                        if j >= 1:
                            jj = j - 1
                            S.wait(vector, "SCA", sc_at[jj])
                            if jj >= 3:
                                GS.wait(vector, st_at[jj - 3])
                            vector.tensor_scalar(sl(jj, 4), sl(jj, 6), MAGIC,
                                                     32767.0,
                                                     AL.subtract, AL.min)
                            assert S.cinc(vector, "VEC") == g2_at[jj]
                            GC.wait(vector, ldu_at[jj])
                            vector.scalar_tensor_tensor(sl(jj, 5), sl(jj, 4),
                                                            SCs("s42"),
                                                            sl(jj, 0),
                                                            AL.mult, AL.mult)
                            vector.tensor_reduce(M1[:, jj:jj + 1], sl(jj, 5),
                                                     AX, AL.max,
                                                     apply_absolute_value=True)
                            assert S.cinc(vector, "VEC") == tl_at[jj]

                my_barrier()
                emit_allreduce(ccb + 4, M1, NJ, "s5", "i5")

            if active():
                OBH = [AR_(0, 12288).bitcast(BF16),
                       AR_(12288, 24576).bitcast(BF16)]
                WDB = AR_(24576, 30720).bitcast(BF16)

                def obh(tb, ic):
                    return OBH[tb][:, ic * TT:(ic + 1) * TT]

                def obl(tb, ic):
                    return OBH[tb][:, 12288 + ic * TT:12288 + (ic + 1) * TT]

                def wdsl(j, plane):
                    base = (j % 2) * 6144 + plane * 3072
                    return WDB[:, base:base + 3072]
                b_vec, b_ten = S.cnt["VEC"], S.cnt["TEN"]
                cc_, cl, cs_ = (GC.plan_counters(), GL.plan_counters(),
                                GS.plan_counters())
                oldu_at = [[None] * IC for _ in range(NT)]
                oldg_at = [[None] * IC for _ in range(NT)]
                wld_at = [[None] * DC for _ in range(NT)]
                xo_at = [[None] * DC for _ in range(NT)]
                st_at = [[None] * DC for _ in range(NT)]
                for t in range(NT):
                    for ic in range(IC):
                        oldu_at[t][ic] = DmaGroup.plan(cc_, (ic % 2) * 2, GC.K)
                        oldg_at[t][ic] = DmaGroup.plan(cc_, (ic % 2) * 2 + 1, GC.K)
                    for h in range(DC):
                        wld_at[t][h] = (
                            DmaGroup.plan(cl, 2 + (t * DC + h) % 2, GL.K),
                            DmaGroup.plan(cl, 4 + (t * DC + h) % 2, GL.K))
                        xo_at[t][h] = DmaGroup.plan(cl, 6 + h % 2, GL.K)
                        st_at[t][h] = DmaGroup.plan(cs_, h % 2, GS.K)
                mm_at = [[b_ten + t * DC + h + 1 for h in range(DC)]
                         for t in range(NT)]
                ov_at = [[0] * IC for _ in range(NT)]
                xn_at = [[0] * DC for _ in range(NT)]
                vp = b_vec + 1
                for t in range(NT + 1):
                    if t < NT:
                        for ic in range(IC):
                            vp += 1
                            ov_at[t][ic] = vp
                    if t >= 1:
                        for h in range(DC):
                            vp += 1
                            xn_at[t - 1][h] = vp

                def slo(ic, k):
                    return TSL(16 + (ic % 2) * 2 + k)

                @block.gpsimd
                def _(gp):
                    for t in range(NT + 1):
                        if t < NT:
                            if t >= 2:
                                S.wait(gp, "TEN", mm_at[t - 2][DC - 1])
                            for ic in range(IC):
                                if t * IC + ic >= 2:
                                    pj = t * IC + ic - 2
                                    S.wait(gp, "VEC", ov_at[pj // IC][pj % IC])
                                src = (slice(ic * 128, (ic + 1) * 128),
                                       slice(t * TT, (t + 1) * TT))
                                k = gp.dma_start(out=slo(ic, 0)[:],
                                                 in_=ap["uq16"][src[0], src[1]])
                                GC.issue(k, (ic % 2) * 2, oldu_at[t][ic])
                                k = gp.dma_start(out=slo(ic, 1)[:],
                                                 in_=ap["g2q16"][src[0], src[1]])
                                GC.issue(k, (ic % 2) * 2 + 1, oldg_at[t][ic])
                        if t >= 1:
                            for h in range(DC):
                                S.wait(gp, "VEC", xn_at[t - 1][h])
                                k = gp.dma_start(
                                    out=ap[x_dst][h * 128:(h + 1) * 128,
                                                  (t - 1) * TT:t * TT],
                                    in_=TSL(14 + h % 2)[:])
                                GS.issue(k, h % 2, st_at[t - 1][h])
                    GS.wait_all(gp)

                @bvec
                def _(vector):
                    vector.tensor_scalar(SCs("h5"), SCs("s5"), 0.5, None,
                                         AL.mult)
                    S.cinc(vector, "VEC")
                    for t in range(NT + 1):
                        if t < NT:
                            for ic in range(IC):
                                GC.wait(vector, oldu_at[t][ic])
                                GC.wait(vector, oldg_at[t][ic])
                                s0, s1 = slo(ic, 0), slo(ic, 1)
                                tmp = TSL(22)
                                tmp2 = TSL(23)
                                vector.scalar_tensor_tensor(
                                    tmp, s1, SCs("s42"), s0, AL.mult, AL.mult)
                                vector.tensor_scalar(tmp, tmp, SCs("i5"), MAGIC,
                                                     AL.mult, AL.add)
                                vector.tensor_scalar(tmp2, tmp, MAGIC, 32767.0,
                                                     AL.subtract, AL.min)
                                vector.tensor_copy(obh(t % 2, ic), tmp2)
                                vector.tensor_tensor(obl(t % 2, ic), tmp2,
                                                     obh(t % 2, ic),
                                                     AL.subtract)
                                assert S.cinc(vector, "VEC") == ov_at[t][ic]
                        if t >= 1:
                            for h in range(DC):
                                j = (t - 1) * DC + h
                                S.wait(vector, "TEN", mm_at[t - 1][h])
                                GL.wait(vector, xo_at[t - 1][h])
                                if j >= 2:
                                    GS.wait(vector,
                                            st_at[(j - 2) // DC][(j - 2) % DC])
                                vector.scalar_tensor_tensor(
                                    TSL(14 + h % 2), PS[4 + j % 2][:, 0:TT],
                                    SCs("h5"), TSL(12 + h % 2), AL.mult, AL.add)
                                assert S.cinc(vector, "VEC") == xn_at[t - 1][h]

                @block.sync
                def _(sync):
                    for t in range(NT):
                        for h in range(DC):
                            j = t * DC + h
                            if j >= 2:
                                S.wait(sync, "TEN",
                                       mm_at[(j - 2) // DC][(j - 2) % DC])
                            k = ld3(sync, wdsl(j, 0),
                                    ap["wdTh"][:, h * 128:(h + 1) * 128])
                            GL.issue(k, 2 + j % 2, wld_at[t][h][0])
                            k = ld3(sync, wdsl(j, 1),
                                    ap["wdTl"][:, h * 128:(h + 1) * 128])
                            GL.issue(k, 4 + j % 2, wld_at[t][h][1])
                            if j >= 2:
                                S.wait(sync, "VEC",
                                       xn_at[(j - 2) // DC][(j - 2) % DC])
                            k = sync.dma_start(
                                out=TSL(12 + h % 2)[:],
                                in_=ap[x_src][h * 128:(h + 1) * 128,
                                              t * TT:(t + 1) * TT])
                            GL.issue(k, 6 + h % 2, xo_at[t][h])

                @block.tensor
                def _(tensor):
                    for t in range(NT):
                        S.wait(tensor, "VEC", ov_at[t][IC - 1])
                        for h in range(DC):
                            j = t * DC + h
                            GL.wait(tensor, wld_at[t][h][0])
                            GL.wait(tensor, wld_at[t][h][1])
                            if j >= 2:
                                S.wait(tensor, "VEC",
                                       xn_at[(j - 2) // DC][(j - 2) % DC])
                            for ic in range(IC):
                                wh = wdsl(j, 0)[:, ic * 128:(ic + 1) * 128]
                                wl = wdsl(j, 1)[:, ic * 128:(ic + 1) * 128]
                                tensor.matmul(PS[4 + j % 2][:, 0:TT], wh,
                                              obh(t % 2, ic),
                                              start=(ic == 0), stop=False)
                                tensor.matmul(PS[4 + j % 2][:, 0:TT], wh,
                                              obl(t % 2, ic),
                                              start=False, stop=False)
                                k = tensor.matmul(PS[4 + j % 2][:, 0:TT], wl,
                                                  obh(t % 2, ic), start=False,
                                                  stop=(ic == IC - 1))
                            assert S.inc(k, "TEN") == mm_at[t][h]

                my_barrier()

        def emit_a2_tile(t, x_src):
            XT0 = POOL[:, 0:4096]
            SQB = POOL[:, 8192:12288]
            LH = ARENA[:, 8192:10240].bitcast(BF16)
            LL = ARENA[:, 10240:12288].bitcast(BF16)
            QB = AR_(12288, 16384)
            KB = AR_(16384, 20480)
            WVB = AR_(20480, 24576).bitcast(BF16)
            WQK = POOL[:, 4096:8192].bitcast(BF16)
            T2B = XT0
            T1B = SQB
            VO = [TSL(0), TSL(1)]

            def lh_c(c):
                return LH[:, c * TT:(c + 1) * TT]

            def ll_c(c):
                return LL[:, c * TT:(c + 1) * TT]

            b = {n: S.cnt[n] for n in ("TEN", "SCA", "VEC")}
            cl = GL.plan_counters()
            ld_x = DmaGroup.plan(cl, t % 2, GL.K)
            sq_d = b["SCA"] + 1
            sd_d = b["SCA"] + 2
            st_d = b["TEN"] + 1
            bc_d = b["TEN"] + 2
            row_d = b["VEC"] + 1
            rr_d = b["VEC"] + 2
            ln_d = b["VEC"] + 2 + DC

            @block.sync
            def _(sync):
                i = ld3(sync, XT0[:], ap[x_src][:, t * TT:(t + 1) * TT])
                GL.issue(i, t % 2, ld_x)

            @bsca
            def _(scalar):
                GL.wait(scalar, ld_x)
                scalar.activation(SQB[:], XT0[:], AF.Square)
                assert S.cinc(scalar, "SCA") == sq_d
                S.wait(scalar, "VEC", row_d)
                scalar.activation(sd_row, var_row, AF.Sqrt,
                                  bias=EPS[0:1, 0:1], scale=1.0)
                assert S.cinc(scalar, "SCA") == sd_d

            @block.tensor
            def _(tensor):
                GL.wait(tensor, ld_x)
                S.wait(tensor, "SCA", sq_d)
                for c in range(DC):
                    tensor.matmul(PS[0][0:1, 0:TT], CST["ones128"][:],
                                  XT0[:, c * TT:(c + 1) * TT],
                                  start=(c == 0), stop=(c == DC - 1))
                for c in range(DC):
                    i = tensor.matmul(PS[1][0:1, 0:TT], CST["ones128"][:],
                                      SQB[:, c * TT:(c + 1) * TT],
                                      start=(c == 0), stop=(c == DC - 1))
                assert S.inc(i, "TEN") == st_d
                S.wait(tensor, "VEC", rr_d)
                tensor.matmul(PS[2][:, 0:TT], CST["ones1x"][:],
                              murow, start=True, stop=True)
                i = tensor.matmul(PS[3][:, 0:TT], CST["ones1x"][:],
                                  rrow, start=True, stop=True)
                assert S.inc(i, "TEN") == bc_d

            @bvec
            def _(vector):
                S.wait(vector, "TEN", st_d)
                vector.tensor_scalar(murow, PS[0][0:1, 0:TT],
                                     1.0 / float(D), None, AL.mult)
                vector.tensor_scalar(var_row, PS[1][0:1, 0:TT],
                                     1.0 / float(D), None, AL.mult)
                vector.tensor_tensor(sd_row, murow, murow, AL.mult)
                vector.tensor_tensor(var_row, var_row, sd_row, AL.subtract)
                assert S.cinc(vector, "VEC") == row_d
                S.wait(vector, "SCA", sd_d)
                vector.reciprocal(rrow, sd_row)
                assert S.cinc(vector, "VEC") == rr_d
                S.wait(vector, "TEN", bc_d)
                tmp = POOL[:, 8192:8704]
                tmp2 = POOL[:, 8704:9216]
                for c in range(DC):
                    vector.tensor_tensor(tmp, XT0[:, c * TT:(c + 1) * TT],
                                         PS[2][:, 0:TT], AL.subtract)
                    vector.tensor_tensor(tmp2, tmp, PS[3][:, 0:TT], AL.mult)
                    vector.tensor_copy(lh_c(c), tmp2)
                    vector.tensor_tensor(ll_c(c), tmp2, lh_c(c), AL.subtract)
                    S.cinc(vector, "VEC")
                assert S.cnt["VEC"] == ln_d

            b2 = {n: S.cnt[n] for n in ("TEN", "SCA", "VEC")}
            cl2, cs2 = GL.plan_counters(), GS.plan_counters()
            NU = 16
            wld = [None] * NU
            for u in range(NU):
                wld[u] = (DmaGroup.plan(cl2, 4 + (u % 2) * 2, GL.K),
                          DmaGroup.plan(cl2, 5 + (u % 2) * 2, GL.K))
            pmm_at = [b2["TEN"] + u + 1 for u in range(NU)]
            pev_at = [b2["SCA"] + u + 1 for u in range(NU)]

            def wqk_h(u):
                return WQK[:, (u % 2) * 4096:(u % 2) * 4096 + 1024]

            def wqk_l(u):
                return WQK[:, (u % 2) * 4096 + 1024:(u % 2) * 4096 + 2048]

            @block.sync
            def _(sync):
                for u in range(NU):
                    s_, o = u // 8, u % 8
                    nm = ("wqT", "wkT")[s_]
                    if u >= 2:
                        S.wait(sync, "TEN", pmm_at[u - 2])
                    i = ld3(sync, wqk_h(u),
                            ap[nm + "h"][:, o * 128:(o + 1) * 128])
                    GL.issue(i, 4 + (u % 2) * 2, wld[u][0])
                    i = ld3(sync, wqk_l(u),
                            ap[nm + "l"][:, o * 128:(o + 1) * 128])
                    GL.issue(i, 5 + (u % 2) * 2, wld[u][1])

            @block.tensor
            def _(tensor):
                S.wait(tensor, "VEC", ln_d)
                for u in range(NU):
                    s_, o = u // 8, u % 8
                    GL.wait(tensor, wld[u][0])
                    GL.wait(tensor, wld[u][1])
                    if u >= 2:
                        S.wait(tensor, "SCA", pev_at[u - 2])
                    for c in range(DC):
                        wh = wqk_h(u)[:, c * 128:(c + 1) * 128]
                        wl = wqk_l(u)[:, c * 128:(c + 1) * 128]
                        tensor.matmul(PS[u % 2][:, 0:TT], wh, lh_c(c),
                                      start=(c == 0), stop=False)
                        tensor.matmul(PS[u % 2][:, 0:TT], wh, ll_c(c),
                                      start=False, stop=False)
                        i = tensor.matmul(PS[u % 2][:, 0:TT], wl, lh_c(c),
                                          start=False, stop=(c == DC - 1))
                    assert S.inc(i, "TEN") == pmm_at[u]

            @bsca
            def _(scalar):
                for u in range(NU):
                    s_, o = u // 8, u % 8
                    S.wait(scalar, "TEN", pmm_at[u])
                    buf = QB if s_ == 0 else KB
                    scalar.copy(buf[:, o * TT:(o + 1) * TT], PS[u % 2][:, 0:TT])
                    assert S.cinc(scalar, "SCA") == pev_at[u]

            b3 = {n: S.cnt[n] for n in ("TEN", "SCA")}
            cl3, cs3 = GL.plan_counters(), GS.plan_counters()
            wvld = [(DmaGroup.plan(cl3, 2, GL.K), DmaGroup.plan(cl3, 3, GL.K))
                    for _ in range(2)]
            vst = [DmaGroup.plan(cs3, 2 + u % 2, GS.K) for u in range(8)]
            vmm = [b3["TEN"] + u + 1 for u in range(8)]
            vev = [b3["SCA"] + u + 1 for u in range(8)]

            @block.sync
            def _(sync):
                for o2 in range(2):
                    if o2 >= 1:
                        S.wait(sync, "TEN", vmm[3])
                    i = ld3(sync, WVB[:, 0:4096],
                            ap["wvTh"][:, o2 * TT:(o2 + 1) * TT])
                    GL.issue(i, 2, wvld[o2][0])
                    i = ld3(sync, WVB[:, 4096:8192],
                            ap["wvTl"][:, o2 * TT:(o2 + 1) * TT])
                    GL.issue(i, 3, wvld[o2][1])

            @block.tensor
            def _(tensor):
                for u in range(8):
                    o2, ts4 = u // 4, u % 4
                    GL.wait(tensor, wvld[o2][0])
                    GL.wait(tensor, wvld[o2][1])
                    if u < 2:
                        S.wait(tensor, "SCA", pev_at[NU - 2 + u])
                    else:
                        S.wait(tensor, "SCA", vev[u - 2])
                    for c in range(DC):
                        xh = LH[:, c * TT + ts4 * 128:c * TT + (ts4 + 1) * 128]
                        xl = LL[:, c * TT + ts4 * 128:c * TT + (ts4 + 1) * 128]
                        whm = WVB[:, c * 512:(c + 1) * 512]
                        wlm = WVB[:, 4096 + c * 512:4096 + (c + 1) * 512]
                        tensor.matmul(PS[u % 2][:, 0:TT], xh, whm,
                                      start=(c == 0), stop=False)
                        tensor.matmul(PS[u % 2][:, 0:TT], xl, whm,
                                      start=False, stop=False)
                        i = tensor.matmul(PS[u % 2][:, 0:TT], xh, wlm,
                                          start=False, stop=(c == DC - 1))
                    assert S.inc(i, "TEN") == vmm[u]

            @bsca
            def _(scalar):
                for u in range(8):
                    S.wait(scalar, "TEN", vmm[u])
                    if u >= 2:
                        GS.wait(scalar, vst[u - 2])
                    scalar.copy(VO[u % 2][:], PS[u % 2][:, 0:TT])
                    assert S.cinc(scalar, "SCA") == vev[u]

            @block.gpsimd
            def _(gp):
                for u in range(8):
                    o2, ts4 = u // 4, u % 4
                    S.wait(gp, "SCA", vev[u])
                    r0 = t * TT + ts4 * 128
                    i = gp.dma_start(
                        out=ap["vtm"][r0:r0 + 128, o2 * TT:(o2 + 1) * TT],
                        in_=VO[u % 2][:])
                    GS.issue(i, 2 + u % 2, vst[u])
                GS.wait_all(gp)

            rsl = slice(t * TT, (t + 1) * TT)
            prev_vqr = [None]
            for s_, buf in ((0, QB), (1, KB)):
                b4 = {n: S.cnt[n] for n in ("TEN", "SCA", "VEC")}
                cs4 = GS.plan_counters()
                sq2_d = b4["SCA"] + 1
                sd4 = [b4["SCA"] + 1 + c + 1 for c in range(DC)]
                rmm, rbb, pmm2 = {}, {}, {}
                tc = b4["TEN"]
                for st_ in range(DC + 2):
                    if st_ < DC:
                        tc += 1
                        rmm[st_] = tc
                    if 1 <= st_ <= DC:
                        tc += 1
                        rbb[st_ - 1] = tc
                    if st_ >= 2:
                        tc += 1
                        pmm2[st_ - 2] = tc
                vme, vrc, vqn, vt2 = {}, {}, {}, {}
                vc = b4["VEC"]
                for st_ in range(DC + 2):
                    if st_ < DC:
                        vc += 1
                        vme[st_] = vc
                        vc += 1
                        vrc[st_] = vc
                    if 1 <= st_ <= DC:
                        vc += 1
                        vqn[st_ - 1] = vc
                    if st_ >= 2:
                        vc += 1
                        vt2[st_ - 2] = vc
                vt1 = vc + 1
                vqr = vc + 2
                stq = DmaGroup.plan(cs4, s_, GS.K)

                def rslot(c):
                    return RMSW[0:2, (c % 4) * 512:(c % 4 + 1) * 512]

                @bsca
                def _(scalar, s_=s_, buf=buf, sq2_d=sq2_d, sd4=sd4,
                      vme=vme, pq=prev_vqr[0], rslot=rslot):
                    if s_ == 0:
                        GS.wait(scalar, vst[6])
                        GS.wait(scalar, vst[7])
                    else:
                        S.wait(scalar, "VEC", pq)
                    scalar.activation(T2B[:], buf[:], AF.Square)
                    assert S.cinc(scalar, "SCA") == sq2_d
                    for c in range(DC):
                        S.wait(scalar, "VEC", vme[c])
                        scalar.activation(rslot(c), rslot(c), AF.Sqrt,
                                          bias=EPS[0:2, 1:2], scale=1.0)
                        assert S.cinc(scalar, "SCA") == sd4[c]

                @block.tensor
                def _(tensor, s_=s_, buf=buf, sq2_d=sq2_d, sd4=sd4, rmm=rmm,
                      rbb=rbb, pmm2=pmm2, vme=vme, vrc=vrc, vqn=vqn, vt2=vt2,
                      rslot=rslot):
                    for st_ in range(DC + 2):
                        if st_ < DC:
                            c = st_
                            if c == 0:
                                S.wait(tensor, "SCA", sq2_d)
                            if c >= 4:
                                S.wait(tensor, "VEC", vt2[c - 4])
                            i = tensor.matmul(PS[4 + c % 4][0:2, 0:TT],
                                              CST["rmsmask"][:],
                                              T2B[:, c * TT:(c + 1) * TT],
                                              start=True, stop=True)
                            assert S.inc(i, "TEN") == rmm[c]
                        if 1 <= st_ <= DC:
                            c = st_ - 1
                            S.wait(tensor, "VEC", vrc[c])
                            if c >= 4:
                                S.wait(tensor, "VEC", vqn[c - 4])
                            stat = (GBC[0:2, c * 128:(c + 1) * 128]
                                    if s_ == 0 else CST["rmsbc"][:])
                            i = tensor.matmul(PS[c % 4][:, 0:TT], stat,
                                              rslot(c), start=True, stop=True)
                            assert S.inc(i, "TEN") == rbb[c]
                        if st_ >= 2:
                            c = st_ - 2
                            S.wait(tensor, "VEC", vqn[c])
                            i = tensor.matmul(PS[4 + c % 4][:, 0:TT],
                                              CST["perm32"][:],
                                              buf[:, c * TT:(c + 1) * TT],
                                              start=True, stop=True)
                            assert S.inc(i, "TEN") == pmm2[c]

                @bvec
                def _(vector, s_=s_, buf=buf, rmm=rmm, rbb=rbb, pmm2=pmm2,
                      vme=vme, vrc=vrc, vqn=vqn, vt2=vt2, vt1=vt1, vqr=vqr,
                      sd4=sd4, rslot=rslot):
                    for st_ in range(DC + 2):
                        if st_ < DC:
                            c = st_
                            S.wait(vector, "TEN", rmm[c])
                            vector.tensor_scalar(rslot(c),
                                                 PS[4 + c % 4][0:2, 0:TT],
                                                 1.0 / float(HD), None,
                                                 AL.mult)
                            assert S.cinc(vector, "VEC") == vme[c]
                            S.wait(vector, "SCA", sd4[c])
                            vector.reciprocal(rslot(c), rslot(c))
                            assert S.cinc(vector, "VEC") == vrc[c]
                        if 1 <= st_ <= DC:
                            c = st_ - 1
                            S.wait(vector, "TEN", rbb[c])
                            vector.tensor_tensor(buf[:, c * TT:(c + 1) * TT],
                                                 buf[:, c * TT:(c + 1) * TT],
                                                 PS[c % 4][:, 0:TT], AL.mult)
                            assert S.cinc(vector, "VEC") == vqn[c]
                        if st_ >= 2:
                            c = st_ - 2
                            S.wait(vector, "TEN", pmm2[c])
                            vector.tensor_tensor(T2B[:, c * TT:(c + 1) * TT],
                                                 PS[4 + c % 4][:, 0:TT],
                                                 ROPE_B[:, rsl], AL.mult)
                            assert S.cinc(vector, "VEC") == vt2[c]
                    rab = ROPE_A[:, rsl].rearrange(
                        "p (o k) -> p o k", o=1).broadcast_to([128, DC, TT])
                    vector.tensor_tensor(
                        T1B[:].rearrange("p (c k) -> p c k", c=DC),
                        buf[:].rearrange("p (c k) -> p c k", c=DC),
                        rab, AL.mult)
                    assert S.cinc(vector, "VEC") == vt1
                    vector.tensor_tensor(buf[:], T1B[:], T2B[:], AL.add)
                    assert S.cinc(vector, "VEC") == vqr

                prev_vqr[0] = vqr

                @block.gpsimd
                def _(gp, s_=s_, buf=buf, vqr=vqr, stq=stq):
                    S.wait(gp, "VEC", vqr)
                    i = gp.dma_start(
                        out=ap[("qrot", "krot")[s_]][:, t * TT:(t + 1) * TT]
                            .rearrange("(c p) k -> p c k", p=128),
                        in_=buf[:].rearrange("p (c k) -> p c k", c=DC))
                    GS.issue(i, s_, stq)
                    GS.wait(gp, stq)

            my_barrier()

        def emit_a3_wp(wp, x_src, x_dst):
            Q = AR_(8192, 12288)
            K_ = AR_(12288, 16384)
            V4 = AR_(16384, 20480)
            Y = AR_(20480, 24576)
            YH = V4[:, 0:2048].bitcast(BF16)
            YL = V4[:, 2048:4096].bitcast(BF16)
            WOB = POOL[:, 4096:6144].bitcast(BF16)
            XOL = [TSL(12), TSL(13)]
            SM3 = [TSL(16), TSL(17), TSL(0)]
            EN3 = [TSL(20), TSL(21), TSL(22)]
            ET3 = [[TSL(x)[:, ks * WIN:(ks + 1) * WIN] for ks in range(2)]
                   for x in (18, 19, 23)]

            cl, cs_ = GL.plan_counters(), GS.plan_counters()
            ldq = DmaGroup.plan(cl, 0, GL.K)
            ldk = DmaGroup.plan(cl, 1, GL.K)
            ldv = DmaGroup.plan(cl, 2, GL.K)
            wo_at = [(DmaGroup.plan(cl, 4 + h % 2, GL.K),
                      DmaGroup.plan(cl, 6 + h % 2, GL.K)) for h in range(DC)]
            cgc = GC.plan_counters()
            xo_at = [DmaGroup.plan(cgc, h % 2, GC.K) for h in range(DC)]
            st_at = [DmaGroup.plan(cs_, h % 2, GS.K) for h in range(DC)]

            b = {n: S.cnt[n] for n in ("TEN", "SCA", "VEC")}
            units = [(w, h) for w in range(2) for h in range(NH)]
            NU = len(units)
            sc_at, tp_at, pv_at = {}, {}, {}
            ms_at, nm_at, ex_at, et_at, ye_at = {}, {}, {}, {}, {}
            tc = b["TEN"]
            for u in range(3):
                for qs in range(2):
                    tc += 1
                    sc_at[units[u] + (qs,)] = tc
            for u in range(NU):
                if u + 3 < NU:
                    for qs in range(2):
                        tc += 1
                        sc_at[units[u + 3] + (qs,)] = tc
                for qs in range(2):
                    for ks in range(2):
                        tc += 1
                        tp_at[units[u] + (qs, ks)] = tc
                tc += 1
                pv_at[units[u]] = tc
            womm_at = [tc + h + 1 for h in range(DC)]
            vc2 = b["VEC"]
            for u in range(NU + 1):
                if u < NU:
                    for qs in range(2):
                        vc2 += 1
                        ms_at[units[u] + (qs,)] = vc2
                if u >= 1:
                    for qs in range(2):
                        vc2 += 1
                        nm_at[units[u - 1] + (qs,)] = vc2
            ysp = vc2 + 1
            vc2 += 1
            xn_at = [vc2 + h + 1 for h in range(DC)]
            sc2 = b["SCA"]
            for u in range(NU + 1):
                if u < NU:
                    for qs in range(2):
                        sc2 += 1
                        ex_at[units[u] + (qs,)] = sc2
                if u >= 1:
                    w_, h_ = units[u - 1]
                    sc2 += 1
                    et_at[(w_, h_)] = sc2
                    if h_ % 2 == 1:
                        sc2 += 1
                        ye_at[(w_, h_ // 2)] = sc2

            def scb(u):
                return PS[u % 3]

            def tpb(u):
                return PS[3 + u % 3]

            @block.sync
            def _(sync):
                i = ld3(sync, Q[:], ap["qrot"][:, wp * TT:(wp + 1) * TT])
                GL.issue(i, 0, ldq)
                i = ld3(sync, K_[:], ap["krot"][:, wp * TT:(wp + 1) * TT])
                GL.issue(i, 1, ldk)
                i = ld3(sync, V4[:], ap["vtm"][wp * TT:(wp + 1) * TT, :])
                GL.issue(i, 2, ldv)
                for h in range(DC):
                    if h >= 2:
                        S.wait(sync, "TEN", womm_at[h - 2])
                    i = ld3(sync, WOB[:, (h % 2) * 2048:(h % 2) * 2048 + 1024],
                            ap["woTh"][:, h * 128:(h + 1) * 128])
                    GL.issue(i, 4 + h % 2, wo_at[h][0])
                    i = ld3(sync, WOB[:, (h % 2) * 2048 + 1024:(h % 2) * 2048 + 2048],
                            ap["woTl"][:, h * 128:(h + 1) * 128])
                    GL.issue(i, 6 + h % 2, wo_at[h][1])

            def emit_sc(tensor, u):
                w, h = units[u]
                hc, off = h // 2, (h % 2) * 64
                kw = K_[off:off + 64,
                        hc * TT + w * WIN:hc * TT + (w + 1) * WIN]
                if u >= 3:
                    S.wait(tensor, "VEC", ms_at[units[u - 3] + (1,)])
                for qs in range(2):
                    q0 = hc * TT + w * WIN + qs * 128
                    i = tensor.matmul(scb(u)[:, qs * WIN:(qs + 1) * WIN],
                                      Q[off:off + 64, q0:q0 + 128],
                                      kw, start=True, stop=True)
                    assert S.inc(i, "TEN") == sc_at[(w, h, qs)]

            @block.tensor
            def _(tensor):
                GL.wait(tensor, ldq)
                GL.wait(tensor, ldk)
                GL.wait(tensor, ldv)
                for u in range(3):
                    emit_sc(tensor, u)
                for u, (w, h) in enumerate(units):
                    if u + 3 < NU:
                        emit_sc(tensor, u + 3)
                    if u >= 3:
                        S.wait(tensor, "SCA", et_at[units[u - 3]])
                    for qs in range(2):
                        S.wait(tensor, "VEC", nm_at[(w, h, qs)])
                        for ks in range(2):
                            i = tensor.transpose(
                                tpb(u)[:, ks * WIN + qs * 128:
                                       ks * WIN + (qs + 1) * 128],
                                EN3[u % 3][:, qs * WIN + ks * 128:
                                           qs * WIN + (ks + 1) * 128],
                                CST["ident"][:])
                            assert S.inc(i, "TEN") == tp_at[(w, h, qs, ks)]
                    S.wait(tensor, "SCA", et_at[(w, h)])
                    if h >= 2:
                        S.wait(tensor, "SCA", ye_at[(w, h // 2 - 1)])
                    off_y = (h % 2) * 64
                    for ks in range(2):
                        s4 = (w * 2 + ks) * 1024
                        i = tensor.matmul(
                            PS[6 + w][off_y:off_y + 64, 0:WIN],
                            V4[:, s4 + h * 64:s4 + (h + 1) * 64],
                            ET3[u % 3][ks][:, 0:WIN],
                            start=(ks == 0), stop=(ks == 1))
                    assert S.inc(i, "TEN") == pv_at[(w, h)]
                S.wait(tensor, "VEC", ysp)
                for h in range(DC):
                    GL.wait(tensor, wo_at[h][0])
                    GL.wait(tensor, wo_at[h][1])
                    if h >= 2:
                        S.wait(tensor, "VEC", xn_at[h - 2])
                    for c in range(DC):
                        wh = WOB[:, (h % 2) * 2048 + c * 128:
                                 (h % 2) * 2048 + (c + 1) * 128]
                        wl = WOB[:, (h % 2) * 2048 + 1024 + c * 128:
                                 (h % 2) * 2048 + 1024 + (c + 1) * 128]
                        yh = YH[:, c * TT:(c + 1) * TT]
                        yl = YL[:, c * TT:(c + 1) * TT]
                        tensor.matmul(PS[h % 2][:, 0:TT], wh, yh,
                                      start=(c == 0), stop=False)
                        tensor.matmul(PS[h % 2][:, 0:TT], wh, yl,
                                      start=False, stop=False)
                        i = tensor.matmul(PS[h % 2][:, 0:TT], wl, yh,
                                          start=False, stop=(c == DC - 1))
                    assert S.inc(i, "TEN") == womm_at[h]

            @bvec
            def _(vector):
                for u in range(NU + 1):
                    if u < NU:
                        w, h = units[u]
                        for qs in range(2):
                            if qs == 0:
                                S.wait(vector, "TEN", sc_at[(w, h, 1)])
                                if u >= 3:
                                    S.wait(vector, "SCA",
                                           ex_at[units[u - 3] + (1,)])
                            sm = SM3[u % 3][:, qs * WIN:(qs + 1) * WIN]
                            vector.scalar_tensor_tensor(
                                sm, scb(u)[:, qs * WIN:(qs + 1) * WIN], 0.125,
                                CST[f"maskq{qs}"][:], AL.mult, AL.min)
                            vector.tensor_reduce(
                                NMC[:, (u % 3) * 2 + qs:(u % 3) * 2 + qs + 1],
                                sm, AX, AL.max, negate=True)
                            assert S.cinc(vector, "VEC") == ms_at[(w, h, qs)]
                    if u >= 1:
                        w, h = units[u - 1]
                        l = (u - 1) % 3
                        en_f = EN3[l]
                        for qs in range(2):
                            S.wait(vector, "SCA", ex_at[(w, h, qs)])
                            en = en_f[:, qs * WIN:(qs + 1) * WIN]
                            vector.reciprocal(
                                NMC[:, 12 + l * 2 + qs:13 + l * 2 + qs],
                                NMC[:, 6 + l * 2 + qs:7 + l * 2 + qs])
                            vector.tensor_scalar(
                                en, en,
                                NMC[:, 12 + l * 2 + qs:13 + l * 2 + qs],
                                None, AL.mult)
                            assert S.cinc(vector, "VEC") == nm_at[(w, h, qs)]
                S.wait(vector, "SCA", ye_at[(1, NH // 2 - 1)])
                vector.tensor_copy(YH[:], Y[:])
                vector.tensor_tensor(YL[:], Y[:], YH[:], AL.subtract)
                assert S.cinc(vector, "VEC") == ysp
                for h in range(DC):
                    S.wait(vector, "TEN", womm_at[h])
                    GC.wait(vector, xo_at[h])
                    if h >= 2:
                        GS.wait(vector, st_at[h - 2])
                    vector.tensor_tensor(TSL(14 + h % 2)[:],
                                         PS[h % 2][:, 0:TT],
                                         XOL[h % 2][:], AL.add)
                    assert S.cinc(vector, "VEC") == xn_at[h]

            @bsca
            def _(scalar):
                for u in range(NU + 1):
                    if u < NU:
                        w, h = units[u]
                        l = u % 3
                        for qs in range(2):
                            S.wait(scalar, "VEC", ms_at[(w, h, qs)])
                            if qs == 0 and u >= 3:
                                S.wait(scalar, "TEN",
                                       tp_at[units[u - 3] + (1, 1)])
                            scalar.activation(
                                EN3[l][:, qs * WIN:(qs + 1) * WIN],
                                SM3[l][:, qs * WIN:(qs + 1) * WIN], AF.Exp,
                                bias=NMC[:, l * 2 + qs:l * 2 + qs + 1],
                                scale=1.0,
                                accum_out=NMC[:, 6 + l * 2 + qs:
                                              7 + l * 2 + qs])
                            assert S.cinc(scalar, "SCA") == ex_at[(w, h, qs)]
                    if u >= 1:
                        w, h = units[u - 1]
                        lp = (u - 1) % 3
                        S.wait(scalar, "TEN", tp_at[(w, h, 1, 1)])
                        scalar.copy(ET3[lp][0][:, 0:WIN],
                                    tpb(u - 1)[:, 0:WIN])
                        scalar.copy(ET3[lp][1][:, 0:WIN],
                                    tpb(u - 1)[:, WIN:2 * WIN])
                        assert S.cinc(scalar, "SCA") == et_at[(w, h)]
                        if h % 2 == 1:
                            S.wait(scalar, "TEN", pv_at[(w, h)])
                            hp = h // 2
                            scalar.copy(Y[:, hp * TT + w * WIN:
                                          hp * TT + (w + 1) * WIN],
                                        PS[6 + w][:, 0:WIN])
                            assert S.cinc(scalar, "SCA") == ye_at[(w, hp)]

            @block.gpsimd
            def _(gp):
                for h in range(DC + 1):
                    if h < DC:
                        if h >= 2:
                            S.wait(gp, "VEC", xn_at[h - 2])
                        i = gp.dma_start(
                            out=XOL[h % 2][:],
                            in_=ap[x_src][h * 128:(h + 1) * 128,
                                          wp * TT:(wp + 1) * TT])
                        GC.issue(i, h % 2, xo_at[h])
                    if h >= 1:
                        S.wait(gp, "VEC", xn_at[h - 1])
                        i = gp.dma_start(
                            out=ap[x_dst][(h - 1) * 128:h * 128,
                                          wp * TT:(wp + 1) * TT],
                            in_=TSL(14 + (h - 1) % 2)[:])
                        GS.issue(i, (h - 1) % 2, st_at[h - 1])
                GS.wait_all(gp)

            my_barrier()

        def emit_attn(x_src, x_dst):
            if active():
                cl = GL.plan_counters()
                la = DmaGroup.plan(cl, 0, GL.K)
                lb = DmaGroup.plan(cl, 1, GL.K)

                @block.sync
                def _(sync):
                    i = sync.dma_start(out=ROPE_A[:], in_=ap["cropeA"])
                    GL.issue(i, 0, la)
                    i = sync.dma_start(out=ROPE_B[:], in_=ap["cropeB"])
                    GL.issue(i, 1, lb)
                    GL.wait(sync, la)
                    GL.wait(sync, lb)

                @bvec
                def _(vector):
                    for c in range(DC):
                        vector.tensor_scalar(GBC[0:2, c * 128:(c + 1) * 128],
                                             CST["rmsbc"][0:2, 0:128],
                                             QG[0:2, c:c + 1], None, AL.mult)
                        S.cinc(vector, "VEC")

                my_barrier()
                for t in range(NT):
                    emit_a2_tile(t, x_src)
            if active():
                for wp in range(NT):
                    emit_a3_wp(wp, x_src, x_dst)

        emit_ffn("x", "xmid", 0)
        emit_attn("xmid", "xmid2")
        emit_ffn("xmid2", "y", 5)

        @block.gpsimd
        def _(gp):
            GS.wait_all(gp)

    return nc


def kernel(**inputs):
    x = np.ascontiguousarray(inputs["L0_attn_x"])
    per_core = []
    for b in range(N_CORES):
        m = {
            "x": np.ascontiguousarray(x[b].T),
            "wq": np.ascontiguousarray(inputs["L0_attn_wq"]),
            "wk": np.ascontiguousarray(inputs["L0_attn_wk"]),
            "wv": np.ascontiguousarray(inputs["L0_attn_wv"]),
            "wo": np.ascontiguousarray(inputs["L0_attn_wo"]),
            "wg": np.ascontiguousarray(inputs["L0_ffn_w_gate"]),
            "wu": np.ascontiguousarray(inputs["L0_ffn_w_up"]),
            "wd": np.ascontiguousarray(inputs["L0_ffn_w_down"]),
            "qg": np.ascontiguousarray(
                inputs["L0_attn_q_gain"].reshape(NH // 2, 2).T),
        }
        for n, v in _CONSTS.items():
            m["c" + n] = v
        per_core.append(m)
    nc = build_nc()
    res = run_bass_kernel_spmd(nc, per_core, core_ids=list(range(N_CORES)))
    out = np.empty((N_CORES, T, D), np.float32)
    for b in range(N_CORES):
        out[b] = res.results[b]["y"].T
    return out


# version 11 (final)
# speedup vs baseline: 1.1301x; 1.0190x over previous
import contextlib
import os

import numpy as np

import concourse.bass as bass
import concourse.mybir as mybir
from concourse.bass_utils import run_bass_kernel_spmd

F32 = mybir.dt.float32
BF16 = mybir.dt.bfloat16
I16 = mybir.dt.int16
U16 = mybir.dt.uint16
I8 = mybir.dt.int8
AL = mybir.AluOpType
AF = mybir.ActivationFunctionType
AX = mybir.AxisListType.X

D = 1024
T = 4096
FI = 3072
NH = 16
HD = 64
WIN = 256
DC = D // 128
IC = FI // 128
TT = 512
NT = T // TT
NW = T // WIN

MAGIC = float(np.float32(12582912.0))
FMIN = float(np.finfo(np.float32).min)
FMAX = float(np.finfo(np.float32).max)
LN_EPS = 1e-5
RMS_EPS = 1.1920928955078125e-07
N_CORES = 8

_DBG = os.environ.get("MK_DBG", "") == "1"
_NPH = int(os.environ.get("MK_PHASES", "99"))


def _host_constants():
    c = {}
    c["ident"] = np.eye(128, dtype=np.float32)
    perm = np.zeros((128, 128), np.float32)
    for p in range(128):
        perm[p, p ^ 32] = 1.0
    c["perm32"] = perm
    half = HD // 2
    inv_freq = (1.0 / (10000.0 ** (np.arange(0, HD, 2, dtype=np.float32) / np.float32(HD)))).astype(np.float32)
    freqs = np.outer(np.arange(T, dtype=np.float32), inv_freq).astype(np.float32)
    cosv, sinv = np.cos(freqs).astype(np.float32), np.sin(freqs).astype(np.float32)
    A = np.zeros((128, T), np.float32)
    B = np.zeros((128, T), np.float32)
    for p in range(128):
        j = p % 64
        jj = j if j < half else j - half
        A[p] = cosv[:, jj]
        B[p] = sinv[:, jj] if j < half else -sinv[:, jj]
    c["ropeA"], c["ropeB"] = A, B
    for s in range(2):
        m = np.full((128, WIN), FMIN, np.float32)
        for r in range(128):
            m[r, : s * 128 + r + 1] = FMAX
        c[f"maskq{s}"] = m
    c["ones128"] = np.ones((128, 1), np.float32)
    c["ones1x"] = np.ones((1, 128), np.float32)
    rmsmask = np.zeros((128, 2), np.float32)
    rmsmask[:64, 0] = 1.0
    rmsmask[64:, 1] = 1.0
    c["rmsmask"] = rmsmask
    rmsbc = np.zeros((2, 128), np.float32)
    rmsbc[0, :64] = 1.0
    rmsbc[1, 64:] = 1.0
    c["rmsbc"] = rmsbc
    return c


_CONSTS = _host_constants()


class Sems:

    def __init__(self):
        self.sem = {}
        self.cnt = {}

    def add(self, nc, ctx, names):
        for n in names:
            self.sem[n] = ctx.enter_context(nc.semaphore(n))
            self.cnt[n] = 0

    def inc(self, instr, name, n=1):
        instr.then_inc(self.sem[name], n)
        self.cnt[name] += n
        return self.cnt[name]

    def wait(self, eng, name, v):
        if v > 0:
            eng.wait_ge(self.sem[name], v)

    def cinc(self, chain, name, n=1):
        chain.flush()
        chain.e.sem_inc(self.sem[name], n)
        self.cnt[name] += n
        return self.cnt[name]


class Chain:

    EMITS = {"tensor_scalar", "tensor_tensor", "scalar_tensor_tensor",
             "tensor_reduce", "tensor_copy", "tensor_scalar_mul",
             "tensor_add", "tensor_sub", "tensor_mul", "reciprocal",
             "memset", "activation", "copy", "sqrt", "mul", "add",
             "square"}

    def __init__(self, eng, S, sem):
        self.e = eng
        self.S = S
        self.semname = sem
        self.waited = S.cnt[sem]

    def flush(self):
        if self.S.cnt[self.semname] > self.waited:
            self.e.wait_ge(self.S.sem[self.semname], self.S.cnt[self.semname])
            self.waited = self.S.cnt[self.semname]

    def __getattr__(self, n):
        f = getattr(self.e, n)
        if n not in Chain.EMITS:
            return f

        def g(*a, **k):
            self.flush()
            i = f(*a, **k)
            self.S.cnt[self.semname] += 1
            i.then_inc(self.S.sem[self.semname], 1)
            return i
        return g


class DmaGroup:

    def __init__(self, nc, ctx, name, K):
        self.K = K
        self.sems = [ctx.enter_context(nc.semaphore(f"{name}{k}"))
                     for k in range(K)]
        self.cnt = [0] * K

    def plan_counters(self):
        return list(self.cnt)

    @staticmethod
    def plan(counters, slot, K):
        k = slot % K
        counters[k] += 16
        return (k, counters[k])

    def issue(self, instr, slot, expect=None):
        k = slot % self.K
        instr.then_inc(self.sems[k], 16)
        self.cnt[k] += 16
        tok = (k, self.cnt[k])
        if expect is not None:
            assert tok == expect, (tok, expect)
        return tok

    def wait(self, eng, tok):
        if tok is not None:
            eng.wait_ge(self.sems[tok[0]], tok[1])

    def wait_all(self, eng):
        for k, c in enumerate(self.cnt):
            if c:
                eng.wait_ge(self.sems[k], c)


def build_nc():
    nc = bass.Bass("TRN2", target_bir_lowering=False)
    ap = {}

    def din(name, shape, dt=F32):
        ap[name] = nc.dram_tensor(name, shape, dt, kind="ExternalInput").ap()

    def dint(name, shape, dt=F32, dbg=False):
        kind = "ExternalOutput" if (dbg and _DBG) else "Internal"
        ap[name] = nc.dram_tensor(name, shape, dt, kind=kind).ap()

    din("x", [D, T])
    for n, sh in (("wq", [D, D]), ("wk", [D, D]), ("wv", [D, D]), ("wo", [D, D]),
                  ("wg", [FI, D]), ("wu", [FI, D]), ("wd", [D, FI])):
        din(n, sh)
    din("qg", [2, NH // 2])
    for n, v in _CONSTS.items():
        din("c" + n, list(v.shape))
    ap["y"] = nc.dram_tensor("y", [D, T], F32, kind="ExternalOutput").ap()

    for n, sh in (("wqT", [D, D]), ("wkT", [D, D]), ("wvT", [D, D]),
                  ("woT", [D, D]), ("wgT", [D, FI]), ("wuT", [D, FI]),
                  ("wdT", [FI, D])):
        dint(n + "h", sh, BF16, dbg=(n == "wgT"))
        dint(n + "l", sh, BF16)
    for n in ("wg", "wu", "wd"):
        dint("q8" + n, list(ap[n].shape), I8)
    dint("up_raw", [FI, T], dbg=True)
    dint("gate_raw", [FI, T])
    dint("uq16", [FI, T], I16)
    dint("gq16", [FI, T], I16)
    dint("sq16", [FI, T], U16)
    dint("g2q16", [FI, T], I16, dbg=True)
    dint("xmid", [D, T], dbg=True)
    dint("xmid2", [D, T], dbg=True)
    dint("qrot", [D, T], dbg=True)
    dint("krot", [D, T])
    dint("vtm", [T, D], dbg=True)
    for k in range(10):
        dint(f"cci{k}", [1, 1])
        dint(f"cco{k}", [1, 1])
        dint(f"colb{k}", [128, 1])
        dint(f"sdr{k}", [1, 2])

    with contextlib.ExitStack() as ctx:
        ent = ctx.enter_context
        ARENA = ent(nc.sbuf_tensor("ARENA", [128, 32768], F32))
        POOL = ent(nc.sbuf_tensor("POOL", [128, 12288], F32))
        STAT = ent(nc.sbuf_tensor("STAT", [128, 768], F32))
        SCAL = ent(nc.sbuf_tensor("SCAL", [128, 16], F32))
        ROWS = ent(nc.sbuf_tensor("ROWS", [1, 4 * TT], F32))
        RMSW = ent(nc.sbuf_tensor("RMSW", [2, 4 * TT], F32))
        CST = {n: ent(nc.sbuf_tensor("cs" + n, list(v.shape), F32))
               for n, v in _CONSTS.items() if n not in ("ropeA", "ropeB")}
        QG = ent(nc.sbuf_tensor("qg_sb", [2, NH // 2], F32))
        EPS = ent(nc.sbuf_tensor("eps_sb", [128, 3], F32))
        PS = [ent(nc.psum_tensor(f"ps{k}", [128, 512], F32)) for k in range(8)]

        S = Sems()
        S.add(nc, ctx, ["TEN", "VEC", "SCA", "CCS", "VE2", "SC2"])
        GL = DmaGroup(nc, ctx, "GL", 8)
        GS = DmaGroup(nc, ctx, "GS", 8)
        GC = DmaGroup(nc, ctx, "GC", 8)
        GA = DmaGroup(nc, ctx, "GA", 1)
        BAR = ctx.enter_context(nc.semaphore("BAR"))

        block = ent(nc.Block())

        def bvec(f):
            block.vector(lambda v: f(Chain(v, S, "VE2")))

        def bsca(f):
            block.scalar(lambda s2: f(Chain(s2, S, "SC2")))

        barctr = [0]

        def my_barrier():
            barctr[0] += 1
            tgt = 5 * barctr[0]
            for name in ("sync", "scalar", "tensor", "vector", "gpsimd"):
                def f(eng):
                    eng.drain().then_inc(BAR, 1)
                    eng.wait_ge(BAR, tgt)
                getattr(block, name)(f)

        def ld3(eng, out2d, src2d):
            C = src2d.shape[0] // 128
            return eng.dma_start(
                out=out2d.rearrange("p (c t) -> p c t", c=C),
                in_=src2d.rearrange("(c p) t -> p c t", p=128))

        def TSL(k):
            return POOL[:, k * 512:(k + 1) * 512]

        def AR_(a, b):
            return ARENA[:, a:b]

        SLOT = {"s1": 0, "i1": 1, "s2": 2, "i2": 3, "s3": 4, "i3": 5,
                "s4": 6, "i4": 7, "s5": 8, "i5": 9, "s42": 10, "s3c": 11, "h5": 12}

        def SCs(name):
            return SCAL[:, SLOT[name]:SLOT[name] + 1]

        M0 = STAT[:, 0:192]
        M1 = STAT[:, 192:384]
        PMcol = STAT[:, 384:385]
        Grow = STAT[0:1, 385:386]
        SROW = STAT[0:1, 386:388]
        ROWB = STAT[0:1, 388:516]
        NMC = STAT[:, 520:538]

        murow = ROWS[0:1, 0:TT]
        rrow = ROWS[0:1, TT:2 * TT]
        var_row = ROWS[0:1, 2 * TT:3 * TT]
        sd_row = ROWS[0:1, 3 * TT:4 * TT]

        ROPE_A = AR_(24576, 28672)
        ROPE_B = AR_(28672, 32768)
        GBC = ARENA[0:2, 0:1024]

        ph = [0]

        def active():
            ph[0] += 1
            return ph[0] <= _NPH

        @block.sync
        def _(sync):
            names = [n for n in _CONSTS if n not in ("ropeA", "ropeB")]
            for idx, n in enumerate(names):
                i = sync.dma_start(out=CST[n][:], in_=ap["c" + n])
                GL.issue(i, idx)
            i = sync.dma_start(out=QG[:], in_=ap["qg"])
            GL.issue(i, len(names))
            GL.wait_all(sync)

        @bvec
        def _(vector):
            vector.memset(EPS[:, 0:1], LN_EPS)
            vector.memset(EPS[:, 1:2], RMS_EPS)
            vector.memset(EPS[:, 2:3], MAGIC)

        my_barrier()

        def emit_wprep():
            plans = [("wq", D, D, False), ("wk", D, D, False),
                     ("wv", D, D, False), ("wo", D, D, False),
                     ("wg", FI, D, True), ("wu", FI, D, True),
                     ("wd", D, FI, True)]
            loads = []
            for (n, rows, cols, quant) in plans:
                for oc in range(rows // 128):
                    loads.append((n, oc, cols, quant))
            NL = len(loads)

            wbuf = [AR_(0, 3072), AR_(3072, 6144)]
            qf = [AR_(6144, 9216), AR_(9216, 12288)]
            dq = [AR_(12288, 15360), AR_(15360, 18432)]
            sA = [M0[:, 0:96], M0[:, 96:192]]
            sR = [M1[:, 0:96], M1[:, 96:192]]
            EV = AR_(18432, 19456)
            EVH = AR_(19456, 19968).bitcast(BF16)
            EVL = AR_(19968, 20480).bitcast(BF16)

            b_vec, b_ten, b_sca = S.cnt["VEC"], S.cnt["TEN"], S.cnt["SCA"]
            cl, cc_, cs_ = (GL.plan_counters(), GC.plan_counters(),
                            GS.plan_counters())
            ld_at = [None] * NL
            q8st_at = [None] * NL
            q8ld_at = [None] * NL
            qrdy_at = [0] * NL
            deq_at = [0] * NL
            ntp = [0] * NL
            tp0 = [0] * NL
            g = 0
            for idx, (n, oc, cols, quant) in enumerate(loads):
                ld_at[idx] = DmaGroup.plan(cl, idx % 2, GL.K)
                if quant:
                    q8st_at[idx] = DmaGroup.plan(cc_, idx % 2, GC.K)
                    q8ld_at[idx] = DmaGroup.plan(cc_, idx % 2, GC.K)
                ntp[idx] = cols // 128
                tp0[idx] = g
                g += ntp[idx]
            NG = g
            NGRP = NG // 4
            grp = []
            for idx in range(NL):
                for li in range(ntp[idx] // 4):
                    grp.append((idx, li))
            assert len(grp) == NGRP
            g0 = [tp0[idx] // 4 for idx in range(NL)]
            vec = b_vec
            vsp_at = [0] * NGRP
            for s in range(NL + 2):
                if 0 <= s - 2 < NL:
                    for li in range(ntp[s - 2] // 4):
                        vec += 1
                        vsp_at[g0[s - 2] + li] = vec
                if s < NL and loads[s][3]:
                    vec += 1
                    qrdy_at[s] = vec
                if 1 <= s:
                    if s - 1 < NL:
                        vec += 1
                        deq_at[s - 1] = vec
            ten_at = [b_ten + j + 1 for j in range(NG)]
            sca_at = [b_sca + j + 1 for j in range(NG)]
            sth_at = [DmaGroup.plan(cs_, gg % 2, GS.K) for gg in range(NGRP)]
            stl_at = [DmaGroup.plan(cs_, 2 + gg % 2, GS.K)
                      for gg in range(NGRP)]

            @block.sync
            def _(sync):
                for idx, (n, oc, cols, quant) in enumerate(loads):
                    b = idx % 2
                    if idx >= 2:
                        S.wait(sync, "VEC", deq_at[idx - 2])
                    i = sync.dma_start(out=wbuf[b][:, 0:cols],
                                       in_=ap[n][oc * 128:(oc + 1) * 128, :])
                    GL.issue(i, idx % 2, ld_at[idx])

            @bvec
            def _(vector):
                for s in range(NL + 2):
                    if 0 <= s - 2 < NL:
                        idx2 = s - 2
                        for li in range(ntp[idx2] // 4):
                            gg = g0[idx2] + li
                            j3 = tp0[idx2] + 4 * li + 3
                            S.wait(vector, "SCA", sca_at[j3])
                            if gg >= 2:
                                GS.wait(vector, sth_at[gg - 2])
                                GS.wait(vector, stl_at[gg - 2])
                            h = (gg % 2) * 512
                            vector.tensor_copy(EVH[:, h:h + 512],
                                               EV[:, h:h + 512])
                            vector.tensor_tensor(EVL[:, h:h + 512],
                                                 EV[:, h:h + 512],
                                                 EVH[:, h:h + 512],
                                                 AL.subtract)
                            assert S.cinc(vector, "VEC") == vsp_at[gg]
                    if s < NL and loads[s][3]:
                        n, oc, cols, quant = loads[s]
                        b = s % 2
                        GL.wait(vector, ld_at[s])
                        nb = cols // 32
                        w3 = wbuf[b][:, 0:cols].rearrange(
                            "p (b k) -> p b k", k=32)
                        vector.tensor_reduce(
                            sA[b][:, 0:nb].rearrange("p (b o) -> p b o", o=1),
                            w3, AX, AL.max, apply_absolute_value=True)
                        vector.tensor_scalar(sA[b][:, 0:nb], sA[b][:, 0:nb],
                                             1.0 / 7.0, 1e-12,
                                             AL.mult, AL.max)
                        vector.reciprocal(sR[b][:, 0:nb], sA[b][:, 0:nb])
                        rbc = sR[b][:, 0:nb].rearrange(
                            "p (b o) -> p b o", o=1).broadcast_to([128, nb, 32])
                        vector.tensor_tensor(
                            qf[b][:, 0:cols].rearrange("p (b k) -> p b k",
                                                       k=32),
                            w3, rbc, AL.mult)
                        vector.tensor_scalar(qf[b][:, 0:cols],
                                             qf[b][:, 0:cols],
                                             1.0, MAGIC, AL.mult, AL.add)
                        vector.tensor_scalar(qf[b][:, 0:cols],
                                             qf[b][:, 0:cols],
                                             MAGIC, 7.0,
                                             AL.subtract, AL.min)
                        assert S.cinc(vector, "VEC") == qrdy_at[s]
                    if 1 <= s and s - 1 < NL:
                        idx1 = s - 1
                        n, oc, cols, quant = loads[idx1]
                        b = idx1 % 2
                        if quant:
                            nb = cols // 32
                            GC.wait(vector, q8ld_at[idx1])
                            sbc = sA[b][:, 0:nb].rearrange(
                                "p (b o) -> p b o",
                                o=1).broadcast_to([128, nb, 32])
                            vector.tensor_tensor(
                                dq[b][:, 0:cols].rearrange(
                                    "p (b k) -> p b k", k=32),
                                qf[b][:, 0:cols].rearrange(
                                    "p (b k) -> p b k", k=32),
                                sbc, AL.mult)
                            assert S.cinc(vector, "VEC") == deq_at[idx1]
                        else:
                            GL.wait(vector, ld_at[idx1])
                            vector.tensor_copy(dq[b][:, 0:cols],
                                               wbuf[b][:, 0:cols])
                            assert S.cinc(vector, "VEC") == deq_at[idx1]

            @block.gpsimd
            def _(gp):
                for s in range(NL + 2):
                    if 0 <= s - 2 < NL:
                        idx2 = s - 2
                        n, oc, cols, quant = loads[idx2]
                        for li in range(ntp[idx2] // 4):
                            gg = g0[idx2] + li
                            S.wait(gp, "VEC", vsp_at[gg])
                            h = (gg % 2) * 512
                            r0 = 4 * li * 128
                            i = gp.dma_start(
                                out=ap[n + "Th"][r0:r0 + 512,
                                                 oc * 128:(oc + 1) * 128]
                                .rearrange("(g p) c -> p g c", p=128),
                                in_=EVH[:, h:h + 512]
                                .rearrange("p (g c) -> p g c", g=4))
                            GS.issue(i, gg % 2, sth_at[gg])
                            i = gp.dma_start(
                                out=ap[n + "Tl"][r0:r0 + 512,
                                                 oc * 128:(oc + 1) * 128]
                                .rearrange("(g p) c -> p g c", p=128),
                                in_=EVL[:, h:h + 512]
                                .rearrange("p (g c) -> p g c", g=4))
                            GS.issue(i, 2 + gg % 2, stl_at[gg])
                    if s < NL and loads[s][3]:
                        n, oc, cols, quant = loads[s]
                        b = s % 2
                        S.wait(gp, "VEC", qrdy_at[s])
                        i = gp.dma_start(
                            out=ap["q8" + n][oc * 128:(oc + 1) * 128, :],
                            in_=qf[b][:, 0:cols])
                        GC.issue(i, b, q8st_at[s])
                    if 1 <= s and s - 1 < NL and loads[s - 1][3]:
                        n, oc, cols, quant = loads[s - 1]
                        b = (s - 1) % 2
                        GC.wait(gp, q8st_at[s - 1])
                        i = gp.dma_start(
                            out=qf[b][:, 0:cols],
                            in_=ap["q8" + n][oc * 128:(oc + 1) * 128, :])
                        GC.issue(i, b, q8ld_at[s - 1])
                GS.wait_all(gp)

            @block.tensor
            def _(tensor):
                for idx, (n, oc, cols, quant) in enumerate(loads):
                    b = idx % 2
                    S.wait(tensor, "VEC", deq_at[idx])
                    for k in range(ntp[idx]):
                        j = tp0[idx] + k
                        if j >= 4:
                            S.wait(tensor, "SCA", sca_at[j - 4])
                        i = tensor.transpose(PS[4 + j % 4][:, 0:128],
                                             dq[b][:, k * 128:(k + 1) * 128],
                                             CST["ident"][:])
                        assert S.inc(i, "TEN") == ten_at[j]

            @bsca
            def _(scalar):
                for j in range(NG):
                    S.wait(scalar, "TEN", ten_at[j])
                    if j >= 8:
                        S.wait(scalar, "VEC", vsp_at[j // 4 - 2])
                    i = scalar.copy(EV[:, (j % 8) * 128:(j % 8 + 1) * 128],
                                    PS[4 + j % 4][:, 0:128])
                    assert S.cinc(scalar, "SCA") == sca_at[j]

        if active():
            emit_wprep()
            my_barrier()

        def emit_allreduce(k, statr, ncols, s_slot, i_slot):
            b_vec, b_ccs = S.cnt["VEC"], S.cnt["CCS"]
            v1, v2, v3 = b_vec + 1, b_vec + 2, b_vec + 3
            ca = GA.plan_counters()
            g = [DmaGroup.plan(ca, 0, 1) for _ in range(7)]
            cc1 = b_ccs + 1

            @bvec
            def _(vector):
                vector.tensor_reduce(PMcol[:], statr[:, 0:ncols], AX, AL.max)
                assert S.cinc(vector, "VEC") == v1
                GA.wait(vector, g[1])
                vector.tensor_reduce(Grow[:], ROWB[:], AX, AL.max)
                assert S.cinc(vector, "VEC") == v2
                GA.wait(vector, g[3])
                vector.tensor_scalar(SROW[0:1, 0:1], Grow[:],
                                         1.0 / 32767.0, 1e-12,
                                         AL.mult, AL.max)
                vector.reciprocal(SROW[0:1, 1:2], SROW[0:1, 0:1])
                assert S.cinc(vector, "VEC") == v3

            @block.gpsimd
            def _(gp):
                S.wait(gp, "VEC", v1)
                i = gp.dma_start(out=ap[f"colb{k}"], in_=PMcol[:])
                GA.issue(i, 0, g[0])
                GA.wait(gp, g[0])
                i = gp.dma_start(out=ROWB[:], in_=ap[f"colb{k}"].transpose([1, 0]))
                GA.issue(i, 0, g[1])
                S.wait(gp, "VEC", v2)
                i = gp.dma_start(out=ap[f"cci{k}"], in_=Grow[:])
                GA.issue(i, 0, g[2])
                GA.wait(gp, g[2])
                i = gp.collective_compute(
                    "AllReduce", AL.max, replica_groups=[list(range(N_CORES))],
                    ins=[ap[f"cci{k}"]], outs=[ap[f"cco{k}"]])
                assert S.inc(i, "CCS") == cc1
                S.wait(gp, "CCS", cc1)
                i = gp.dma_start(out=Grow[:], in_=ap[f"cco{k}"])
                GA.issue(i, 0, g[3])
                S.wait(gp, "VEC", v3)
                i = gp.dma_start(out=ap[f"sdr{k}"], in_=SROW[:])
                GA.issue(i, 0, g[4])
                GA.wait(gp, g[4])
                i = gp.dma_start(out=SCs(s_slot),
                                 in_=ap[f"sdr{k}"][0:1, 0:1].broadcast_to([128, 1]))
                GA.issue(i, 0, g[5])
                GA.wait(gp, g[5])
                i = gp.dma_start(out=SCs(i_slot),
                                 in_=ap[f"sdr{k}"][0:1, 1:2].broadcast_to([128, 1]))
                GA.issue(i, 0, g[6])
                GA.wait(gp, g[6])

            my_barrier()

        def emit_ln(x_src, consume, ninc_per_chunk):
            XT2 = [POOL[:, 0:4096], POOL[:, 4096:8192]]
            SQB = POOL[:, 8192:12288]
            b_ten, b_sca, b_vec = S.cnt["TEN"], S.cnt["SCA"], S.cnt["VEC"]
            cl = GL.plan_counters()
            ld_at = [DmaGroup.plan(cl, t % 2, GL.K) for t in range(NT)]
            sq_at = [b_sca + 2 * t + 1 for t in range(NT)]
            sd_at = [b_sca + 2 * t + 2 for t in range(NT)]
            stats_at = [b_ten + 2 * t + 1 for t in range(NT)]
            bcast_at = [b_ten + 2 * t + 2 for t in range(NT)]
            vpt = 2 + DC * ninc_per_chunk
            row_at = [b_vec + vpt * t + 1 for t in range(NT)]
            rr_at = [b_vec + vpt * t + 2 for t in range(NT)]
            tile_at = [b_vec + vpt * (t + 1) for t in range(NT)]

            @block.sync
            def _(sync):
                for t in range(NT):
                    if t >= 2:
                        S.wait(sync, "VEC", tile_at[t - 2])
                    i = ld3(sync, XT2[t % 2][:],
                            ap[x_src][:, t * TT:(t + 1) * TT])
                    GL.issue(i, t % 2, ld_at[t])

            @bsca
            def _(scalar):
                for t in range(NT):
                    GL.wait(scalar, ld_at[t])
                    if t >= 1:
                        S.wait(scalar, "TEN", stats_at[t - 1])
                        S.wait(scalar, "VEC", tile_at[t - 1])
                    i = scalar.activation(SQB[:], XT2[t % 2][:], AF.Square)
                    assert S.cinc(scalar, "SCA") == sq_at[t]
                    S.wait(scalar, "VEC", row_at[t])
                    i = scalar.activation(sd_row, var_row, AF.Sqrt,
                                          bias=EPS[0:1, 0:1], scale=1.0)
                    assert S.cinc(scalar, "SCA") == sd_at[t]

            @block.tensor
            def _(tensor):
                for t in range(NT):
                    GL.wait(tensor, ld_at[t])
                    S.wait(tensor, "SCA", sq_at[t])
                    if t >= 1:
                        S.wait(tensor, "VEC", row_at[t - 1])
                        S.wait(tensor, "VEC", tile_at[t - 1])
                    for c in range(DC):
                        tensor.matmul(PS[0][0:1, 0:TT], CST["ones128"][:],
                                      XT2[t % 2][:, c * TT:(c + 1) * TT],
                                      start=(c == 0), stop=(c == DC - 1))
                    for c in range(DC):
                        i = tensor.matmul(PS[1][0:1, 0:TT], CST["ones128"][:],
                                          SQB[:, c * TT:(c + 1) * TT],
                                          start=(c == 0), stop=(c == DC - 1))
                    assert S.inc(i, "TEN") == stats_at[t]
                    S.wait(tensor, "VEC", rr_at[t])
                    tensor.matmul(PS[2][:, 0:TT], CST["ones1x"][:],
                                  murow, start=True, stop=True)
                    i = tensor.matmul(PS[3][:, 0:TT], CST["ones1x"][:],
                                      rrow, start=True, stop=True)
                    assert S.inc(i, "TEN") == bcast_at[t]

            @bvec
            def _(vector):
                for t in range(NT):
                    S.wait(vector, "TEN", stats_at[t])
                    vector.tensor_scalar(murow, PS[0][0:1, 0:TT],
                                         1.0 / float(D), None, AL.mult)
                    vector.tensor_scalar(var_row, PS[1][0:1, 0:TT],
                                         1.0 / float(D), None, AL.mult)
                    vector.tensor_tensor(sd_row, murow, murow, AL.mult)
                    vector.tensor_tensor(var_row, var_row, sd_row,
                                             AL.subtract)
                    assert S.cinc(vector, "VEC") == row_at[t]
                    S.wait(vector, "SCA", sd_at[t])
                    vector.reciprocal(rrow, sd_row)
                    assert S.cinc(vector, "VEC") == rr_at[t]
                    S.wait(vector, "TEN", bcast_at[t])
                    for c in range(DC):
                        consume(vector, t, c, XT2[t % 2][:, c * TT:(c + 1) * TT])
                    assert S.cnt["VEC"] == tile_at[t]

        def emit_ffn(x_src, x_dst, ccb):
            def p0a_consume(vector, t, c, xc):
                tmp = AR_(0, 512)
                tmp2 = AR_(512, 1024)
                vector.tensor_tensor(tmp, xc, PS[2][:, 0:TT], AL.subtract)
                vector.tensor_tensor(tmp2, tmp, PS[3][:, 0:TT], AL.mult)
                vector.tensor_reduce(M0[:, t * DC + c:t * DC + c + 1],
                                         tmp2, AX, AL.max,
                                         apply_absolute_value=True)
                S.cinc(vector, "VEC")

            if active():
                emit_ln(x_src, p0a_consume, 1)
                my_barrier()
                emit_allreduce(ccb + 0, M0, 64, "s1", "i1")

            XQH = ARENA[:, 0:16384].bitcast(BF16)
            XQL = ARENA[:, 16384:32768].bitcast(BF16)

            def xqh_ct(c, t):
                return XQH[:, c * 4096 + t * TT:c * 4096 + (t + 1) * TT]

            def xql_ct(c, t):
                return XQL[:, c * 4096 + t * TT:c * 4096 + (t + 1) * TT]

            def p0b_consume(vector, t, c, xc):
                tmp = POOL[:, 8192:8704]
                tmp2 = POOL[:, 8704:9216]
                vector.tensor_tensor(tmp, xc, PS[2][:, 0:TT], AL.subtract)
                vector.tensor_tensor(tmp2, tmp, PS[3][:, 0:TT], AL.mult)
                vector.tensor_scalar(tmp, tmp2, SCs("i1"), MAGIC,
                                     AL.mult, AL.add)
                vector.tensor_scalar(tmp2, tmp, MAGIC, 32767.0,
                                     AL.subtract, AL.min)
                vector.tensor_copy(xqh_ct(c, t), tmp2)
                vector.tensor_tensor(xql_ct(c, t), tmp2, xqh_ct(c, t),
                                     AL.subtract)
                S.cinc(vector, "VEC")

            if active():
                emit_ln(x_src, p0b_consume, 1)
                my_barrier()

            def xq_c(c):
                return AR_(c * 4096, (c + 1) * 4096)

            if active():
                WST = [POOL[:, 8192:10240], POOL[:, 10240:12288]]
                b_ten, b_sca, b_vec = S.cnt["TEN"], S.cnt["SCA"], S.cnt["VEC"]
                cl, cs_ = GL.plan_counters(), GS.plan_counters()
                NJ = IC * NT
                ldu_at = [None] * IC
                ldg_at = [None] * IC
                for i in range(IC):
                    ldu_at[i] = (DmaGroup.plan(cl, i % 2, GL.K),
                                 DmaGroup.plan(cl, 2 + i % 2, GL.K))
                    ldg_at[i] = (DmaGroup.plan(cl, 4 + i % 2, GL.K),
                                 DmaGroup.plan(cl, 6 + i % 2, GL.K))
                mmu_at = [b_ten + 2 * j + 1 for j in range(NJ)]
                mmg_at = [b_ten + 2 * j + 2 for j in range(NJ)]
                evu_at = [b_sca + 2 * j + 1 for j in range(NJ)]
                evg_at = [b_sca + 2 * j + 2 for j in range(NJ)]
                mxu_at = [b_vec + 2 * j + 1 for j in range(NJ)]
                mxg_at = [b_vec + 2 * j + 2 for j in range(NJ)]
                stu_at = [None] * NJ
                stg_at = [None] * NJ
                cs2 = cs_
                for j in range(NJ):
                    stu_at[j] = DmaGroup.plan(cs2, j % 4, GS.K)
                    stg_at[j] = DmaGroup.plan(cs2, 4 + j % 4, GS.K)

                WB = POOL[:, 8192:12288].bitcast(BF16)

                def wsl(i, plane):
                    base = (i % 2) * 4096 + plane * 1024
                    return WB[:, base:base + 1024]

                @block.sync
                def _(sync):
                    for i in range(IC):
                        if i >= 2:
                            S.wait(sync, "TEN", mmg_at[(i - 1) * NT - 1])
                        for pl, (nm, tok) in enumerate(
                                (("wuTh", ldu_at[i][0]), ("wuTl", ldu_at[i][1]),
                                 ("wgTh", ldg_at[i][0]), ("wgTl", ldg_at[i][1]))):
                            k = ld3(sync, wsl(i, pl),
                                    ap[nm][:, i * 128:(i + 1) * 128])
                            GL.issue(k, 2 * pl + i % 2, tok)

                @block.tensor
                def _(tensor):
                    for i in range(IC):
                        for tok in ldu_at[i] + ldg_at[i]:
                            GL.wait(tensor, tok)
                        for t in range(NT):
                            j = i * NT + t
                            if j >= 2:
                                S.wait(tensor, "SCA", evu_at[j - 2])
                            for c in range(DC):
                                wh = wsl(i, 0)[:, c * 128:(c + 1) * 128]
                                wl = wsl(i, 1)[:, c * 128:(c + 1) * 128]
                                tensor.matmul(PS[j % 2][:, 0:TT], wh,
                                              xqh_ct(c, t),
                                              start=(c == 0), stop=False)
                                tensor.matmul(PS[j % 2][:, 0:TT], wh,
                                              xql_ct(c, t),
                                              start=False, stop=False)
                                k = tensor.matmul(PS[j % 2][:, 0:TT], wl,
                                                  xqh_ct(c, t), start=False,
                                                  stop=(c == DC - 1))
                            assert S.inc(k, "TEN") == mmu_at[j]
                            if j >= 2:
                                S.wait(tensor, "SCA", evg_at[j - 2])
                            for c in range(DC):
                                wh = wsl(i, 2)[:, c * 128:(c + 1) * 128]
                                wl = wsl(i, 3)[:, c * 128:(c + 1) * 128]
                                tensor.matmul(PS[2 + j % 2][:, 0:TT], wh,
                                              xqh_ct(c, t),
                                              start=(c == 0), stop=False)
                                tensor.matmul(PS[2 + j % 2][:, 0:TT], wh,
                                              xql_ct(c, t),
                                              start=False, stop=False)
                                k = tensor.matmul(PS[2 + j % 2][:, 0:TT], wl,
                                                  xqh_ct(c, t), start=False,
                                                  stop=(c == DC - 1))
                            assert S.inc(k, "TEN") == mmg_at[j]

                @bsca
                def _(scalar):
                    for j in range(NJ):
                        S.wait(scalar, "TEN", mmu_at[j])
                        if j >= 4:
                            S.wait(scalar, "VEC", mxu_at[j - 4])
                            GS.wait(scalar, stu_at[j - 4])
                        scalar.activation(TSL(j % 4), PS[j % 2][:, 0:TT],
                                          AF.Copy, scale=SCs("s1"))
                        assert S.cinc(scalar, "SCA") == evu_at[j]
                        S.wait(scalar, "TEN", mmg_at[j])
                        if j >= 4:
                            S.wait(scalar, "VEC", mxg_at[j - 4])
                            GS.wait(scalar, stg_at[j - 4])
                        scalar.activation(TSL(4 + j % 4), PS[2 + j % 2][:, 0:TT],
                                          AF.Copy, scale=SCs("s1"))
                        assert S.cinc(scalar, "SCA") == evg_at[j]

                @bvec
                def _(vector):
                    for j in range(NJ):
                        S.wait(vector, "SCA", evu_at[j])
                        vector.tensor_reduce(M0[:, j:j + 1], TSL(j % 4),
                                                 AX, AL.max,
                                                 apply_absolute_value=True)
                        assert S.cinc(vector, "VEC") == mxu_at[j]
                        S.wait(vector, "SCA", evg_at[j])
                        vector.tensor_reduce(M1[:, j:j + 1], TSL(4 + j % 4),
                                                 AX, AL.max,
                                                 apply_absolute_value=True)
                        assert S.cinc(vector, "VEC") == mxg_at[j]

                @block.gpsimd
                def _(gp):
                    for j in range(NJ):
                        i, t = j // NT, j % NT
                        S.wait(gp, "SCA", evu_at[j])
                        k = gp.dma_start(
                            out=ap["up_raw"][i * 128:(i + 1) * 128,
                                             t * TT:(t + 1) * TT],
                            in_=TSL(j % 4)[:])
                        GS.issue(k, j % 4, stu_at[j])
                        S.wait(gp, "SCA", evg_at[j])
                        k = gp.dma_start(
                            out=ap["gate_raw"][i * 128:(i + 1) * 128,
                                               t * TT:(t + 1) * TT],
                            in_=TSL(4 + j % 4)[:])
                        GS.issue(k, 4 + j % 4, stg_at[j])
                    GS.wait_all(gp)

                my_barrier()
                emit_allreduce(ccb + 1, M0, NJ, "s2", "i2")
                emit_allreduce(ccb + 2, M1, NJ, "s3", "i3")

            if active():
                NJ = IC * NT
                b_vec, b_sca = S.cnt["VEC"], S.cnt["SCA"]
                cl, cs_ = GL.plan_counters(), GS.plan_counters()
                cc2 = GC.plan_counters()
                ldu_at = [None] * NJ
                ldg_at = [None] * NJ
                stu_at = [None] * NJ
                stg_at = [None] * NJ
                sts_at = [None] * NJ
                for j in range(NJ):
                    ldu_at[j] = DmaGroup.plan(cl, (j % 3) * 2, GL.K)
                    ldg_at[j] = DmaGroup.plan(cl, (j % 3) * 2 + 1, GL.K)
                    stu_at[j] = DmaGroup.plan(cs_, j % 3, GS.K)
                    stg_at[j] = DmaGroup.plan(cs_, 3 + j % 3, GS.K)
                    sts_at[j] = DmaGroup.plan(cc2, j % 3, GC.K)
                sig_at, w1s_at = {}, {}
                sc2 = b_sca
                for j in range(NJ):
                    sc2 += 1
                    sig_at[j] = sc2
                    sc2 += 1
                    w1s_at[j] = sc2
                uq_at, gq_at, sq_at, tl_at = {}, {}, {}, {}
                vc2 = b_vec + 1
                for j in range(NJ + 1):
                    if j < NJ:
                        vc2 += 1
                        uq_at[j] = vc2
                        vc2 += 1
                        gq_at[j] = vc2
                    if j >= 1:
                        vc2 += 1
                        sq_at[j - 1] = vc2
                        vc2 += 1
                        tl_at[j - 1] = vc2

                def sl(j, k):
                    return TSL((j % 3) * 8 + k)

                @block.sync
                def _(sync):
                    for j in range(NJ):
                        i, t = j // NT, j % NT
                        if j >= 3:
                            S.wait(sync, "VEC", uq_at[j - 3])
                        k = sync.dma_start(
                            out=sl(j, 0)[:],
                            in_=ap["up_raw"][i * 128:(i + 1) * 128,
                                             t * TT:(t + 1) * TT])
                        GL.issue(k, (j % 3) * 2, ldu_at[j])
                        if j >= 3:
                            S.wait(sync, "VEC", gq_at[j - 3])
                        k = sync.dma_start(
                            out=sl(j, 1)[:],
                            in_=ap["gate_raw"][i * 128:(i + 1) * 128,
                                               t * TT:(t + 1) * TT])
                        GL.issue(k, (j % 3) * 2 + 1, ldg_at[j])

                @bsca
                def _(scalar):
                    for j in range(NJ):
                        S.wait(scalar, "VEC", gq_at[j])
                        scalar.activation(sl(j, 6), sl(j, 4), AF.Sigmoid,
                                          scale=SCs("s3"))
                        assert S.cinc(scalar, "SCA") == sig_at[j]
                        scalar.activation(sl(j, 2), sl(j, 6), AF.Copy,
                                          scale=65536.0, bias=MAGIC)
                        assert S.cinc(scalar, "SCA") == w1s_at[j]

                @bvec
                def _(vector):
                    vector.tensor_scalar(SCs("s3c"), SCs("s3"),
                                             float(2.0 ** -16), None, AL.mult)
                    S.cinc(vector, "VEC")
                    for j in range(NJ + 1):
                        if j < NJ:
                            GL.wait(vector, ldu_at[j])
                            if j >= 3:
                                GS.wait(vector, stu_at[j - 3])
                            vector.tensor_scalar(sl(j, 2), sl(j, 0),
                                                 SCs("i2"), MAGIC,
                                                 AL.mult, AL.add)
                            vector.tensor_scalar(sl(j, 3), sl(j, 2), MAGIC,
                                                     32767.0,
                                                     AL.subtract, AL.min)
                            assert S.cinc(vector, "VEC") == uq_at[j]
                            GL.wait(vector, ldg_at[j])
                            if j >= 3:
                                GS.wait(vector, stg_at[j - 3])
                            vector.tensor_scalar(sl(j, 2), sl(j, 1),
                                                 SCs("i3"), MAGIC,
                                                 AL.mult, AL.add)
                            vector.tensor_scalar(sl(j, 4), sl(j, 2), MAGIC,
                                                     32767.0,
                                                     AL.subtract, AL.min)
                            assert S.cinc(vector, "VEC") == gq_at[j]
                        if j >= 1:
                            jj = j - 1
                            S.wait(vector, "SCA", w1s_at[jj])
                            if jj >= 3:
                                GC.wait(vector, sts_at[jj - 3])
                            vector.tensor_scalar(sl(jj, 7), sl(jj, 2), MAGIC,
                                                 65535.0,
                                                 AL.subtract, AL.min)
                            assert S.cinc(vector, "VEC") == sq_at[jj]
                            vector.scalar_tensor_tensor(sl(jj, 5), sl(jj, 7),
                                                            SCs("s3c"),
                                                            sl(jj, 4),
                                                            AL.mult, AL.mult)
                            vector.tensor_reduce(M0[:, jj:jj + 1], sl(jj, 5),
                                                     AX, AL.max,
                                                     apply_absolute_value=True)
                            assert S.cinc(vector, "VEC") == tl_at[jj]

                @block.gpsimd
                def _(gp):
                    for j in range(NJ):
                        i, t = j // NT, j % NT
                        dst = (slice(i * 128, (i + 1) * 128),
                               slice(t * TT, (t + 1) * TT))
                        S.wait(gp, "VEC", uq_at[j])
                        k = gp.dma_start(out=ap["uq16"][dst[0], dst[1]],
                                         in_=sl(j, 3)[:])
                        GS.issue(k, j % 3, stu_at[j])
                        S.wait(gp, "VEC", gq_at[j])
                        k = gp.dma_start(out=ap["gq16"][dst[0], dst[1]],
                                         in_=sl(j, 4)[:])
                        GS.issue(k, 3 + j % 3, stg_at[j])
                        S.wait(gp, "VEC", sq_at[j])
                        k = gp.dma_start(out=ap["sq16"][dst[0], dst[1]],
                                         in_=sl(j, 7)[:])
                        GC.issue(k, j % 3, sts_at[j])
                    GS.wait_all(gp)
                    GC.wait_all(gp)

                my_barrier()
                emit_allreduce(ccb + 3, M0, NJ, "s4", "i4")

            if active():
                NJ = IC * NT
                b_vec, b_sca = S.cnt["VEC"], S.cnt["SCA"]
                cc_, cl, cs_ = (GC.plan_counters(), GL.plan_counters(),
                                GS.plan_counters())
                ldu_at = [None] * NJ
                ldg_at = [None] * NJ
                lds_at = [None] * NJ
                st_at = [None] * NJ
                for j in range(NJ):
                    ldu_at[j] = DmaGroup.plan(cc_, (j % 3) * 2, GC.K)
                    ldg_at[j] = DmaGroup.plan(cc_, (j % 3) * 2 + 1, GC.K)
                    lds_at[j] = DmaGroup.plan(cs_, 3 + j % 3, GS.K)
                    st_at[j] = DmaGroup.plan(cs_, j % 3, GS.K)
                sc_at = [b_sca + j + 1 for j in range(NJ)]
                st4_at, g2_at, tl_at = {}, {}, {}
                vc2 = b_vec + 1
                for j in range(NJ + 1):
                    if j < NJ:
                        vc2 += 1
                        st4_at[j] = vc2
                    if j >= 1:
                        vc2 += 1
                        g2_at[j - 1] = vc2
                        vc2 += 1
                        tl_at[j - 1] = vc2

                def sl(j, k):
                    return TSL((j % 3) * 8 + k)

                @block.gpsimd
                def _(gp):
                    for j in range(NJ + 2):
                        if j < NJ:
                            i, t = j // NT, j % NT
                            src = (slice(i * 128, (i + 1) * 128),
                                   slice(t * TT, (t + 1) * TT))
                            if j >= 3:
                                S.wait(gp, "VEC", tl_at[j - 3])
                            k = gp.dma_start(out=sl(j, 0)[:],
                                             in_=ap["uq16"][src[0], src[1]])
                            GC.issue(k, (j % 3) * 2, ldu_at[j])
                            k = gp.dma_start(out=sl(j, 1)[:],
                                             in_=ap["gq16"][src[0], src[1]])
                            GC.issue(k, (j % 3) * 2 + 1, ldg_at[j])
                            k = gp.dma_start(out=sl(j, 2)[:],
                                             in_=ap["sq16"][src[0], src[1]])
                            GS.issue(k, 3 + j % 3, lds_at[j])
                        if j >= 2:
                            jj = j - 2
                            i, t = jj // NT, jj % NT
                            S.wait(gp, "VEC", g2_at[jj])
                            k = gp.dma_start(
                                out=ap["g2q16"][i * 128:(i + 1) * 128,
                                                t * TT:(t + 1) * TT],
                                in_=sl(jj, 4)[:])
                            GS.issue(k, jj % 3, st_at[jj])
                    GS.wait_all(gp)
                    GC.wait_all(gp)

                @bsca
                def _(scalar):
                    for j in range(NJ):
                        S.wait(scalar, "VEC", st4_at[j])
                        if j >= 3:
                            S.wait(scalar, "VEC", g2_at[j - 3])
                        scalar.activation(sl(j, 6), sl(j, 3), AF.Copy,
                                          scale=SCs("i4"), bias=MAGIC)
                        assert S.cinc(scalar, "SCA") == sc_at[j]

                @bvec
                def _(vector):
                    vector.tensor_tensor(SCs("s42"), SCs("s4"), SCs("s2"),
                                             AL.mult)
                    S.cinc(vector, "VEC")
                    for j in range(NJ + 1):
                        if j < NJ:
                            GC.wait(vector, ldg_at[j])
                            GS.wait(vector, lds_at[j])
                            if j >= 3:
                                S.wait(vector, "SCA", sc_at[j - 3])
                            vector.scalar_tensor_tensor(sl(j, 3), sl(j, 2),
                                                        SCs("s3c"), sl(j, 1),
                                                        AL.mult, AL.mult)
                            assert S.cinc(vector, "VEC") == st4_at[j]
                        if j >= 1:
                            jj = j - 1
                            S.wait(vector, "SCA", sc_at[jj])
                            if jj >= 3:
                                GS.wait(vector, st_at[jj - 3])
                            vector.tensor_scalar(sl(jj, 4), sl(jj, 6), MAGIC,
                                                     32767.0,
                                                     AL.subtract, AL.min)
                            assert S.cinc(vector, "VEC") == g2_at[jj]
                            GC.wait(vector, ldu_at[jj])
                            vector.scalar_tensor_tensor(sl(jj, 5), sl(jj, 4),
                                                            SCs("s42"),
                                                            sl(jj, 0),
                                                            AL.mult, AL.mult)
                            vector.tensor_reduce(M1[:, jj:jj + 1], sl(jj, 5),
                                                     AX, AL.max,
                                                     apply_absolute_value=True)
                            assert S.cinc(vector, "VEC") == tl_at[jj]

                my_barrier()
                emit_allreduce(ccb + 4, M1, NJ, "s5", "i5")

            if active():
                OBH = [AR_(0, 12288).bitcast(BF16),
                       AR_(12288, 24576).bitcast(BF16)]
                WDB = AR_(24576, 30720).bitcast(BF16)

                def obh(tb, ic):
                    return OBH[tb][:, ic * TT:(ic + 1) * TT]

                def obl(tb, ic):
                    return OBH[tb][:, 12288 + ic * TT:12288 + (ic + 1) * TT]

                def wdsl(j, plane):
                    base = (j % 2) * 6144 + plane * 3072
                    return WDB[:, base:base + 3072]
                b_vec, b_ten = S.cnt["VEC"], S.cnt["TEN"]
                cc_, cl, cs_ = (GC.plan_counters(), GL.plan_counters(),
                                GS.plan_counters())
                oldu_at = [[None] * IC for _ in range(NT)]
                oldg_at = [[None] * IC for _ in range(NT)]
                wld_at = [[None] * DC for _ in range(NT)]
                xo_at = [[None] * DC for _ in range(NT)]
                st_at = [[None] * DC for _ in range(NT)]
                for t in range(NT):
                    for ic in range(IC):
                        oldu_at[t][ic] = DmaGroup.plan(cc_, (ic % 2) * 2, GC.K)
                        oldg_at[t][ic] = DmaGroup.plan(cc_, (ic % 2) * 2 + 1, GC.K)
                    for h in range(DC):
                        wld_at[t][h] = (
                            DmaGroup.plan(cl, 2 + (t * DC + h) % 2, GL.K),
                            DmaGroup.plan(cl, 4 + (t * DC + h) % 2, GL.K))
                        xo_at[t][h] = DmaGroup.plan(cl, 6 + h % 2, GL.K)
                        st_at[t][h] = DmaGroup.plan(cs_, h % 2, GS.K)
                mm_at = [[b_ten + t * DC + h + 1 for h in range(DC)]
                         for t in range(NT)]
                ov_at = [[0] * IC for _ in range(NT)]
                xn_at = [[0] * DC for _ in range(NT)]
                vp = b_vec + 1
                for t in range(NT + 1):
                    if t < NT:
                        for ic in range(IC):
                            vp += 1
                            ov_at[t][ic] = vp
                    if t >= 1:
                        for h in range(DC):
                            vp += 1
                            xn_at[t - 1][h] = vp

                def slo(ic, k):
                    return TSL(16 + (ic % 2) * 2 + k)

                @block.gpsimd
                def _(gp):
                    for t in range(NT + 1):
                        if t < NT:
                            if t >= 2:
                                S.wait(gp, "TEN", mm_at[t - 2][DC - 1])
                            for ic in range(IC):
                                if t * IC + ic >= 2:
                                    pj = t * IC + ic - 2
                                    S.wait(gp, "VEC", ov_at[pj // IC][pj % IC])
                                src = (slice(ic * 128, (ic + 1) * 128),
                                       slice(t * TT, (t + 1) * TT))
                                k = gp.dma_start(out=slo(ic, 0)[:],
                                                 in_=ap["uq16"][src[0], src[1]])
                                GC.issue(k, (ic % 2) * 2, oldu_at[t][ic])
                                k = gp.dma_start(out=slo(ic, 1)[:],
                                                 in_=ap["g2q16"][src[0], src[1]])
                                GC.issue(k, (ic % 2) * 2 + 1, oldg_at[t][ic])
                        if t >= 1:
                            for h in range(DC):
                                S.wait(gp, "VEC", xn_at[t - 1][h])
                                k = gp.dma_start(
                                    out=ap[x_dst][h * 128:(h + 1) * 128,
                                                  (t - 1) * TT:t * TT],
                                    in_=TSL(14 + h % 2)[:])
                                GS.issue(k, h % 2, st_at[t - 1][h])
                    GS.wait_all(gp)

                @bvec
                def _(vector):
                    vector.tensor_scalar(SCs("h5"), SCs("s5"), 0.5, None,
                                         AL.mult)
                    S.cinc(vector, "VEC")
                    for t in range(NT + 1):
                        if t < NT:
                            for ic in range(IC):
                                GC.wait(vector, oldu_at[t][ic])
                                GC.wait(vector, oldg_at[t][ic])
                                s0, s1 = slo(ic, 0), slo(ic, 1)
                                tmp = TSL(22)
                                tmp2 = TSL(23)
                                vector.scalar_tensor_tensor(
                                    tmp, s1, SCs("s42"), s0, AL.mult, AL.mult)
                                vector.tensor_scalar(tmp, tmp, SCs("i5"), MAGIC,
                                                     AL.mult, AL.add)
                                vector.tensor_scalar(tmp2, tmp, MAGIC, 32767.0,
                                                     AL.subtract, AL.min)
                                vector.tensor_copy(obh(t % 2, ic), tmp2)
                                vector.tensor_tensor(obl(t % 2, ic), tmp2,
                                                     obh(t % 2, ic),
                                                     AL.subtract)
                                assert S.cinc(vector, "VEC") == ov_at[t][ic]
                        if t >= 1:
                            for h in range(DC):
                                j = (t - 1) * DC + h
                                S.wait(vector, "TEN", mm_at[t - 1][h])
                                GL.wait(vector, xo_at[t - 1][h])
                                if j >= 2:
                                    GS.wait(vector,
                                            st_at[(j - 2) // DC][(j - 2) % DC])
                                vector.scalar_tensor_tensor(
                                    TSL(14 + h % 2), PS[4 + j % 2][:, 0:TT],
                                    SCs("h5"), TSL(12 + h % 2), AL.mult, AL.add)
                                assert S.cinc(vector, "VEC") == xn_at[t - 1][h]

                @block.sync
                def _(sync):
                    for t in range(NT):
                        for h in range(DC):
                            j = t * DC + h
                            if j >= 2:
                                S.wait(sync, "TEN",
                                       mm_at[(j - 2) // DC][(j - 2) % DC])
                            k = ld3(sync, wdsl(j, 0),
                                    ap["wdTh"][:, h * 128:(h + 1) * 128])
                            GL.issue(k, 2 + j % 2, wld_at[t][h][0])
                            k = ld3(sync, wdsl(j, 1),
                                    ap["wdTl"][:, h * 128:(h + 1) * 128])
                            GL.issue(k, 4 + j % 2, wld_at[t][h][1])
                            if j >= 2:
                                S.wait(sync, "VEC",
                                       xn_at[(j - 2) // DC][(j - 2) % DC])
                            k = sync.dma_start(
                                out=TSL(12 + h % 2)[:],
                                in_=ap[x_src][h * 128:(h + 1) * 128,
                                              t * TT:(t + 1) * TT])
                            GL.issue(k, 6 + h % 2, xo_at[t][h])

                @block.tensor
                def _(tensor):
                    for t in range(NT):
                        S.wait(tensor, "VEC", ov_at[t][IC - 1])
                        for h in range(DC):
                            j = t * DC + h
                            GL.wait(tensor, wld_at[t][h][0])
                            GL.wait(tensor, wld_at[t][h][1])
                            if j >= 2:
                                S.wait(tensor, "VEC",
                                       xn_at[(j - 2) // DC][(j - 2) % DC])
                            for ic in range(IC):
                                wh = wdsl(j, 0)[:, ic * 128:(ic + 1) * 128]
                                wl = wdsl(j, 1)[:, ic * 128:(ic + 1) * 128]
                                tensor.matmul(PS[4 + j % 2][:, 0:TT], wh,
                                              obh(t % 2, ic),
                                              start=(ic == 0), stop=False)
                                tensor.matmul(PS[4 + j % 2][:, 0:TT], wh,
                                              obl(t % 2, ic),
                                              start=False, stop=False)
                                k = tensor.matmul(PS[4 + j % 2][:, 0:TT], wl,
                                                  obh(t % 2, ic), start=False,
                                                  stop=(ic == IC - 1))
                            assert S.inc(k, "TEN") == mm_at[t][h]

                my_barrier()

        def emit_a2_tile(t, x_src):
            XT0 = POOL[:, 0:4096]
            SQB = POOL[:, 8192:12288]
            LH = ARENA[:, 8192:10240].bitcast(BF16)
            LL = ARENA[:, 10240:12288].bitcast(BF16)
            QB = AR_(12288, 16384)
            KB = AR_(16384, 20480)
            WVB = AR_(20480, 24576).bitcast(BF16)
            WQK = POOL[:, 4096:8192].bitcast(BF16)
            T2B = XT0
            T1B = SQB
            VO = [TSL(0), TSL(1)]

            def lh_c(c):
                return LH[:, c * TT:(c + 1) * TT]

            def ll_c(c):
                return LL[:, c * TT:(c + 1) * TT]

            b = {n: S.cnt[n] for n in ("TEN", "SCA", "VEC")}
            cl = GL.plan_counters()
            ld_x = DmaGroup.plan(cl, t % 2, GL.K)
            sq_d = b["SCA"] + 1
            sd_d = b["SCA"] + 2
            st_d = b["TEN"] + 1
            bc_d = b["TEN"] + 2
            row_d = b["VEC"] + 1
            rr_d = b["VEC"] + 2
            ln_d = b["VEC"] + 2 + DC

            @block.sync
            def _(sync):
                i = ld3(sync, XT0[:], ap[x_src][:, t * TT:(t + 1) * TT])
                GL.issue(i, t % 2, ld_x)

            @bsca
            def _(scalar):
                GL.wait(scalar, ld_x)
                scalar.activation(SQB[:], XT0[:], AF.Square)
                assert S.cinc(scalar, "SCA") == sq_d
                S.wait(scalar, "VEC", row_d)
                scalar.activation(sd_row, var_row, AF.Sqrt,
                                  bias=EPS[0:1, 0:1], scale=1.0)
                assert S.cinc(scalar, "SCA") == sd_d

            @block.tensor
            def _(tensor):
                GL.wait(tensor, ld_x)
                S.wait(tensor, "SCA", sq_d)
                for c in range(DC):
                    tensor.matmul(PS[0][0:1, 0:TT], CST["ones128"][:],
                                  XT0[:, c * TT:(c + 1) * TT],
                                  start=(c == 0), stop=(c == DC - 1))
                for c in range(DC):
                    i = tensor.matmul(PS[1][0:1, 0:TT], CST["ones128"][:],
                                      SQB[:, c * TT:(c + 1) * TT],
                                      start=(c == 0), stop=(c == DC - 1))
                assert S.inc(i, "TEN") == st_d
                S.wait(tensor, "VEC", rr_d)
                tensor.matmul(PS[2][:, 0:TT], CST["ones1x"][:],
                              murow, start=True, stop=True)
                i = tensor.matmul(PS[3][:, 0:TT], CST["ones1x"][:],
                                  rrow, start=True, stop=True)
                assert S.inc(i, "TEN") == bc_d

            @bvec
            def _(vector):
                S.wait(vector, "TEN", st_d)
                vector.tensor_scalar(murow, PS[0][0:1, 0:TT],
                                     1.0 / float(D), None, AL.mult)
                vector.tensor_scalar(var_row, PS[1][0:1, 0:TT],
                                     1.0 / float(D), None, AL.mult)
                vector.tensor_tensor(sd_row, murow, murow, AL.mult)
                vector.tensor_tensor(var_row, var_row, sd_row, AL.subtract)
                assert S.cinc(vector, "VEC") == row_d
                S.wait(vector, "SCA", sd_d)
                vector.reciprocal(rrow, sd_row)
                assert S.cinc(vector, "VEC") == rr_d
                S.wait(vector, "TEN", bc_d)
                tmp = POOL[:, 8192:8704]
                tmp2 = POOL[:, 8704:9216]
                for c in range(DC):
                    vector.tensor_tensor(tmp, XT0[:, c * TT:(c + 1) * TT],
                                         PS[2][:, 0:TT], AL.subtract)
                    vector.tensor_tensor(tmp2, tmp, PS[3][:, 0:TT], AL.mult)
                    vector.tensor_copy(lh_c(c), tmp2)
                    vector.tensor_tensor(ll_c(c), tmp2, lh_c(c), AL.subtract)
                    S.cinc(vector, "VEC")
                assert S.cnt["VEC"] == ln_d

            b2 = {n: S.cnt[n] for n in ("TEN", "SCA", "VEC")}
            cl2, cs2 = GL.plan_counters(), GS.plan_counters()
            NU = 16
            wld = [None] * NU
            for u in range(NU):
                wld[u] = (DmaGroup.plan(cl2, 4 + (u % 2) * 2, GL.K),
                          DmaGroup.plan(cl2, 5 + (u % 2) * 2, GL.K))
            pmm_at = [b2["TEN"] + u + 1 for u in range(NU)]
            pev_at = [b2["SCA"] + u + 1 for u in range(NU)]

            def wqk_h(u):
                return WQK[:, (u % 2) * 4096:(u % 2) * 4096 + 1024]

            def wqk_l(u):
                return WQK[:, (u % 2) * 4096 + 1024:(u % 2) * 4096 + 2048]

            @block.sync
            def _(sync):
                for u in range(NU):
                    s_, o = u // 8, u % 8
                    nm = ("wqT", "wkT")[s_]
                    if u >= 2:
                        S.wait(sync, "TEN", pmm_at[u - 2])
                    i = ld3(sync, wqk_h(u),
                            ap[nm + "h"][:, o * 128:(o + 1) * 128])
                    GL.issue(i, 4 + (u % 2) * 2, wld[u][0])
                    i = ld3(sync, wqk_l(u),
                            ap[nm + "l"][:, o * 128:(o + 1) * 128])
                    GL.issue(i, 5 + (u % 2) * 2, wld[u][1])

            @block.tensor
            def _(tensor):
                S.wait(tensor, "VEC", ln_d)
                for u in range(NU):
                    s_, o = u // 8, u % 8
                    GL.wait(tensor, wld[u][0])
                    GL.wait(tensor, wld[u][1])
                    if u >= 2:
                        S.wait(tensor, "SCA", pev_at[u - 2])
                    for c in range(DC):
                        wh = wqk_h(u)[:, c * 128:(c + 1) * 128]
                        wl = wqk_l(u)[:, c * 128:(c + 1) * 128]
                        tensor.matmul(PS[u % 2][:, 0:TT], wh, lh_c(c),
                                      start=(c == 0), stop=False)
                        tensor.matmul(PS[u % 2][:, 0:TT], wh, ll_c(c),
                                      start=False, stop=False)
                        i = tensor.matmul(PS[u % 2][:, 0:TT], wl, lh_c(c),
                                          start=False, stop=(c == DC - 1))
                    assert S.inc(i, "TEN") == pmm_at[u]

            @bsca
            def _(scalar):
                for u in range(NU):
                    s_, o = u // 8, u % 8
                    S.wait(scalar, "TEN", pmm_at[u])
                    buf = QB if s_ == 0 else KB
                    scalar.copy(buf[:, o * TT:(o + 1) * TT], PS[u % 2][:, 0:TT])
                    assert S.cinc(scalar, "SCA") == pev_at[u]

            b3 = {n: S.cnt[n] for n in ("TEN", "SCA")}
            cl3, cs3 = GL.plan_counters(), GS.plan_counters()
            wvld = [(DmaGroup.plan(cl3, 2, GL.K), DmaGroup.plan(cl3, 3, GL.K))
                    for _ in range(2)]
            vst = [DmaGroup.plan(cs3, 2 + u % 2, GS.K) for u in range(8)]
            vmm = [b3["TEN"] + u + 1 for u in range(8)]
            vev = [b3["SCA"] + u + 1 for u in range(8)]

            @block.sync
            def _(sync):
                for o2 in range(2):
                    if o2 >= 1:
                        S.wait(sync, "TEN", vmm[3])
                    i = ld3(sync, WVB[:, 0:4096],
                            ap["wvTh"][:, o2 * TT:(o2 + 1) * TT])
                    GL.issue(i, 2, wvld[o2][0])
                    i = ld3(sync, WVB[:, 4096:8192],
                            ap["wvTl"][:, o2 * TT:(o2 + 1) * TT])
                    GL.issue(i, 3, wvld[o2][1])

            @block.tensor
            def _(tensor):
                for u in range(8):
                    o2, ts4 = u // 4, u % 4
                    GL.wait(tensor, wvld[o2][0])
                    GL.wait(tensor, wvld[o2][1])
                    if u < 2:
                        S.wait(tensor, "SCA", pev_at[NU - 2 + u])
                    else:
                        S.wait(tensor, "SCA", vev[u - 2])
                    for c in range(DC):
                        xh = LH[:, c * TT + ts4 * 128:c * TT + (ts4 + 1) * 128]
                        xl = LL[:, c * TT + ts4 * 128:c * TT + (ts4 + 1) * 128]
                        whm = WVB[:, c * 512:(c + 1) * 512]
                        wlm = WVB[:, 4096 + c * 512:4096 + (c + 1) * 512]
                        tensor.matmul(PS[u % 2][:, 0:TT], xh, whm,
                                      start=(c == 0), stop=False)
                        tensor.matmul(PS[u % 2][:, 0:TT], xl, whm,
                                      start=False, stop=False)
                        i = tensor.matmul(PS[u % 2][:, 0:TT], xh, wlm,
                                          start=False, stop=(c == DC - 1))
                    assert S.inc(i, "TEN") == vmm[u]

            @bsca
            def _(scalar):
                for u in range(8):
                    S.wait(scalar, "TEN", vmm[u])
                    if u >= 2:
                        GS.wait(scalar, vst[u - 2])
                    scalar.copy(VO[u % 2][:], PS[u % 2][:, 0:TT])
                    assert S.cinc(scalar, "SCA") == vev[u]

            @block.gpsimd
            def _(gp):
                for u in range(8):
                    o2, ts4 = u // 4, u % 4
                    S.wait(gp, "SCA", vev[u])
                    r0 = t * TT + ts4 * 128
                    i = gp.dma_start(
                        out=ap["vtm"][r0:r0 + 128, o2 * TT:(o2 + 1) * TT],
                        in_=VO[u % 2][:])
                    GS.issue(i, 2 + u % 2, vst[u])
                GS.wait_all(gp)

            rsl = slice(t * TT, (t + 1) * TT)
            prev_vqr = [None]
            for s_, buf in ((0, QB), (1, KB)):
                b4 = {n: S.cnt[n] for n in ("TEN", "SCA", "VEC")}
                cs4 = GS.plan_counters()
                sq2_d = b4["SCA"] + 1
                sd4 = [b4["SCA"] + 1 + c + 1 for c in range(DC)]
                rmm, rbb, pmm2 = {}, {}, {}
                tc = b4["TEN"]
                for st_ in range(DC + 2):
                    if st_ < DC:
                        tc += 1
                        rmm[st_] = tc
                    if 1 <= st_ <= DC:
                        tc += 1
                        rbb[st_ - 1] = tc
                    if st_ >= 2:
                        tc += 1
                        pmm2[st_ - 2] = tc
                vme, vrc, vqn, vt2 = {}, {}, {}, {}
                vc = b4["VEC"]
                for st_ in range(DC + 2):
                    if st_ < DC:
                        vc += 1
                        vme[st_] = vc
                        vc += 1
                        vrc[st_] = vc
                    if 1 <= st_ <= DC:
                        vc += 1
                        vqn[st_ - 1] = vc
                    if st_ >= 2:
                        vc += 1
                        vt2[st_ - 2] = vc
                vt1 = vc + 1
                vqr = vc + 2
                stq = DmaGroup.plan(cs4, s_, GS.K)

                def rslot(c):
                    return RMSW[0:2, (c % 4) * 512:(c % 4 + 1) * 512]

                @bsca
                def _(scalar, s_=s_, buf=buf, sq2_d=sq2_d, sd4=sd4,
                      vme=vme, pq=prev_vqr[0], rslot=rslot):
                    if s_ == 0:
                        GS.wait(scalar, vst[6])
                        GS.wait(scalar, vst[7])
                    else:
                        S.wait(scalar, "VEC", pq)
                    scalar.activation(T2B[:], buf[:], AF.Square)
                    assert S.cinc(scalar, "SCA") == sq2_d
                    for c in range(DC):
                        S.wait(scalar, "VEC", vme[c])
                        scalar.activation(rslot(c), rslot(c), AF.Sqrt,
                                          bias=EPS[0:2, 1:2], scale=1.0)
                        assert S.cinc(scalar, "SCA") == sd4[c]

                @block.tensor
                def _(tensor, s_=s_, buf=buf, sq2_d=sq2_d, sd4=sd4, rmm=rmm,
                      rbb=rbb, pmm2=pmm2, vme=vme, vrc=vrc, vqn=vqn, vt2=vt2,
                      rslot=rslot):
                    for st_ in range(DC + 2):
                        if st_ < DC:
                            c = st_
                            if c == 0:
                                S.wait(tensor, "SCA", sq2_d)
                            if c >= 4:
                                S.wait(tensor, "VEC", vt2[c - 4])
                            i = tensor.matmul(PS[4 + c % 4][0:2, 0:TT],
                                              CST["rmsmask"][:],
                                              T2B[:, c * TT:(c + 1) * TT],
                                              start=True, stop=True)
                            assert S.inc(i, "TEN") == rmm[c]
                        if 1 <= st_ <= DC:
                            c = st_ - 1
                            S.wait(tensor, "VEC", vrc[c])
                            if c >= 4:
                                S.wait(tensor, "VEC", vqn[c - 4])
                            stat = (GBC[0:2, c * 128:(c + 1) * 128]
                                    if s_ == 0 else CST["rmsbc"][:])
                            i = tensor.matmul(PS[c % 4][:, 0:TT], stat,
                                              rslot(c), start=True, stop=True)
                            assert S.inc(i, "TEN") == rbb[c]
                        if st_ >= 2:
                            c = st_ - 2
                            S.wait(tensor, "VEC", vqn[c])
                            i = tensor.matmul(PS[4 + c % 4][:, 0:TT],
                                              CST["perm32"][:],
                                              buf[:, c * TT:(c + 1) * TT],
                                              start=True, stop=True)
                            assert S.inc(i, "TEN") == pmm2[c]

                @bvec
                def _(vector, s_=s_, buf=buf, rmm=rmm, rbb=rbb, pmm2=pmm2,
                      vme=vme, vrc=vrc, vqn=vqn, vt2=vt2, vt1=vt1, vqr=vqr,
                      sd4=sd4, rslot=rslot):
                    for st_ in range(DC + 2):
                        if st_ < DC:
                            c = st_
                            S.wait(vector, "TEN", rmm[c])
                            vector.tensor_scalar(rslot(c),
                                                 PS[4 + c % 4][0:2, 0:TT],
                                                 1.0 / float(HD), None,
                                                 AL.mult)
                            assert S.cinc(vector, "VEC") == vme[c]
                            S.wait(vector, "SCA", sd4[c])
                            vector.reciprocal(rslot(c), rslot(c))
                            assert S.cinc(vector, "VEC") == vrc[c]
                        if 1 <= st_ <= DC:
                            c = st_ - 1
                            S.wait(vector, "TEN", rbb[c])
                            vector.tensor_tensor(buf[:, c * TT:(c + 1) * TT],
                                                 buf[:, c * TT:(c + 1) * TT],
                                                 PS[c % 4][:, 0:TT], AL.mult)
                            assert S.cinc(vector, "VEC") == vqn[c]
                        if st_ >= 2:
                            c = st_ - 2
                            S.wait(vector, "TEN", pmm2[c])
                            vector.tensor_tensor(T2B[:, c * TT:(c + 1) * TT],
                                                 PS[4 + c % 4][:, 0:TT],
                                                 ROPE_B[:, rsl], AL.mult)
                            assert S.cinc(vector, "VEC") == vt2[c]
                    rab = ROPE_A[:, rsl].rearrange(
                        "p (o k) -> p o k", o=1).broadcast_to([128, DC, TT])
                    vector.tensor_tensor(
                        T1B[:].rearrange("p (c k) -> p c k", c=DC),
                        buf[:].rearrange("p (c k) -> p c k", c=DC),
                        rab, AL.mult)
                    assert S.cinc(vector, "VEC") == vt1
                    vector.tensor_tensor(buf[:], T1B[:], T2B[:], AL.add)
                    assert S.cinc(vector, "VEC") == vqr

                prev_vqr[0] = vqr

                @block.gpsimd
                def _(gp, s_=s_, buf=buf, vqr=vqr, stq=stq):
                    S.wait(gp, "VEC", vqr)
                    i = gp.dma_start(
                        out=ap[("qrot", "krot")[s_]][:, t * TT:(t + 1) * TT]
                            .rearrange("(c p) k -> p c k", p=128),
                        in_=buf[:].rearrange("p (c k) -> p c k", c=DC))
                    GS.issue(i, s_, stq)
                    GS.wait(gp, stq)

            my_barrier()

        def emit_a3_wp(wp, x_src, x_dst):
            Q = AR_(8192, 12288)
            K_ = AR_(12288, 16384)
            V4 = AR_(16384, 20480)
            Y = AR_(20480, 24576)
            YH = V4[:, 0:2048].bitcast(BF16)
            YL = V4[:, 2048:4096].bitcast(BF16)
            WOB = POOL[:, 4096:6144].bitcast(BF16)
            XOL = [TSL(12), TSL(13)]
            SM3 = [TSL(16), TSL(17), TSL(0)]
            EN3 = [TSL(20), TSL(21), TSL(22)]
            ET3 = [[TSL(x)[:, ks * WIN:(ks + 1) * WIN] for ks in range(2)]
                   for x in (18, 19, 23)]

            cl, cs_ = GL.plan_counters(), GS.plan_counters()
            ldq = DmaGroup.plan(cl, 0, GL.K)
            ldk = DmaGroup.plan(cl, 1, GL.K)
            ldv = DmaGroup.plan(cl, 2, GL.K)
            wo_at = [(DmaGroup.plan(cl, 4 + h % 2, GL.K),
                      DmaGroup.plan(cl, 6 + h % 2, GL.K)) for h in range(DC)]
            cgc = GC.plan_counters()
            xo_at = [DmaGroup.plan(cgc, h % 2, GC.K) for h in range(DC)]
            st_at = [DmaGroup.plan(cs_, h % 2, GS.K) for h in range(DC)]

            b = {n: S.cnt[n] for n in ("TEN", "SCA", "VEC")}
            units = [(w, h) for w in range(2) for h in range(NH)]
            NU = len(units)
            sc_at, tp_at, pv_at = {}, {}, {}
            ms_at, nm_at, ex_at, et_at, ye_at = {}, {}, {}, {}, {}
            tc = b["TEN"]
            for u in range(3):
                for qs in range(2):
                    tc += 1
                    sc_at[units[u] + (qs,)] = tc
            for u in range(NU):
                if u + 3 < NU:
                    for qs in range(2):
                        tc += 1
                        sc_at[units[u + 3] + (qs,)] = tc
                for qs in range(2):
                    for ks in range(2):
                        tc += 1
                        tp_at[units[u] + (qs, ks)] = tc
                tc += 1
                pv_at[units[u]] = tc
            womm_at = [tc + h + 1 for h in range(DC)]
            vc2 = b["VEC"]
            for u in range(NU + 1):
                if u < NU:
                    for qs in range(2):
                        vc2 += 1
                        ms_at[units[u] + (qs,)] = vc2
                if u >= 1:
                    for qs in range(2):
                        vc2 += 1
                        nm_at[units[u - 1] + (qs,)] = vc2
            ysp = vc2 + 1
            vc2 += 1
            xn_at = [vc2 + h + 1 for h in range(DC)]
            sc2 = b["SCA"]
            for u in range(NU + 1):
                if u < NU:
                    for qs in range(2):
                        sc2 += 1
                        ex_at[units[u] + (qs,)] = sc2
                if u >= 1:
                    w_, h_ = units[u - 1]
                    sc2 += 1
                    et_at[(w_, h_)] = sc2
                    if h_ % 2 == 1:
                        sc2 += 1
                        ye_at[(w_, h_ // 2)] = sc2

            def scb(u):
                return PS[u % 3]

            def tpb(u):
                return PS[3 + u % 3]

            @block.sync
            def _(sync):
                i = ld3(sync, Q[:], ap["qrot"][:, wp * TT:(wp + 1) * TT])
                GL.issue(i, 0, ldq)
                i = ld3(sync, K_[:], ap["krot"][:, wp * TT:(wp + 1) * TT])
                GL.issue(i, 1, ldk)
                i = ld3(sync, V4[:], ap["vtm"][wp * TT:(wp + 1) * TT, :])
                GL.issue(i, 2, ldv)
                for h in range(DC):
                    if h >= 2:
                        S.wait(sync, "TEN", womm_at[h - 2])
                    i = ld3(sync, WOB[:, (h % 2) * 2048:(h % 2) * 2048 + 1024],
                            ap["woTh"][:, h * 128:(h + 1) * 128])
                    GL.issue(i, 4 + h % 2, wo_at[h][0])
                    i = ld3(sync, WOB[:, (h % 2) * 2048 + 1024:(h % 2) * 2048 + 2048],
                            ap["woTl"][:, h * 128:(h + 1) * 128])
                    GL.issue(i, 6 + h % 2, wo_at[h][1])

            def emit_sc(tensor, u):
                w, h = units[u]
                hc, off = h // 2, (h % 2) * 64
                kw = K_[off:off + 64,
                        hc * TT + w * WIN:hc * TT + (w + 1) * WIN]
                if u >= 3:
                    S.wait(tensor, "VEC", ms_at[units[u - 3] + (1,)])
                for qs in range(2):
                    q0 = hc * TT + w * WIN + qs * 128
                    i = tensor.matmul(scb(u)[:, qs * WIN:(qs + 1) * WIN],
                                      Q[off:off + 64, q0:q0 + 128],
                                      kw, start=True, stop=True)
                    assert S.inc(i, "TEN") == sc_at[(w, h, qs)]

            @block.tensor
            def _(tensor):
                GL.wait(tensor, ldq)
                GL.wait(tensor, ldk)
                GL.wait(tensor, ldv)
                for u in range(3):
                    emit_sc(tensor, u)
                for u, (w, h) in enumerate(units):
                    if u + 3 < NU:
                        emit_sc(tensor, u + 3)
                    if u >= 3:
                        S.wait(tensor, "SCA", et_at[units[u - 3]])
                    for qs in range(2):
                        S.wait(tensor, "VEC", nm_at[(w, h, qs)])
                        for ks in range(2):
                            i = tensor.transpose(
                                tpb(u)[:, ks * WIN + qs * 128:
                                       ks * WIN + (qs + 1) * 128],
                                EN3[u % 3][:, qs * WIN + ks * 128:
                                           qs * WIN + (ks + 1) * 128],
                                CST["ident"][:])
                            assert S.inc(i, "TEN") == tp_at[(w, h, qs, ks)]
                    S.wait(tensor, "SCA", et_at[(w, h)])
                    if h >= 2:
                        S.wait(tensor, "SCA", ye_at[(w, h // 2 - 1)])
                    off_y = (h % 2) * 64
                    for ks in range(2):
                        s4 = (w * 2 + ks) * 1024
                        i = tensor.matmul(
                            PS[6 + w][off_y:off_y + 64, 0:WIN],
                            V4[:, s4 + h * 64:s4 + (h + 1) * 64],
                            ET3[u % 3][ks][:, 0:WIN],
                            start=(ks == 0), stop=(ks == 1))
                    assert S.inc(i, "TEN") == pv_at[(w, h)]
                S.wait(tensor, "VEC", ysp)
                for h in range(DC):
                    GL.wait(tensor, wo_at[h][0])
                    GL.wait(tensor, wo_at[h][1])
                    if h >= 2:
                        S.wait(tensor, "VEC", xn_at[h - 2])
                    for c in range(DC):
                        wh = WOB[:, (h % 2) * 2048 + c * 128:
                                 (h % 2) * 2048 + (c + 1) * 128]
                        wl = WOB[:, (h % 2) * 2048 + 1024 + c * 128:
                                 (h % 2) * 2048 + 1024 + (c + 1) * 128]
                        yh = YH[:, c * TT:(c + 1) * TT]
                        yl = YL[:, c * TT:(c + 1) * TT]
                        tensor.matmul(PS[h % 2][:, 0:TT], wh, yh,
                                      start=(c == 0), stop=False)
                        tensor.matmul(PS[h % 2][:, 0:TT], wh, yl,
                                      start=False, stop=False)
                        i = tensor.matmul(PS[h % 2][:, 0:TT], wl, yh,
                                          start=False, stop=(c == DC - 1))
                    assert S.inc(i, "TEN") == womm_at[h]

            @bvec
            def _(vector):
                for u in range(NU + 1):
                    if u < NU:
                        w, h = units[u]
                        for qs in range(2):
                            if qs == 0:
                                S.wait(vector, "TEN", sc_at[(w, h, 1)])
                                if u >= 3:
                                    S.wait(vector, "SCA",
                                           ex_at[units[u - 3] + (1,)])
                            sm = SM3[u % 3][:, qs * WIN:(qs + 1) * WIN]
                            vector.scalar_tensor_tensor(
                                sm, scb(u)[:, qs * WIN:(qs + 1) * WIN], 0.125,
                                CST[f"maskq{qs}"][:], AL.mult, AL.min)
                            vector.tensor_reduce(
                                NMC[:, (u % 3) * 2 + qs:(u % 3) * 2 + qs + 1],
                                sm, AX, AL.max, negate=True)
                            assert S.cinc(vector, "VEC") == ms_at[(w, h, qs)]
                    if u >= 1:
                        w, h = units[u - 1]
                        l = (u - 1) % 3
                        en_f = EN3[l]
                        for qs in range(2):
                            S.wait(vector, "SCA", ex_at[(w, h, qs)])
                            en = en_f[:, qs * WIN:(qs + 1) * WIN]
                            vector.reciprocal(
                                NMC[:, 12 + l * 2 + qs:13 + l * 2 + qs],
                                NMC[:, 6 + l * 2 + qs:7 + l * 2 + qs])
                            vector.tensor_scalar(
                                en, en,
                                NMC[:, 12 + l * 2 + qs:13 + l * 2 + qs],
                                None, AL.mult)
                            assert S.cinc(vector, "VEC") == nm_at[(w, h, qs)]
                S.wait(vector, "SCA", ye_at[(1, NH // 2 - 1)])
                vector.tensor_copy(YH[:], Y[:])
                vector.tensor_tensor(YL[:], Y[:], YH[:], AL.subtract)
                assert S.cinc(vector, "VEC") == ysp
                for h in range(DC):
                    S.wait(vector, "TEN", womm_at[h])
                    GC.wait(vector, xo_at[h])
                    if h >= 2:
                        GS.wait(vector, st_at[h - 2])
                    vector.tensor_tensor(TSL(14 + h % 2)[:],
                                         PS[h % 2][:, 0:TT],
                                         XOL[h % 2][:], AL.add)
                    assert S.cinc(vector, "VEC") == xn_at[h]

            @bsca
            def _(scalar):
                for u in range(NU + 1):
                    if u < NU:
                        w, h = units[u]
                        l = u % 3
                        for qs in range(2):
                            S.wait(scalar, "VEC", ms_at[(w, h, qs)])
                            if qs == 0 and u >= 3:
                                S.wait(scalar, "TEN",
                                       tp_at[units[u - 3] + (1, 1)])
                            scalar.activation(
                                EN3[l][:, qs * WIN:(qs + 1) * WIN],
                                SM3[l][:, qs * WIN:(qs + 1) * WIN], AF.Exp,
                                bias=NMC[:, l * 2 + qs:l * 2 + qs + 1],
                                scale=1.0,
                                accum_out=NMC[:, 6 + l * 2 + qs:
                                              7 + l * 2 + qs])
                            assert S.cinc(scalar, "SCA") == ex_at[(w, h, qs)]
                    if u >= 1:
                        w, h = units[u - 1]
                        lp = (u - 1) % 3
                        S.wait(scalar, "TEN", tp_at[(w, h, 1, 1)])
                        scalar.copy(ET3[lp][0][:, 0:WIN],
                                    tpb(u - 1)[:, 0:WIN])
                        scalar.copy(ET3[lp][1][:, 0:WIN],
                                    tpb(u - 1)[:, WIN:2 * WIN])
                        assert S.cinc(scalar, "SCA") == et_at[(w, h)]
                        if h % 2 == 1:
                            S.wait(scalar, "TEN", pv_at[(w, h)])
                            hp = h // 2
                            scalar.copy(Y[:, hp * TT + w * WIN:
                                          hp * TT + (w + 1) * WIN],
                                        PS[6 + w][:, 0:WIN])
                            assert S.cinc(scalar, "SCA") == ye_at[(w, hp)]

            @block.gpsimd
            def _(gp):
                for h in range(DC + 1):
                    if h < DC:
                        if h >= 2:
                            S.wait(gp, "VEC", xn_at[h - 2])
                        i = gp.dma_start(
                            out=XOL[h % 2][:],
                            in_=ap[x_src][h * 128:(h + 1) * 128,
                                          wp * TT:(wp + 1) * TT])
                        GC.issue(i, h % 2, xo_at[h])
                    if h >= 1:
                        S.wait(gp, "VEC", xn_at[h - 1])
                        i = gp.dma_start(
                            out=ap[x_dst][(h - 1) * 128:h * 128,
                                          wp * TT:(wp + 1) * TT],
                            in_=TSL(14 + (h - 1) % 2)[:])
                        GS.issue(i, (h - 1) % 2, st_at[h - 1])
                GS.wait_all(gp)

            my_barrier()

        def emit_attn(x_src, x_dst):
            if active():
                cl = GL.plan_counters()
                la = DmaGroup.plan(cl, 0, GL.K)
                lb = DmaGroup.plan(cl, 1, GL.K)

                @block.sync
                def _(sync):
                    i = sync.dma_start(out=ROPE_A[:], in_=ap["cropeA"])
                    GL.issue(i, 0, la)
                    i = sync.dma_start(out=ROPE_B[:], in_=ap["cropeB"])
                    GL.issue(i, 1, lb)
                    GL.wait(sync, la)
                    GL.wait(sync, lb)

                @bvec
                def _(vector):
                    for c in range(DC):
                        vector.tensor_scalar(GBC[0:2, c * 128:(c + 1) * 128],
                                             CST["rmsbc"][0:2, 0:128],
                                             QG[0:2, c:c + 1], None, AL.mult)
                        S.cinc(vector, "VEC")

                my_barrier()
                for t in range(NT):
                    emit_a2_tile(t, x_src)
            if active():
                for wp in range(NT):
                    emit_a3_wp(wp, x_src, x_dst)

        emit_ffn("x", "xmid", 0)
        emit_attn("xmid", "xmid2")
        emit_ffn("xmid2", "y", 5)

        @block.gpsimd
        def _(gp):
            GS.wait_all(gp)

    return nc


def kernel(**inputs):
    x = np.ascontiguousarray(inputs["L0_attn_x"])
    per_core = []
    for b in range(N_CORES):
        m = {
            "x": np.ascontiguousarray(x[b].T),
            "wq": np.ascontiguousarray(inputs["L0_attn_wq"]),
            "wk": np.ascontiguousarray(inputs["L0_attn_wk"]),
            "wv": np.ascontiguousarray(inputs["L0_attn_wv"]),
            "wo": np.ascontiguousarray(inputs["L0_attn_wo"]),
            "wg": np.ascontiguousarray(inputs["L0_ffn_w_gate"]),
            "wu": np.ascontiguousarray(inputs["L0_ffn_w_up"]),
            "wd": np.ascontiguousarray(inputs["L0_ffn_w_down"]),
            "qg": np.ascontiguousarray(
                inputs["L0_attn_q_gain"].reshape(NH // 2, 2).T),
        }
        for n, v in _CONSTS.items():
            m["c" + n] = v
        per_core.append(m)
    nc = build_nc()
    res = run_bass_kernel_spmd(nc, per_core, core_ids=list(range(N_CORES)))
    out = np.empty((N_CORES, T, D), np.float32)
    for b in range(N_CORES):
        out[b] = res.results[b]["y"].T
    return out
